# version 11
# baseline (speedup 1.0000x reference)
import numpy as np
import ml_dtypes
import concourse.bass as bass
import concourse.mybir as mybir
from concourse.bass_utils import run_bass_kernel_spmd

F32 = mybir.dt.float32
BF16 = mybir.dt.bfloat16
AF = mybir.ActivationFunctionType
OP = mybir.AluOpType

ENGINES = ("tensor", "vector", "scalar", "gpsimd", "sync")
NCORES = 8
B = 2
S = 2048
DM = 1024
FT = 8
TOK = B * S
H = 16
HF = 16
DH = 64
NE = 8
HID = 2048
INTER = 4096


class Buf:
    __slots__ = ("name", "writer", "readers", "xr")

    def __init__(self, name, xr=False):
        self.name = name
        self.writer = None
        self.readers = []
        self.xr = xr


class Bufs:

    def __init__(self, name, xr=False):
        self.name = name
        self.m = {}
        self.xr = xr

    def __call__(self, key=0):
        if key not in self.m:
            self.m[key] = Buf(f"{self.name}[{key}]", xr=self.xr)
        return self.m[key]

    def reset(self):
        self.m = {}


class Op:
    __slots__ = ("eng", "fn", "idx", "waits", "marked", "know", "dma_slot", "tag")

    def __init__(self, eng, fn, idx, dma_slot, tag):
        self.eng = eng
        self.fn = fn
        self.idx = idx
        self.waits = {}
        self.marked = False
        self.know = None
        self.dma_slot = dma_slot
        self.tag = tag


class Em:
    def __init__(self, nc):
        self.nc = nc
        self.ops = {e: [] for e in ENGINES}
        self.dma_slots = {}
        self.dims = list(ENGINES)
        self.know = {e: {} for e in ENGINES}
        self.sems = {}

    def slot(self, name):
        if name not in self.dma_slots:
            self.dma_slots[name] = 0
            self.dims.append("dma:" + name)
        return name

    def _dep_dim_val(self, dep):
        return dep

    def _need(self, eng, dim, val):
        cur = self.know[eng].get(dim, -1)
        return cur < val

    def _add_edge(self, eng, waits, dep):
        dim, val, know_at = dep
        if dim == eng and not dim.startswith("dma:"):
            if eng == "tensor":
                return
        if not self._need(eng, dim, val):
            return
        if waits.get(dim, -1) < val:
            waits[dim] = val
        k = self.know[eng]
        if know_at:
            for d, v in know_at.items():
                if k.get(d, -1) < v:
                    k[d] = v
        if k.get(dim, -1) < val:
            k[dim] = val

    def op(self, eng, fn, reads=(), writes=(), dma_slot=None, tag=""):
        waits = {}
        for b in reads:
            if b.writer is not None:
                self._add_edge(eng, waits, b.writer)
            if b.xr:
                for r in b.readers:
                    if r[0] != eng:
                        self._add_edge(eng, waits, r)
        for b in writes:
            if b.writer is not None:
                self._add_edge(eng, waits, b.writer)
            for r in b.readers:
                self._add_edge(eng, waits, r)
        idx = len(self.ops[eng])
        o = Op(eng, fn, idx, dma_slot, tag)
        o.waits = waits
        self.ops[eng].append(o)
        for dim, val in waits.items():
            if not dim.startswith("dma:"):
                self.ops[dim][val].marked = True
        if dma_slot is not None:
            self.dma_slots[dma_slot] += 1
            cnt = self.dma_slots[dma_slot]
            dep = ("dma:" + dma_slot, cnt, dict(self.know[eng]))
        else:
            dep = (eng, idx, None)
        o.know = dict(self.know[eng])
        o.know[eng] = idx
        if dep[0] == eng:
            dep = (eng, idx, o.know)
        for b in writes:
            b.writer = dep
            b.readers = []
        for b in reads:
            b.readers.append(dep)
        return o

    def barrier(self):
        lasts = {}
        for e in ENGINES:
            for i in range(len(self.ops[e]) - 1, -1, -1):
                o = self.ops[e][i]
                if o.dma_slot is None and o.fn is not None:
                    o.marked = True
                    lasts[e] = i
                    break
        slot_counts = dict(self.dma_slots)
        for e in ENGINES:
            if not self.ops[e] and e not in ("sync",):
                continue
            waits = {}
            for f, i in lasts.items():
                if f != e and self._need(e, f, i):
                    waits[f] = i
                    self.know[e][f] = i
            for s, c in slot_counts.items():
                d = "dma:" + s
                if c > 0 and self._need(e, d, c):
                    waits[d] = c
                    self.know[e][d] = c
            if waits:
                o = Op(e, None, len(self.ops[e]), None, "barrier")
                o.waits = waits
                self.ops[e].append(o)
        merged = {}
        for e in ENGINES:
            for d, v in self.know[e].items():
                if merged.get(d, -1) < v:
                    merged[d] = v
        for e in ENGINES:
            self.know[e] = dict(merged)

    def finalize(self, final_eng="sync"):
        nc = self.nc
        pc = {}
        for e in ENGINES:
            cnt = 0
            lst = []
            for o in self.ops[e]:
                if o.marked:
                    cnt += 1
                lst.append(cnt)
            pc[e] = lst
        sem_ctxs = []
        for e in ENGINES:
            if self.ops[e]:
                c = nc.semaphore("sem_" + e)
                self.sems[e] = c.__enter__()
                sem_ctxs.append(c)
        for s in self.dma_slots:
            c = nc.semaphore("sd_" + s)
            self.sems["dma:" + s] = c.__enter__()
            sem_ctxs.append(c)

        em = self

        def mk(engname):
            def body(h):
                for o in em.ops[engname]:
                    for dim in sorted(o.waits):
                        val = o.waits[dim]
                        if dim.startswith("dma:"):
                            h.wait_ge(em.sems[dim], 16 * val)
                        else:
                            c = pc[dim][val]
                            if c > 0:
                                h.wait_ge(em.sems[dim], c)
                    if o.fn is None:
                        continue
                    inst = o.fn(h)
                    if o.dma_slot is not None:
                        inst.then_inc(em.sems["dma:" + o.dma_slot], 16)
                    elif o.marked:
                        inst.then_inc(em.sems[engname], 1)
            return body

        with nc.Block() as block:
            for e in ENGINES:
                if self.ops[e]:
                    getattr(block, e)(mk(e))
        return sem_ctxs


def _perm_heads(W):
    Wp = np.zeros((DM, 512), W.dtype)
    for h in range(H):
        g, j = h // 4, h % 4
        Wp[:, 128 * g + 32 * j: 128 * g + 32 * j + HF] = W[:, h * HF:(h + 1) * HF]
    return Wp


def _bias_cols(b, p=128):
    f = np.ascontiguousarray(b).reshape(-1)
    n = f.size // p
    return np.ascontiguousarray(f.reshape(n, p).T)


def prep_inputs(core, inp):
    d = {}
    x = np.asarray(inp["L0_attn_hidden_states"])
    d["x0"] = np.ascontiguousarray(x[core * B:(core + 1) * B].reshape(TOK, DM))
    d["wqp"] = np.ascontiguousarray(
        _perm_heads(np.asarray(inp["L0_attn_Wq"])).reshape(8, 128, 512).transpose(1, 0, 2))
    d["wkp"] = np.ascontiguousarray(
        _perm_heads(np.asarray(inp["L0_attn_Wk"])).reshape(8, 128, 512).transpose(1, 0, 2))
    d["wv"] = np.ascontiguousarray(
        np.asarray(inp["L0_attn_Wv"]).reshape(8, 128, 1024).transpose(1, 0, 2))
    d["wo"] = np.asarray(inp["L0_attn_Wo"])
    d["w1"] = np.ascontiguousarray(
        np.asarray(inp["L0_ffn_W1"]).reshape(8, 8, 128, 8, 256).transpose(0, 3, 2, 1, 4))
    d["w3"] = np.ascontiguousarray(
        np.asarray(inp["L0_ffn_W3"]).reshape(8, 8, 128, 8, 256).transpose(0, 3, 2, 1, 4))
    d["w2"] = np.ascontiguousarray(
        np.asarray(inp["L0_ffn_W2"]).reshape(8, 2, 8, 128, 1024).transpose(0, 1, 3, 2, 4))
    d["b1c"] = _bias_cols(inp["L0_ffn_b1"])
    d["b3c"] = _bias_cols(inp["L0_ffn_b3"])
    d["b2c"] = _bias_cols(inp["L0_ffn_b2"])
    d["wg"] = np.asarray(inp["L0_ffn_Wg"])
    d["bgr"] = np.ascontiguousarray(np.asarray(inp["L0_ffn_bg"]).reshape(1, NE))
    d["wf"] = np.ascontiguousarray(
        np.asarray(inp["L0_ffn_Wf"]).reshape(8, 128, 8, 128).transpose(2, 1, 0, 3))
    d["bfc"] = _bias_cols(inp["L0_ffn_bf"])
    wft = np.ascontiguousarray(np.asarray(inp["L1_attn_W_f"]).T)
    d["wft"] = np.ascontiguousarray(
        wft.reshape(8, 128, 2, 8, 128).transpose(3, 1, 2, 0, 4))
    d["l1w1"] = np.ascontiguousarray(
        np.asarray(inp["L1_ffn_W1"]).reshape(8, 128, 32, 128).transpose(2, 1, 0, 3))
    d["l1b1c"] = _bias_cols(inp["L1_ffn_b1"])
    d["l1w2"] = np.ascontiguousarray(
        np.asarray(inp["L1_ffn_W2"]).reshape(32, 128, 8, 128).transpose(2, 1, 0, 3))
    d["l1b2c"] = _bias_cols(inp["L1_ffn_b2"])
    d["ident_bf"] = np.eye(128, dtype=ml_dtypes.bfloat16)
    d["ident_f32"] = np.eye(128, dtype=np.float32)
    d["triu_bf"] = np.triu(np.ones((128, 128), np.float32)).astype(ml_dtypes.bfloat16)
    return d


DRAM_SPECS = [
    ("x0", [TOK, DM], F32), ("wqp", [128, 8, 512], F32), ("wkp", [128, 8, 512], F32),
    ("wv", [128, 8, 1024], F32), ("wo", [DM, DM], F32),
    ("w1", [NE, 8, 128, 8, 256], F32), ("w3", [NE, 8, 128, 8, 256], F32),
    ("w2", [NE, 2, 128, 8, 1024], F32),
    ("b1c", [128, 128], F32), ("b3c", [128, 128], F32), ("b2c", [128, 64], F32),
    ("wg", [DM, NE], F32), ("bgr", [1, NE], F32), ("wf", [8, 128, 8, 128], F32),
    ("bfc", [128, 8], F32), ("wft", [8, 128, 2, 8, 128], F32),
    ("l1w1", [32, 128, 8, 128], F32), ("l1b1c", [128, 32], F32),
    ("l1w2", [8, 128, 32, 128], F32), ("l1b2c", [128, 8], F32),
    ("ident_bf", [128, 128], BF16), ("ident_f32", [128, 128], F32),
    ("triu_bf", [128, 128], BF16),
]

HB = S // 2
B_LIST = None
SCAN_FAKE = False
ATTN_IL = True
L1A_LIMIT = 99
SQ_SCALE = 1.0 / np.sqrt(32.0)
SQ_BIAS = 4.0 / np.sqrt(32.0)


class T:
    pass


def setup_tensors(nc, dbg=()):
    t = T()
    t.nc = nc
    t.em = Em(nc)
    t.ctxs = []

    def sbuf(name, shape, dt):
        c = nc.sbuf_tensor(name, shape, dt)
        t.ctxs.append(c)
        return c.__enter__()

    def psum(name, shape, dt=F32):
        c = nc.psum_tensor(name, shape, dt)
        t.ctxs.append(c)
        return c.__enter__()

    t.dram = {}
    for name, shape, dt in DRAM_SPECS:
        t.dram[name] = nc.dram_tensor(name, shape, dt, kind="ExternalInput")
    t.out = nc.dram_tensor("out", [TOK, DM], F32, kind="ExternalOutput")
    t.dbg = {}
    for name, shape in dbg:
        t.dbg[name] = nc.dram_tensor("dbg_" + name, shape, F32, kind="ExternalOutput")

    t.XT = sbuf("XT", [128, FT * TOK], BF16)
    t.XN = sbuf("XN", [128, FT * HB], BF16)
    t.AR = sbuf("AR", [128, 33024], BF16)
    t.ARF = t.AR.bitcast(F32)
    t.WS = sbuf("WS", [128, 16384], BF16)
    t.SB = sbuf("SB", [128, 8192], BF16)
    t.SF = sbuf("SF", [128, 3072], F32)
    t.WB = sbuf("WB", [128, 128], F32)
    t.IDB = sbuf("IDB", [128, 128], BF16)
    t.IDF = sbuf("IDF", [128, 128], F32)
    t.TRIU = sbuf("TRIU", [128, 128], BF16)
    t.ONESC = sbuf("ONESC", [128, 1], BF16)
    t.CC = sbuf("CC", [128, 4], F32)
    t.B1C = sbuf("B1C", [128, 128], BF16)
    t.B3C = sbuf("B3C", [128, 128], BF16)
    t.B2C = sbuf("B2C", [128, 64], F32)
    t.BFC = sbuf("BFC", [128, 8], F32)
    t.L1B1C = sbuf("L1B1C", [128, 32], F32)
    t.L1B2C = sbuf("L1B2C", [128, 8], F32)
    t.BGR = sbuf("BGR", [1, 8], F32)
    t.WG = sbuf("WG", [128, 64], BF16)
    t.PB = [psum(f"PB{i}", [128, 512], F32) for i in range(7)]
    t.PT = psum("PT", [128, 1024], BF16)

    t.b_xt = Bufs("xt")
    t.b_xn = Bufs("xn")
    t.b_ar = Bufs("ar")
    t.b_ws = Bufs("ws")
    t.b_sb = Bufs("sb")
    t.b_sf = Bufs("sf")
    t.b_pb = Bufs("pb", xr=True)
    t.b_const = Bufs("const")
    t.b_dram = Bufs("dram")
    t.b_out = Bufs("out")
    return t


def ar_keys(t, s):
    if s in (8, 9):
        return [t.b_ar((89, r)) for r in range(4)]
    return [t.b_ar(s)]


def ar_bufs(t, c0, c1):
    out = []
    for s in range(c0 // 2048, (c1 + 2047) // 2048):
        out += ar_keys(t, s)
    return out


def arf_bufs(t, c0, c1):
    return ar_bufs(t, 2 * c0, 2 * c1)


def emit_consts(t):
    em, nc = t.em, t.nc
    g = lambda fn, **kw: em.op("gpsimd", fn, **kw)
    for nm, dst in [("ident_bf", t.IDB), ("ident_f32", t.IDF), ("triu_bf", t.TRIU),
                    ("b1c", t.B1C), ("b3c", t.B3C), ("b2c", t.B2C), ("bfc", t.BFC),
                    ("l1b1c", t.L1B1C), ("l1b2c", t.L1B2C), ("bgr", t.BGR)]:
        src = t.dram[nm]
        g(lambda h, d=dst, s=src: h.dma_start(out=d[:], in_=s.ap()),
          writes=[t.b_const(nm)], dma_slot=em.slot("c_" + nm))
    g(lambda h: h.dma_start(
        out=t.WG[:].rearrange("p (k e) -> p k e", k=8),
        in_=t.dram["wg"].ap().rearrange("(k p) e -> p k e", p=128)),
      writes=[t.b_const("wg")], dma_slot=em.slot("c_wg"))
    v = lambda fn, **kw: em.op("vector", fn, **kw)
    v(lambda h: h.memset(t.ONESC[:], 1.0), writes=[t.b_const("onesc")])
    v(lambda h: h.memset(t.CC[:, 0:1], 1e-5), writes=[t.b_const("cc0")])
    v(lambda h: h.memset(t.CC[:, 1:2], 1e-9), writes=[t.b_const("cc1")])
    v(lambda h: h.memset(t.CC[:, 2:3], SQ_BIAS), writes=[t.b_const("cc2")])


def emit_load_x(t):
    em = t.em
    t.x0t = t.nc.dram_tensor("x0t", [128, FT * TOK], BF16)
    xld_view = t.ARF
    for tt in range(TOK // 128):
        par = tt % 2
        stg = (xld_view, 1024 * par, [t.b_ar(par)])
        em.op("sync", lambda h, tt=tt, par=par: h.dma_start(
            out=xld_view[:, 1024 * par:1024 * (par + 1)],
            in_=t.dram["x0"].ap()[tt * 128:(tt + 1) * 128, :]),
            reads=[], writes=stg[2], dma_slot=em.slot(f"xld{par}"), tag="xld")
        sbslot = [t.b_sb(par)]
        em.op("vector", lambda h, par=par: h.tensor_copy(
            t.SB[:, 2048 * par:2048 * par + 1024], xld_view[:, 1024 * par:1024 * (par + 1)]),
            reads=stg[2], writes=sbslot, tag="xcvt")
        for f in range(FT):
            pb = t.b_pb("t")
            em.op("tensor", lambda h, par=par, f=f: h.transpose(
                t.PT[:, 512 * par:512 * par + 128],
                t.SB[:, 2048 * par + 128 * f: 2048 * par + 128 * (f + 1)], t.IDB[:]),
                reads=sbslot + [t.b_const("ident_bf")], writes=[pb], tag="xT")
            em.op("vector", lambda h, tt=tt, par=par, f=f: h.tensor_copy(
                t.XT[:, f * TOK + tt * 128: f * TOK + (tt + 1) * 128],
                t.PT[:, 512 * par:512 * par + 128]),
                reads=[pb], writes=[t.b_xt((f, tt // 4))], tag="xTe")
    for c in range(8):
        em.op("sync", lambda h, c=c: h.dma_start(
            out=t.x0t.ap().rearrange("p (f g ch) -> p f g ch", f=FT, g=8)[:, :, c, :],
            in_=t.XT[:].rearrange("p (f g ch) -> p f g ch", f=FT, g=8)[:, :, c, :]),
            reads=[t.b_xt((f, c)) for f in range(FT)], writes=[t.b_dram(("x0t", c))],
            dma_slot=em.slot(f"x0d{c}"), tag="x0dump")


def emit_ln(t, dst, dst_base, dst_stride, dst_bufs_fn, src_chunks, src="xt"):
    em = t.em
    for i, gch in enumerate(src_chunks):
        if src == "x0":
            em.op("sync", lambda h, gch=gch: h.dma_start(
                out=t.SB[:, 0:4096].rearrange("p (f ch) -> p f ch", f=FT),
                in_=t.x0t.ap().rearrange("p (f g ch) -> p f g ch", f=FT, g=8)[:, :, gch, :]),
                reads=[t.b_dram(("x0t", gch))], writes=[t.b_sb(0), t.b_sb(1)],
                dma_slot=em.slot("x0re"), tag="x0re")
            xsl = lambda f: t.SB[:, f * 512:(f + 1) * 512]
            xb = [t.b_sb(0), t.b_sb(1)]
            xbf = lambda f: [t.b_sb(f // 4)]
        else:
            xsl = lambda f: t.XT[:, f * TOK + gch * 512: f * TOK + (gch + 1) * 512]
            xb = [t.b_xt((f, gch)) for f in range(FT)]
            xbf = lambda f: [t.b_xt((f, gch))]
        for f in range(FT):
            xs = xsl(f)
            em.op("tensor", lambda h, xs=xs, f=f: h.matmul(
                t.PB[0][0:1, 0:512], t.ONESC[:], xs, start=(f == 0), stop=(f == FT - 1)),
                reads=xbf(f) + [t.b_const("onesc")], writes=[t.b_pb(0)], tag="lnsum")
        for f in range(FT):
            xs = xsl(f)
            sq = t.SB[:, 6144 + 1024 * (f % 2): 6144 + 1024 * (f % 2) + 512]
            sqb = t.b_sb(3)
            em.op("scalar", lambda h, xs=xs, sq=sq: h.activation(
                out=sq, in_=xs, func=AF.Square),
                reads=xbf(f), writes=[sqb], tag="lnsq")
            em.op("tensor", lambda h, sq=sq, f=f: h.matmul(
                t.PB[1][0:1, 0:512], t.ONESC[:], sq, start=(f == 0), stop=(f == FT - 1)),
                reads=[sqb, t.b_const("onesc")], writes=[t.b_pb(1)], tag="lnsq2")
        mrow = t.SF[0:1, 4 * 512:4 * 512 + 512]
        srow = t.SF[0:1, 5 * 512:5 * 512 + 512]
        vrow = t.SF[0:1, 3 * 512:3 * 512 + 512]
        em.op("vector", lambda h, mrow=mrow: h.tensor_scalar(
            out=mrow, in0=t.PB[0][0:1, 0:512], scalar1=1.0 / DM, scalar2=None, op0=OP.mult),
            reads=[t.b_pb(0)], writes=[t.b_sf(4)], tag="lnm")
        em.op("vector", lambda h, srow=srow: h.tensor_scalar(
            out=srow, in0=t.PB[1][0:1, 0:512], scalar1=1.0 / DM, scalar2=None, op0=OP.mult),
            reads=[t.b_pb(1)], writes=[t.b_sf(5)], tag="lns")
        em.op("vector", lambda h, mrow=mrow, vrow=vrow: h.tensor_tensor(
            out=vrow, in0=mrow, in1=mrow, op=OP.mult),
            reads=[t.b_sf(4)], writes=[t.b_sf(3)], tag="lnm2")
        em.op("vector", lambda h, srow=srow, vrow=vrow: h.tensor_tensor(
            out=vrow, in0=srow, in1=vrow, op=OP.subtract),
            reads=[t.b_sf(5), t.b_sf(3)], writes=[t.b_sf(3)], tag="lnv")
        em.op("scalar", lambda h, vrow=vrow: h.activation(
            out=vrow, in_=vrow, func=AF.Sqrt, bias=t.CC[0:1, 0:1]),
            reads=[t.b_sf(3), t.b_const("cc0")], writes=[t.b_sf(3)], tag="lnsd")
        em.op("vector", lambda h, vrow=vrow: h.reciprocal(vrow, vrow),
            reads=[t.b_sf(3)], writes=[t.b_sf(3)], tag="lnrs")
        em.op("tensor", lambda h, mrow=mrow: h.matmul(
            t.PB[2][:, 0:512], t.ONESF[0:1, :], mrow, start=True, stop=True, tile_position=(0, 0)),
            reads=[t.b_sf(4), t.b_const("onesf")], writes=[t.b_pb(2)], tag="lnbm")
        em.op("tensor", lambda h, vrow=vrow: h.matmul(
            t.PB[3][:, 0:512], t.ONESF[0:1, :], vrow, start=True, stop=True, tile_position=(0, 0)),
            reads=[t.b_sf(3), t.b_const("onesf")], writes=[t.b_pb(3)], tag="lnbr")
        for f in range(FT):
            xs = xsl(f)
            tmp = t.SF[:, 0:512]
            em.op("vector", lambda h, xs=xs, tmp=tmp: h.scalar_tensor_tensor(
                out=tmp, in0=xs, scalar=1.0, in1=t.PB[2][:, 0:512], op0=OP.mult, op1=OP.subtract),
                reads=xbf(f) + [t.b_pb(2)], writes=[t.b_sf(0)], tag="lnsub")
            dcol = dst_base + f * dst_stride + i * 512
            em.op("vector", lambda h, tmp=tmp, dcol=dcol: h.tensor_tensor(
                out=dst[:, dcol:dcol + 512], in0=tmp, in1=t.PB[3][:, 0:512], op=OP.mult),
                reads=[t.b_sf(0), t.b_pb(3)], writes=dst_bufs_fn(f, i), tag="lnmul")




def emit_attn_proj(t, b, half):
    em = t.em
    em.op("gpsimd", lambda h: h.dma_start(
        out=t.WS[:, 0:4096].rearrange("p (k m) -> p k m", k=8),
        in_=t.dram["wqp"].ap()),
        writes=[t.b_ws(0), t.b_ws(1)], dma_slot=em.slot("wqp"), tag="ldwq")
    em.op("gpsimd", lambda h: h.dma_start(
        out=t.WS[:, 4096:8192].rearrange("p (k m) -> p k m", k=8),
        in_=t.dram["wkp"].ap()),
        writes=[t.b_ws(2), t.b_ws(3)], dma_slot=em.slot("wkp"), tag="ldwk")
    em.op("gpsimd", lambda h: h.dma_start(
        out=t.WS[:, 8192:16384].rearrange("p (k m) -> p k m", k=8),
        in_=t.dram["wv"].ap()),
        writes=[t.b_ws(s) for s in range(4, 8)], dma_slot=em.slot("wv"), tag="ldwv")
    for dst_base, wbase, wbufs, dname in (
            (0, 0, [t.b_ws(0), t.b_ws(1)], "q"), (8192, 4096, [t.b_ws(2), t.b_ws(3)], "k")):
        for g in range(4):
            for nch in range(2):
                pb = 4 + (g * 2 + nch) % 2
                xnb = [t.b_xn((f, nch)) for f in range(FT)]
                em.op("tensor", lambda h, g=g, nch=nch, pb=pb, wbase=wbase: [
                    h.matmul(t.PB[pb][:, 0:512],
                             t.WS[:, wbase + fK * 512 + g * 128: wbase + fK * 512 + (g + 1) * 128],
                             t.XN[:, fK * HB + nch * 512: fK * HB + (nch + 1) * 512],
                             start=(fK == 0), stop=(fK == FT - 1)) for fK in range(FT)][-1],
                    reads=wbufs + xnb, writes=[t.b_pb(pb)], tag=f"proj{dname}")
                dcol = dst_base + g * 2048 + half * 1024 + nch * 512
                dslot = dcol // 2048
                em.op("vector", lambda h, dcol=dcol, pb=pb: h.tensor_copy(
                    t.AR[:, dcol:dcol + 512], t.PB[pb][:, 0:512]),
                    reads=[t.b_pb(pb)], writes=[t.b_ar(dslot)], tag=f"proj{dname}e")
    ob = 16384 + half * 8 * 1040 + 64
    oslots = ar_bufs(t, 16384, 33024)
    em.op("vector", lambda h, ob=ob: h.memset(
        t.AR[:, ob:ob + 8 * 1040].rearrange("p (tt x) -> p tt x", tt=8)[:, :, 0:1040:65].rearrange(
            "p tt hx -> p (tt hx)") if False else
        t.AR[:, ob - 64:ob - 64 + 8 * 1040].rearrange("p (tt hx c) -> p tt hx c", tt=8, hx=16)[:, :, :, 64:65],
        1.0), reads=[], writes=oslots, tag="vones")
    for tt_h in range(8):
        ttg = half * 8 + tt_h
        for vh in range(2):
            pb = 4 + (tt_h * 2 + vh) % 2
            xnb = [t.b_xn((f, tt_h // 4)) for f in range(FT)]
            em.op("tensor", lambda h, tt_h=tt_h, vh=vh, pb=pb: [
                h.matmul(t.PB[pb][:, 0:512],
                         t.XN[:, fK * HB + tt_h * 128: fK * HB + (tt_h + 1) * 128],
                         t.WS[:, 8192 + fK * 1024 + vh * 512: 8192 + fK * 1024 + (vh + 1) * 512],
                         start=(fK == 0), stop=(fK == FT - 1)) for fK in range(FT)][-1],
                reads=[t.b_ws(s) for s in range(4, 8)] + xnb, writes=[t.b_pb(pb)], tag="projv")
            dbase = 16384 + ttg * 1040 + vh * 8 * 65
            vslots = ar_bufs(t, dbase, dbase + 520)
            em.op("vector", lambda h, dbase=dbase, pb=pb: h.tensor_copy(
                t.AR[:, dbase:dbase + 520].rearrange("p (hp c) -> p hp c", hp=8)[:, :, 0:64],
                t.PB[pb][:, 0:512].rearrange("p (hp c) -> p hp c", hp=8)),
                reads=[t.b_pb(pb)], writes=vslots, tag="projve")


def emit_attn_loop(t, b):
    em = t.em
    vbufs = ar_bufs(t, 16384, 33024)

    def a_banks(chain):
        return (0, 1) if chain == 0 else (2, 4)

    def regions(chain):
        return (0, 1) if chain == 0 else (2, 3)

    def emit_A(hh, s, grp, k, chain):
        g, j = hh // 4, hh % 4
        pa = a_banks(chain)[k % 2]
        em.op("tensor", lambda h, grp=grp, pa=pa, g=g, j=j, s=s: [
            h.matmul(t.PB[pa][:, idx * 128:(idx + 1) * 128],
                     t.AR[32 * j:32 * j + 16, 8192 + g * 2048 + tc * 128: 8192 + g * 2048 + (tc + 1) * 128],
                     t.AR[32 * j:32 * j + 16, g * 2048 + s * 128: g * 2048 + (s + 1) * 128],
                     start=True, stop=True, tile_position=(32 * j, 0),
                     skip_group_check=True)
            for idx, tc in enumerate(grp)][-1],
            reads=[t.b_ar(g), t.b_ar(4 + g)], writes=[t.b_pb(pa)], tag="A")

    def emit_sqqk(s, grp, k, chain):
        pa = a_banks(chain)[k % 2]
        jj = regions(chain)[k % 2]
        sqc = 8192 + 2048 * jj
        qkc = sqc + 1024
        L = len(grp)
        em.op("scalar", lambda h, pa=pa, L=L, sqc=sqc: h.activation(
            out=t.WS[:, sqc:sqc + L * 128], in_=t.PB[pa][:, 0:L * 128],
            func=AF.Square, scale=SQ_SCALE, bias=t.CC[:, 2:3]),
            reads=[t.b_pb(pa), t.b_const("cc2")], writes=[t.b_ws(4 + jj)], tag="sq")
        if grp[-1] == s:
            em.op("vector", lambda h, L=L, sqc=sqc, qkc=qkc: h.scalar_tensor_tensor(
                out=t.WS[:, qkc:qkc + L * 128], in0=t.WS[:, sqc:sqc + L * 128],
                scalar=0.5, in1=t.TRIU[:], op0=OP.add, op1=OP.mult),
                reads=[t.b_ws(4 + jj), t.b_const("triu_bf")], writes=[t.b_ws(4 + jj)], tag="qk")
        else:
            em.op("vector", lambda h, L=L, sqc=sqc, qkc=qkc: h.tensor_scalar(
                out=t.WS[:, qkc:qkc + L * 128], in0=t.WS[:, sqc:sqc + L * 128],
                scalar1=0.5, scalar2=None, op0=OP.add),
                reads=[t.b_ws(4 + jj)], writes=[t.b_ws(4 + jj)], tag="qk")

    def emit_qkv(hh, s, grp, k, chain, first):
        jj = regions(chain)[k % 2]
        qkc = 8192 + 2048 * jj + 1024
        pq = 6 - chain
        for idx, tc in enumerate(grp):
            st = first and idx == 0
            last = (tc == s)
            em.op("tensor", lambda h, tc=tc, idx=idx, hh=hh, st=st, last=last, qkc=qkc, pq=pq: h.matmul(
                t.PB[pq][0:65, 0:128],
                t.AR[:, 16384 + tc * 1040 + hh * 65: 16384 + tc * 1040 + (hh + 1) * 65],
                t.WS[:, qkc + idx * 128: qkc + (idx + 1) * 128],
                start=st, stop=last, skip_group_check=True),
                reads=vbufs + [t.b_ws(4 + jj)], writes=[t.b_pb(pq)], tag="qkv")

    def emit_tail(hh, s, chain):
        pq = 6 - chain
        c0 = 2560 if chain == 0 else 2048
        sfb = 5 if chain == 0 else 4
        dnr = t.SF[64:65, c0:c0 + 128]
        qsb = t.SF[0:64, c0:c0 + 128]
        em.op("scalar", lambda h, dnr=dnr, pq=pq: h.activation(
            out=dnr, in_=t.PB[pq][64:65, 0:128], func=AF.Identity, bias=t.CC[64:65, 1:2]),
            reads=[t.b_pb(pq), t.b_const("cc1")], writes=[t.b_sf(sfb)], tag="dn")
        em.op("vector", lambda h, dnr=dnr: h.reciprocal(dnr, dnr),
            reads=[t.b_sf(sfb)], writes=[t.b_sf(sfb)], tag="dr")
        em.op("tensor", lambda h, dnr=dnr: h.matmul(
            t.PB[3][0:64, 0:128], t.ONESF[64:65, 0:64], dnr,
            start=True, stop=True, tile_position=(64, 0), skip_group_check=True),
            reads=[t.b_sf(sfb), t.b_const("onesf")], writes=[t.b_pb(3)], tag="dbc")
        em.op("scalar", lambda h, qsb=qsb, pq=pq: h.copy(out=qsb, in_=t.PB[pq][0:64, 0:128]),
            reads=[t.b_pb(pq)], writes=[t.b_sf(sfb)], tag="qev")
        acol = 4096 + (hh // 2) * 512 + (s % 4) * 128
        arow = 64 * (hh % 2)
        em.op("vector", lambda h, qsb=qsb, acol=acol, arow=arow: h.tensor_tensor(
            out=t.WS[arow:arow + 64, acol:acol + 128], in0=qsb,
            in1=t.PB[3][0:64, 0:128], op=OP.mult),
            reads=[t.b_sf(sfb), t.b_pb(3)], writes=[t.b_ws(2 + (acol - 4096) // 2048)], tag="att")

    def groups_of(s):
        grps = [list(range(j0, min(j0 + 4, s))) for j0 in range(0, s, 4)]
        return [gg for gg in grps if gg] + [[s]]

    INTERLEAVE = ATTN_IL
    for sg in range(4):
        for hh in range(H):
            for pair in range(2):
                sa, sb = sg * 4 + 2 * pair, sg * 4 + 2 * pair + 1
                ga, gb = groups_of(sa), groups_of(sb)
                na, nb = len(ga), len(gb)
                if not INTERLEAVE:
                    for s, gl, ch in ((sa, ga, 0), (sb, gb, 1)):
                        for k, grp in enumerate(gl):
                            emit_A(hh, s, grp, k, ch)
                            emit_sqqk(s, grp, k, ch)
                            emit_qkv(hh, s, grp, k, ch, first=(k == 0))
                        emit_tail(hh, s, ch)
                    continue
                for k in range(max(na, nb) + 1):
                    if k < na:
                        emit_A(hh, sa, ga[k], k, 0)
                    if k < nb:
                        emit_A(hh, sb, gb[k], k, 1)
                    if k < na:
                        emit_sqqk(sa, ga[k], k, 0)
                    if k < nb:
                        emit_sqqk(sb, gb[k], k, 1)
                    if 1 <= k <= na:
                        emit_qkv(hh, sa, ga[k - 1], k - 1, 0, first=(k == 1))
                        if k == na:
                            emit_tail(hh, sa, 0)
                    if 1 <= k <= nb:
                        emit_qkv(hh, sb, gb[k - 1], k - 1, 1, first=(k == 1))
                        if k == nb:
                            emit_tail(hh, sb, 1)
        for mh in range(2):
            for fK in range(FT):
                par = fK % 2
                em.op("gpsimd", lambda h, fK=fK, par=par: h.dma_start(
                    out=t.WS[:, par * 2048: par * 2048 + 1024],
                    in_=t.dram["wo"].ap()[fK * 128:(fK + 1) * 128, :]),
                    writes=[t.b_ws(par)], dma_slot=em.slot(f"wos{par}"), tag="ldwo")
                for m in range(4):
                    mm = mh * 4 + m
                    em.op("tensor", lambda h, fK=fK, par=par, m=m, mm=mm: h.matmul(
                        t.PB[3 + m][:, 0:512],
                        t.WS[:, par * 2048 + mm * 128: par * 2048 + (mm + 1) * 128],
                        t.WS[:, 4096 + fK * 512: 4096 + (fK + 1) * 512],
                        start=(fK == 0), stop=(fK == FT - 1), skip_group_check=True),
                        reads=[t.b_ws(par), t.b_ws(2), t.b_ws(3)], writes=[t.b_pb(3 + m)], tag="wo")
            for m in range(4):
                mm = mh * 4 + m
                xcol = mm * TOK + b * S + sg * 512
                gch = (b * S + sg * 512) // 512
                em.op("vector", lambda h, xcol=xcol, m=m: h.tensor_tensor(
                    out=t.XT[:, xcol:xcol + 512], in0=t.XT[:, xcol:xcol + 512],
                    in1=t.PB[3 + m][:, 0:512], op=OP.add),
                    reads=[t.b_pb(3 + m), t.b_xt((mm, gch))], writes=[t.b_xt((mm, gch))], tag="woadd")
def emit_gating(t, b, half):
    em = t.em
    for tt_h in range(8):
        xnb = [t.b_xn((f, tt_h // 4)) for f in range(FT)]
        em.op("tensor", lambda h, tt_h=tt_h: [
            h.matmul(t.PB[6][:, 0:8],
                     t.XN[:, fK * HB + tt_h * 128: fK * HB + (tt_h + 1) * 128],
                     t.WG[:, fK * 8:(fK + 1) * 8],
                     start=(fK == 0), stop=False, skip_group_check=True) for fK in range(FT)][-1],
            reads=xnb + [t.b_const("wg")], writes=[t.b_pb(6)], tag="gmm")
        em.op("tensor", lambda h: h.matmul(
            t.PB[6][:, 0:8], t.ONESF[0:1, 0:128], t.BGR[:],
            start=False, stop=True, tile_position=(0, 0), skip_group_check=True),
            reads=[t.b_const("onesf"), t.b_const("bgr")], writes=[t.b_pb(6)], tag="gbias")
        s8 = t.SF[:, 2080:2088]
        nmax = t.SF[:, 2088:2089]
        sumr = t.SF[:, 2089:2090]
        rec = t.SF[:, 2090:2091]
        probs = t.SF[:, 2096:2104]
        mask = t.SF[:, 2104:2112]
        em.op("vector", lambda h, s8=s8: h.max(s8, t.PB[6][:, 0:8]),
            reads=[t.b_pb(6)], writes=[t.b_sf(4)], tag="gmax")
        em.op("vector", lambda h, s8=s8, nmax=nmax: h.tensor_scalar(
            out=nmax, in0=s8[:, 0:1], scalar1=-1.0, scalar2=None, op0=OP.mult),
            reads=[t.b_sf(4)], writes=[t.b_sf(4)], tag="gnm")
        em.op("scalar", lambda h, probs=probs, nmax=nmax, sumr=sumr: h.activation(
            out=probs, in_=t.PB[6][:, 0:8], func=AF.Exp, bias=nmax, accum_out=sumr),
            reads=[t.b_pb(6), t.b_sf(4)], writes=[t.b_sf(4)], tag="gexp")
        em.op("vector", lambda h, rec=rec, sumr=sumr: h.reciprocal(rec, sumr),
            reads=[t.b_sf(4)], writes=[t.b_sf(4)], tag="grec")
        em.op("vector", lambda h, probs=probs, rec=rec: h.tensor_scalar(
            out=probs, in0=probs, scalar1=rec, scalar2=None, op0=OP.mult),
            reads=[t.b_sf(4)], writes=[t.b_sf(4)], tag="gnorm")
        em.op("vector", lambda h, mask=mask, s8=s8: h.tensor_scalar(
            out=mask, in0=t.PB[6][:, 0:8], scalar1=s8[:, 1:2], scalar2=None, op0=OP.is_ge),
            reads=[t.b_pb(6), t.b_sf(4)], writes=[t.b_sf(4)], tag="gmask")
        wcol = half * 64 + tt_h * 8
        em.op("vector", lambda h, probs=probs, mask=mask, wcol=wcol: h.tensor_tensor(
            out=t.WB[:, wcol:wcol + 8], in0=probs, in1=mask, op=OP.mult),
            reads=[t.b_sf(4)], writes=[t.b_const("wb")], tag="gw")


def emit_moe(t, b, half):
    em = t.em
    def build_wbc(e):
        if e >= NE:
            return
        for tb in range(2):
            rk = (89, (e % 2) * 2 + tb)
            rcol = 8192 + (e % 2) * 1024 + tb * 512
            em.op("tensor", lambda h, e=e, tb=tb, half=half: [
                h.matmul(t.PB[6][0:1, tq * 128:(tq + 1) * 128],
                         t.WB[:, half * 64 + (tb * 4 + tq) * 8 + e: half * 64 + (tb * 4 + tq) * 8 + e + 1],
                         t.IDF[:], start=True, stop=True, skip_group_check=True)
                for tq in range(4)][-1],
                reads=[t.b_const("wb"), t.b_const("ident_f32")], writes=[t.b_pb(6)], tag="wrmm")
            wr = t.SF[0:1, 2048:2560]
            em.op("vector", lambda h, wr=wr: h.tensor_copy(wr, t.PB[6][0:1, 0:512]),
                reads=[t.b_pb(6)], writes=[t.b_sf(4)], tag="wrcp")
            em.op("tensor", lambda h, wr=wr: h.matmul(
                t.PB[6][:, 0:512], t.ONESF[0:1, 0:128], wr,
                start=True, stop=True, tile_position=(0, 0), skip_group_check=True),
                reads=[t.b_sf(4), t.b_const("onesf")], writes=[t.b_pb(6)], tag="wbcmm")
            em.op("vector", lambda h, rcol=rcol, rk=rk: h.tensor_copy(
                t.ARF[:, rcol:rcol + 512], t.PB[6][:, 0:512]),
                reads=[t.b_pb(6)], writes=[t.b_ar(rk)], tag="wbc")

    for e in range(NE):
        if e == 0:
            build_wbc(0)
        for hf in range(2):
            em.op("gpsimd", lambda h, e=e, hf=hf: h.dma_start(
                out=t.WS[:, 8192:16384].rearrange("p (k n) -> p k n", k=8),
                in_=t.dram["w2"].ap()[e, hf]),
                writes=[t.b_ws(s) for s in range(4, 8)], dma_slot=em.slot("w2h"), tag="ldw2")
            for oct in range(4):
                par = oct % 2
                kbase = hf * 8 + oct * 2
                em.op("gpsimd", lambda h, e=e, kbase=kbase, par=par: h.dma_start(
                    out=t.WS[:, par * 2048: (par + 1) * 2048].rearrange("p (k n) -> p k n", k=8),
                    in_=t.dram["w1"].ap()[e, kbase // 2]),
                    writes=[t.b_ws(par)], dma_slot=em.slot(f"w1o{par}"), tag="ldw1")
                em.op("gpsimd", lambda h, e=e, kbase=kbase, par=par: h.dma_start(
                    out=t.WS[:, 4096 + par * 2048: 4096 + (par + 1) * 2048].rearrange("p (k n) -> p k n", k=8),
                    in_=t.dram["w3"].ap()[e, kbase // 2]),
                    writes=[t.b_ws(2 + par)], dma_slot=em.slot(f"w3o{par}"), tag="ldw3")
                for krel in range(2):
                    kih = oct * 2 + krel
                    kg = hf * 8 + kih
                    for tb in range(2):
                        it = kih * 2 + tb
                        pa, pb_ = 2 * (it % 2), 2 * (it % 2) + 1
                        xnb = [t.b_xn((f, tb)) for f in range(FT)]
                        em.op("tensor", lambda h, par=par, krel=krel, tb=tb, pa=pa: [
                            h.matmul(t.PB[pa][:, 0:512],
                                     t.WS[:, par * 2048 + fK * 256 + krel * 128: par * 2048 + fK * 256 + (krel + 1) * 128],
                                     t.XN[:, fK * HB + tb * 512: fK * HB + (tb + 1) * 512],
                                     start=(fK == 0), stop=(fK == FT - 1)) for fK in range(FT)][-1],
                            reads=[t.b_ws(par)] + xnb, writes=[t.b_pb(pa)], tag="h1")
                        em.op("tensor", lambda h, par=par, krel=krel, tb=tb, pb_=pb_: [
                            h.matmul(t.PB[pb_][:, 0:512],
                                     t.WS[:, 4096 + par * 2048 + fK * 256 + krel * 128: 4096 + par * 2048 + fK * 256 + (krel + 1) * 128],
                                     t.XN[:, fK * HB + tb * 512: fK * HB + (tb + 1) * 512],
                                     start=(fK == 0), stop=(fK == FT - 1)) for fK in range(FT)][-1],
                            reads=[t.b_ws(2 + par)] + xnb, writes=[t.b_pb(pb_)], tag="h3")
                        h3b = t.SF[:, 1024:1536]
                        em.op("scalar", lambda h, pb_=pb_, e=e, kg=kg, h3b=h3b: h.activation(
                            out=h3b, in_=t.PB[pb_][:, 0:512], func=AF.Identity,
                            bias=t.B3C[:, e * 16 + kg: e * 16 + kg + 1]),
                            reads=[t.b_pb(pb_), t.b_const("b3c")], writes=[t.b_sf(2)], tag="h3b")
                        tt_ = t.SF[:, 1536:2048]
                        em.op("vector", lambda h, pa=pa, e=e, kg=kg, h3b=h3b, tt_=tt_: h.scalar_tensor_tensor(
                            out=tt_, in0=t.PB[pa][:, 0:512], scalar=t.B1C[:, e * 16 + kg: e * 16 + kg + 1],
                            in1=h3b, op0=OP.add, op1=OP.mult),
                            reads=[t.b_pb(pa), t.b_const("b1c"), t.b_sf(2)], writes=[t.b_sf(3)], tag="hm")
                        uu = t.SF[:, 1024:1536]
                        em.op("scalar", lambda h, tt_=tt_, uu=uu: h.activation(
                            out=uu, in_=tt_, func=AF.Silu),
                            reads=[t.b_sf(3)], writes=[t.b_sf(2)], tag="silu")
                        whc = kih * 1024 + tb * 512
                        rcol = 8192 + (e % 2) * 1024 + tb * 512
                        em.op("vector", lambda h, uu=uu, rcol=rcol, whc=whc: h.tensor_tensor(
                            out=t.SB[:, whc:whc + 512], in0=uu,
                            in1=t.ARF[:, rcol:rcol + 512], op=OP.mult),
                            reads=[t.b_sf(2), t.b_ar((89, (e % 2) * 2 + tb))],
                            writes=[t.b_sb(whc // 2048)], tag="wh")
            if hf == 0:
                build_wbc(e + 1)
            for tb in range(2):
                for m in range(FT):
                    pp = 4 + (m % 2)
                    em.op("tensor", lambda h, tb=tb, m=m, pp=pp: [
                        h.matmul(t.PB[pp][:, 0:512],
                                 t.WS[:, 8192 + kih * 1024 + m * 128: 8192 + kih * 1024 + (m + 1) * 128],
                                 t.SB[:, kih * 1024 + tb * 512: kih * 1024 + tb * 512 + 512],
                                 start=(kih == 0), stop=(kih == 7)) for kih in range(8)][-1],
                        reads=[t.b_ws(s) for s in range(4, 8)] + [t.b_sb(i) for i in range(4)],
                        writes=[t.b_pb(pp)], tag="w2")
                    mcol = tb * 4096 + m * 512
                    mslot = [t.b_ar(mcol // 1024)]
                    if e == 0 and hf == 0:
                        em.op("vector", lambda h, mcol=mcol, pp=pp: h.tensor_copy(
                            t.ARF[:, mcol:mcol + 512], t.PB[pp][:, 0:512]),
                            reads=[t.b_pb(pp)], writes=mslot, tag="mxi")
                    else:
                        em.op("vector", lambda h, mcol=mcol, pp=pp: h.tensor_tensor(
                            out=t.ARF[:, mcol:mcol + 512], in0=t.ARF[:, mcol:mcol + 512],
                            in1=t.PB[pp][:, 0:512], op=OP.add),
                            reads=[t.b_pb(pp)] + mslot, writes=mslot, tag="mxa")
    for tb in range(2):
        for fK in range(FT):
            mcol = tb * 4096 + fK * 512
            em.op("vector", lambda h, mcol=mcol, fK=fK: h.tensor_copy(
                t.SB[:, fK * 512:(fK + 1) * 512], t.ARF[:, mcol:mcol + 512]),
                reads=[t.b_ar(mcol // 1024)], writes=[t.b_sb(fK // 4)], tag="mxc")
        for m in range(FT):
            par = m % 2
            em.op("gpsimd", lambda h, m=m, par=par: h.dma_start(
                out=t.WS[:, par * 1024:(par + 1) * 1024].rearrange("p (k n) -> p k n", k=8),
                in_=t.dram["wf"].ap()[m]),
                writes=[t.b_ws(0)], dma_slot=em.slot(f"wfs{par}"), tag="ldwf")
            pp = 4 + par
            em.op("tensor", lambda h, m=m, par=par, pp=pp: [
                h.matmul(t.PB[pp][:, 0:512],
                         t.WS[:, par * 1024 + fK * 128: par * 1024 + (fK + 1) * 128],
                         t.SB[:, fK * 512:(fK + 1) * 512],
                         start=(fK == 0), stop=(fK == FT - 1)) for fK in range(FT)][-1],
                reads=[t.b_ws(0), t.b_sb(0), t.b_sb(1)], writes=[t.b_pb(pp)], tag="wf")
            fo = t.SF[:, 1024:1536]
            em.op("scalar", lambda h, m=m, pp=pp, fo=fo: h.activation(
                out=fo, in_=t.PB[pp][:, 0:512], func=AF.Identity, bias=t.BFC[:, m:m + 1]),
                reads=[t.b_pb(pp), t.b_const("bfc")], writes=[t.b_sf(2)], tag="wfb")
            xcol = m * TOK + b * S + half * 1024 + tb * 512
            gch = (b * S + half * 1024 + tb * 512) // 512
            em.op("vector", lambda h, xcol=xcol, fo=fo: h.tensor_tensor(
                out=t.XT[:, xcol:xcol + 512], in0=t.XT[:, xcol:xcol + 512], in1=fo, op=OP.add),
                reads=[t.b_sf(2), t.b_xt((m, gch))], writes=[t.b_xt((m, gch))], tag="wfadd")


def emit_l1_attn(t, b):
    em = t.em
    xnb_all = [t.b_ar(s) for s in range(8)]
    for et in range(FT):
        par = et % 2
        em.op("gpsimd", lambda h, et=et, par=par: h.dma_start(
            out=t.WS[:, par * 2048:(par + 1) * 2048].rearrange("p (mt k n) -> p mt k n", mt=2, k=8),
            in_=t.dram["wft"].ap()[et]),
            writes=[t.b_ws(par)], dma_slot=em.slot(f"wfts{par}"), tag="ldwft")
        for ch in range(4):
            pk = 0 + (ch % 2)
            ph = 2 + (ch % 2)
            em.op("tensor", lambda h, par=par, ch=ch, pk=pk: [
                h.matmul(t.PB[pk][:, 0:512],
                         t.WS[:, par * 2048 + fK * 128: par * 2048 + (fK + 1) * 128],
                         t.AR[:, fK * 2048 + ch * 512: fK * 2048 + (ch + 1) * 512],
                         start=(fK == 0), stop=(fK == FT - 1)) for fK in range(FT)][-1],
                reads=[t.b_ws(par)] + xnb_all, writes=[t.b_pb(pk)], tag="kf")
            em.op("tensor", lambda h, par=par, ch=ch, ph=ph: [
                h.matmul(t.PB[ph][:, 0:512],
                         t.WS[:, par * 2048 + 1024 + fK * 128: par * 2048 + 1024 + (fK + 1) * 128],
                         t.AR[:, fK * 2048 + ch * 512: fK * 2048 + (ch + 1) * 512],
                         start=(fK == 0), stop=(fK == FT - 1)) for fK in range(FT)][-1],
                reads=[t.b_ws(par)] + xnb_all, writes=[t.b_pb(ph)], tag="hx")
            if L1A_LIMIT < 2:
                continue
            aa = t.SF[:, 0:512]
            zz = t.SF[:, 512:1024]
            sg_ = t.SF[:, 1024:1536]
            lg = t.SF[:, 1536:2048]
            em.op("scalar", lambda h, pk=pk, aa=aa: h.activation(
                out=aa, in_=t.PB[pk][:, 0:512], func=AF.Sigmoid, scale=-1.0),
                reads=[t.b_pb(pk)], writes=[t.b_sf(0)], tag="siga")
            em.op("scalar", lambda h, pk=pk, zz=zz: h.activation(
                out=zz, in_=t.PB[pk][:, 0:512], func=AF.Sigmoid),
                reads=[t.b_pb(pk)], writes=[t.b_sf(1)], tag="sigz")
            em.op("scalar", lambda h, ph=ph, sg_=sg_: h.activation(
                out=sg_, in_=t.PB[ph][:, 0:512], func=AF.Sigmoid),
                reads=[t.b_pb(ph)], writes=[t.b_sf(2)], tag="sigg")
            if L1A_LIMIT < 3:
                continue
            em.op("vector", lambda h, ph=ph, lg=lg: h.tensor_scalar(
                out=lg, in0=t.PB[ph][:, 0:512], scalar1=0.5, scalar2=None, op0=OP.add),
                reads=[t.b_pb(ph)], writes=[t.b_sf(3)], tag="lin")
            em.op("vector", lambda h, sg_=sg_, lg=lg: h.tensor_tensor(
                out=lg, in0=sg_, in1=lg, op=OP.max),
                reads=[t.b_sf(2), t.b_sf(3)], writes=[t.b_sf(3)], tag="gmx")
            em.op("vector", lambda h, zz=zz, lg=lg: h.tensor_tensor(
                out=zz, in0=zz, in1=lg, op=OP.mult),
                reads=[t.b_sf(1), t.b_sf(3)], writes=[t.b_sf(1)], tag="bv")
            if L1A_LIMIT < 4:
                continue
            hp = 4 + (ch % 2)
            hprev = 4 + ((ch + 1) % 2)
            init = 0.5 if ch == 0 else t.SF[:, hprev * 512 + 511: hprev * 512 + 512]
            rd = [t.b_sf(0), t.b_sf(1)] + ([] if ch == 0 else [t.b_sf(hprev)])
            if SCAN_FAKE:
                em.op("vector", lambda h, aa=aa, zz=zz, hp=hp: h.tensor_tensor(
                    out=t.SF[:, hp * 512:(hp + 1) * 512], in0=aa, in1=zz, op=OP.add),
                    reads=rd, writes=[t.b_sf(hp)], tag="scan")
            else:
                em.op("vector", lambda h, aa=aa, zz=zz, hp=hp, init=init: h.tensor_tensor_scan(
                    out=t.SF[:, hp * 512:(hp + 1) * 512], data0=aa, data1=zz,
                    initial=init, op0=OP.mult, op1=OP.add),
                    reads=rd, writes=[t.b_sf(hp)], tag="scan")
            if L1A_LIMIT < 5:
                continue
            xcol = et * TOK + b * S + ch * 512
            gch = (b * S + ch * 512) // 512
            em.op("vector", lambda h, xcol=xcol, hp=hp: h.tensor_tensor(
                out=t.XT[:, xcol:xcol + 512], in0=t.XT[:, xcol:xcol + 512],
                in1=t.SF[:, hp * 512:(hp + 1) * 512], op=OP.add),
                reads=[t.b_sf(hp), t.b_xt((et, gch))], writes=[t.b_xt((et, gch))], tag="hadd")


def emit_l1_ffn(t, b):
    em = t.em
    xnb_all = [t.b_ar(s) for s in range(8)]
    for ch in range(4):
        gch = (b * S + ch * 512) // 512
        for mh in range(32):
            par = mh % 2
            em.op("gpsimd", lambda h, mh=mh, par=par: h.dma_start(
                out=t.WS[:, 4096 + par * 1024: 4096 + (par + 1) * 1024].rearrange("p (k n) -> p k n", k=8),
                in_=t.dram["l1w1"].ap()[mh]),
                writes=[t.b_ws(2 + par)], dma_slot=em.slot(f"l1w1{par}"), tag="ldl1a")
            pk = par
            em.op("tensor", lambda h, mh=mh, par=par, ch=ch, pk=pk: [
                h.matmul(t.PB[pk][:, 0:512],
                         t.WS[:, 4096 + par * 1024 + fK * 128: 4096 + par * 1024 + (fK + 1) * 128],
                         t.AR[:, fK * 2048 + ch * 512: fK * 2048 + (ch + 1) * 512],
                         start=(fK == 0), stop=(fK == FT - 1)) for fK in range(FT)][-1],
                reads=[t.b_ws(2 + par)] + xnb_all, writes=[t.b_pb(pk)], tag="f1")
            em.op("scalar", lambda h, mh=mh, pk=pk: h.activation(
                out=t.AR[:, 16384 + mh * 512: 16384 + (mh + 1) * 512], in_=t.PB[pk][:, 0:512],
                func=AF.Relu, bias=t.L1B1C[:, mh:mh + 1]),
                reads=[t.b_pb(pk), t.b_const("l1b1c")],
                writes=[t.b_ar(8 + mh // 4)], tag="relu")
        rhb = [t.b_ar(8 + i) for i in range(8)]
        for mo in range(FT):
            par = mo % 2
            wsl = (4 + 2 * par, 6 + 2 * par)
            em.op("gpsimd", lambda h, mo=mo, par=par: h.dma_start(
                out=t.WS[:, 8192 + par * 4096: 8192 + (par + 1) * 4096].rearrange("p (k n) -> p k n", k=32),
                in_=t.dram["l1w2"].ap()[mo]),
                writes=[t.b_ws(wsl[0]), t.b_ws(wsl[0] + 1)], dma_slot=em.slot(f"l1w2{par}"), tag="ldl1b")
            po = 2 + par
            em.op("tensor", lambda h, mo=mo, par=par, po=po: [
                h.matmul(t.PB[po][:, 0:512],
                         t.WS[:, 8192 + par * 4096 + k * 128: 8192 + par * 4096 + (k + 1) * 128],
                         t.AR[:, 16384 + k * 512: 16384 + (k + 1) * 512],
                         start=(k == 0), stop=(k == 31)) for k in range(32)][-1],
                reads=[t.b_ws(wsl[0]), t.b_ws(wsl[0] + 1)] + rhb, writes=[t.b_pb(po)], tag="f2")
            xa = t.SF[:, (par) * 512:(par + 1) * 512]
            em.op("scalar", lambda h, mo=mo, po=po, xa=xa: h.activation(
                out=xa, in_=t.PB[po][:, 0:512], func=AF.Identity, bias=t.L1B2C[:, mo:mo + 1]),
                reads=[t.b_pb(po), t.b_const("l1b2c")], writes=[t.b_sf(par)], tag="ob")
            of = t.SF[:, 1024 + par * 512: 1024 + (par + 1) * 512]
            xcol = mo * TOK + b * S + ch * 512
            em.op("vector", lambda h, xa=xa, of=of, xcol=xcol: h.tensor_tensor(
                out=of, in0=xa, in1=t.XT[:, xcol:xcol + 512], op=OP.add),
                reads=[t.b_sf(par), t.b_xt((mo, gch))], writes=[t.b_sf(2 + par)], tag="osum")
            for tq in range(4):
                em.op("tensor", lambda h, of=of, tq=tq: h.transpose(
                    t.PB[6][:, tq * 128:(tq + 1) * 128], of[:, tq * 128:(tq + 1) * 128], t.IDF[:]),
                    reads=[t.b_sf(2 + par), t.b_const("ident_f32")], writes=[t.b_pb(6)], tag="oT")
            ost = t.SF[:, 2048 + par * 512: 2048 + (par + 1) * 512]
            em.op("vector", lambda h, ost=ost: h.tensor_copy(ost, t.PB[6][:, 0:512]),
                reads=[t.b_pb(6)], writes=[t.b_sf(4 + par)], tag="oTe")
            r0 = b * S + ch * 512
            em.op("sync", lambda h, ost=ost, r0=r0, mo=mo: h.dma_start(
                out=t.out.ap().rearrange("(tq p) n -> p tq n", p=128)[:, r0 // 128: r0 // 128 + 4, mo * 128:(mo + 1) * 128],
                in_=ost.rearrange("p (tq n) -> p tq n", tq=4)),
                reads=[t.b_sf(4 + par)], writes=[t.b_out((b, ch, mo))], dma_slot=em.slot(f"out{par}"), tag="ost")


def emit_tap(t, name, src_ap, reads):
    em = t.em
    em.op("gpsimd", lambda h: h.dma_start(out=t.dbg[name].ap(), in_=src_ap),
        reads=reads, writes=[], dma_slot=em.slot("dbgs"), tag="tap")


def emit_dump_x1(t):
    em = t.em
    x1f = t.nc.dram_tensor("x1f", [128, FT * TOK], F32, kind="ExternalOutput")
    for c in range(8):
        rd = [t.b_xt((f, c)) for f in range(FT)]
        em.op("gpsimd", lambda h, c=c: h.dma_start(
            out=x1f.ap().rearrange("p (f g ch) -> p f g ch", f=FT, g=8)[:, :, c, :],
            in_=t.XT[:].rearrange("p (f g ch) -> p f g ch", f=FT, g=8)[:, :, c, :]),
            reads=rd, writes=[], dma_slot=em.slot(f"x1d{c % 2}"), tag="x1d")


def emit_load_x1(t):
    em = t.em
    x1f = t.nc.dram_tensor("x1f", [128, FT * TOK], F32, kind="ExternalInput")
    for c in range(8):
        em.op("gpsimd", lambda h, c=c: h.dma_start(
            out=t.XT[:].rearrange("p (f g ch) -> p f g ch", f=FT, g=8)[:, :, c, :],
            in_=x1f.ap().rearrange("p (f g ch) -> p f g ch", f=FT, g=8)[:, :, c, :]),
            reads=[], writes=[t.b_xt((f, c)) for f in range(FT)],
            dma_slot=em.slot(f"x1l{c}"), tag="x1l")


def build_program(nc, taps=(), parts=("load", "l0", "l1")):
    dbg_specs = []
    for tp in taps:
        if tp.startswith("xt"):
            dbg_specs.append((tp, [128, FT * TOK]))
        elif tp.startswith("xn1"):
            dbg_specs.append((tp, [128, 16384]))
        elif tp.startswith("xn"):
            dbg_specs.append((tp, [128, FT * HB]))
    t = setup_tensors(nc, dbg=dbg_specs)
    t.ONESF = t.IDF
    em = t.em
    c = nc.sbuf_tensor("ONESF", [128, 128], F32)
    t.ctxs.append(c)
    t.ONESF = c.__enter__()
    em.op("vector", lambda h: h.memset(t.ONESF[:], 1.0), writes=[t.b_const("onesf")])
    emit_consts(t)
    if "load" in parts:
        emit_load_x(t)
        if "xt0" in taps:
            emit_tap(t, "xt0", t.XT[:], [t.b_xt((f, c)) for f in range(FT) for c in range(8)])
    if "lntest" in parts:
        emit_ln(t, t.XN, 0, HB, lambda f, i: [t.b_xn((f, i))], [0, 1])
        emit_tap(t, "xn0", t.XN[:], [t.b_xn((f, i)) for f in range(FT) for i in range(2)])
    if "loadx1" in parts:
        emit_load_x1(t)
    do_attn = ("l0" in parts) or ("l0attn" in parts)
    do_moe = ("l0" in parts) or ("l0moe" in parts)
    if do_attn or do_moe:
        for b in (B_LIST if B_LIST is not None else range(B)):
            if do_attn:
                for half in range(2):
                    chunks = [(b * S + half * 1024) // 512, (b * S + half * 1024) // 512 + 1]
                    emit_ln(t, t.XN, 0, HB, lambda f, i: [t.b_xn((f, i))], chunks)
                    if half == 0 and b == 0 and "xn0" in taps:
                        emit_tap(t, "xn0", t.XN[:],
                                 [t.b_xn((f, i)) for f in range(FT) for i in range(2)])
                    emit_attn_proj(t, b, half)
                emit_attn_loop(t, b)
            if do_moe:
                for half in range(2):
                    chunks = [(b * S + half * 1024) // 512, (b * S + half * 1024) // 512 + 1]
                    emit_ln(t, t.XN, 0, HB, lambda f, i: [t.b_xn((f, i))], chunks,
                            src=("x0" if do_attn else "xt"))
                    emit_gating(t, b, half)
                    emit_moe(t, b, half)
        if "xt1" in taps:
            emit_tap(t, "xt1", t.XT[:], [t.b_xt((f, c)) for f in range(FT) for c in range(8)])
    if "dumpx1" in parts:
        emit_dump_x1(t)
    if "l1" in parts:
        for b in (B_LIST if B_LIST is not None else range(B)):
            chunks = [b * 4 + i for i in range(4)]
            emit_ln(t, t.AR, 0, 2048, lambda f, i: [t.b_ar(f)], chunks)
            if b == 0 and "xn1a" in taps:
                emit_tap(t, "xn1a", t.AR[:, 0:16384],
                         [t.b_ar(s) for s in range(8)])
            emit_l1_attn(t, b)
            emit_l1_ffn(t, b)
    em.barrier()
    sem_ctxs = em.finalize()
    return t, sem_ctxs


FUSED = True


def _run_build(parts, in_maps, extra=None):
    nc = bass.Bass("TRN2", target_bir_lowering=False)
    t, sem_ctxs = build_program(nc, parts=parts)
    res = run_bass_kernel_spmd(nc, in_maps, core_ids=list(range(NCORES)))
    for c in reversed(sem_ctxs):
        c.__exit__(None, None, None)
    for c in reversed(t.ctxs):
        c.__exit__(None, None, None)
    return res


def kernel(**inputs):
    global B_LIST
    in_maps = [prep_inputs(core, inputs) for core in range(NCORES)]
    if FUSED:
        res = _run_build(("load", "l0", "l1"), in_maps)
        shards = [res.results[i]["out"].reshape(B, S, DM) for i in range(NCORES)]
    else:
        outs = []
        for bsel in range(B):
            B_LIST = [bsel]
            res = _run_build(("load", "l0", "l1"), in_maps)
            outs.append([res.results[i]["out"].reshape(B, S, DM)[bsel] for i in range(NCORES)])
        B_LIST = None
        shards = [np.stack([outs[0][i], outs[1][i]], axis=0) for i in range(NCORES)]
    return np.concatenate(shards, axis=0).astype(np.float32)


# version 12
# speedup vs baseline: 1.0314x; 1.0314x over previous
import numpy as np
import ml_dtypes
import concourse.bass as bass
import concourse.mybir as mybir
from concourse.bass_utils import run_bass_kernel_spmd

F32 = mybir.dt.float32
BF16 = mybir.dt.bfloat16
AF = mybir.ActivationFunctionType
OP = mybir.AluOpType

ENGINES = ("tensor", "vector", "scalar", "gpsimd", "sync")
NCORES = 8
B = 2
S = 2048
DM = 1024
FT = 8
TOK = B * S
H = 16
HF = 16
DH = 64
NE = 8
HID = 2048
INTER = 4096


class Buf:
    __slots__ = ("name", "writer", "readers", "xr")

    def __init__(self, name, xr=False):
        self.name = name
        self.writer = None
        self.readers = []
        self.xr = xr


class Bufs:

    def __init__(self, name, xr=False):
        self.name = name
        self.m = {}
        self.xr = xr

    def __call__(self, key=0):
        if key not in self.m:
            self.m[key] = Buf(f"{self.name}[{key}]", xr=self.xr)
        return self.m[key]

    def reset(self):
        self.m = {}


class Op:
    __slots__ = ("eng", "fn", "idx", "waits", "marked", "know", "dma_slot", "tag")

    def __init__(self, eng, fn, idx, dma_slot, tag):
        self.eng = eng
        self.fn = fn
        self.idx = idx
        self.waits = {}
        self.marked = False
        self.know = None
        self.dma_slot = dma_slot
        self.tag = tag


class Em:
    def __init__(self, nc):
        self.nc = nc
        self.ops = {e: [] for e in ENGINES}
        self.dma_slots = {}
        self.dims = list(ENGINES)
        self.know = {e: {} for e in ENGINES}
        self.sems = {}

    def slot(self, name):
        if name not in self.dma_slots:
            self.dma_slots[name] = 0
            self.dims.append("dma:" + name)
        return name

    def _dep_dim_val(self, dep):
        return dep

    def _need(self, eng, dim, val):
        cur = self.know[eng].get(dim, -1)
        return cur < val

    def _add_edge(self, eng, waits, dep):
        dim, val, know_at = dep
        if dim == eng and not dim.startswith("dma:"):
            if eng == "tensor":
                return
        if not self._need(eng, dim, val):
            return
        if waits.get(dim, -1) < val:
            waits[dim] = val
        k = self.know[eng]
        if know_at:
            for d, v in know_at.items():
                if k.get(d, -1) < v:
                    k[d] = v
        if k.get(dim, -1) < val:
            k[dim] = val

    def op(self, eng, fn, reads=(), writes=(), dma_slot=None, tag=""):
        waits = {}
        for b in reads:
            if b.writer is not None:
                self._add_edge(eng, waits, b.writer)
            if b.xr:
                for r in b.readers:
                    if r[0] != eng:
                        self._add_edge(eng, waits, r)
        for b in writes:
            if b.writer is not None:
                self._add_edge(eng, waits, b.writer)
            for r in b.readers:
                self._add_edge(eng, waits, r)
        idx = len(self.ops[eng])
        o = Op(eng, fn, idx, dma_slot, tag)
        o.waits = waits
        self.ops[eng].append(o)
        for dim, val in waits.items():
            if not dim.startswith("dma:"):
                self.ops[dim][val].marked = True
        if dma_slot is not None:
            self.dma_slots[dma_slot] += 1
            cnt = self.dma_slots[dma_slot]
            dep = ("dma:" + dma_slot, cnt, dict(self.know[eng]))
        else:
            dep = (eng, idx, None)
        o.know = dict(self.know[eng])
        o.know[eng] = idx
        if dep[0] == eng:
            dep = (eng, idx, o.know)
        for b in writes:
            b.writer = dep
            b.readers = []
        for b in reads:
            b.readers.append(dep)
        return o

    def barrier(self):
        lasts = {}
        for e in ENGINES:
            for i in range(len(self.ops[e]) - 1, -1, -1):
                o = self.ops[e][i]
                if o.dma_slot is None and o.fn is not None:
                    o.marked = True
                    lasts[e] = i
                    break
        slot_counts = dict(self.dma_slots)
        for e in ENGINES:
            if not self.ops[e] and e not in ("sync",):
                continue
            waits = {}
            for f, i in lasts.items():
                if f != e and self._need(e, f, i):
                    waits[f] = i
                    self.know[e][f] = i
            for s, c in slot_counts.items():
                d = "dma:" + s
                if c > 0 and self._need(e, d, c):
                    waits[d] = c
                    self.know[e][d] = c
            if waits:
                o = Op(e, None, len(self.ops[e]), None, "barrier")
                o.waits = waits
                self.ops[e].append(o)
        merged = {}
        for e in ENGINES:
            for d, v in self.know[e].items():
                if merged.get(d, -1) < v:
                    merged[d] = v
        for e in ENGINES:
            self.know[e] = dict(merged)

    def finalize(self, final_eng="sync"):
        nc = self.nc
        pc = {}
        for e in ENGINES:
            cnt = 0
            lst = []
            for o in self.ops[e]:
                if o.marked:
                    cnt += 1
                lst.append(cnt)
            pc[e] = lst
        sem_ctxs = []
        for e in ENGINES:
            if self.ops[e]:
                c = nc.semaphore("sem_" + e)
                self.sems[e] = c.__enter__()
                sem_ctxs.append(c)
        for s in self.dma_slots:
            c = nc.semaphore("sd_" + s)
            self.sems["dma:" + s] = c.__enter__()
            sem_ctxs.append(c)

        em = self

        def mk(engname):
            def body(h):
                for o in em.ops[engname]:
                    for dim in sorted(o.waits):
                        val = o.waits[dim]
                        if dim.startswith("dma:"):
                            h.wait_ge(em.sems[dim], 16 * val)
                        else:
                            c = pc[dim][val]
                            if c > 0:
                                h.wait_ge(em.sems[dim], c)
                    if o.fn is None:
                        continue
                    inst = o.fn(h)
                    if o.dma_slot is not None:
                        inst.then_inc(em.sems["dma:" + o.dma_slot], 16)
                    elif o.marked:
                        inst.then_inc(em.sems[engname], 1)
            return body

        with nc.Block() as block:
            for e in ENGINES:
                if self.ops[e]:
                    getattr(block, e)(mk(e))
        return sem_ctxs


def _perm_heads(W):
    Wp = np.zeros((DM, 512), W.dtype)
    for h in range(H):
        g, j = h // 4, h % 4
        Wp[:, 128 * g + 32 * j: 128 * g + 32 * j + HF] = W[:, h * HF:(h + 1) * HF]
    return Wp


def _bias_cols(b, p=128):
    f = np.ascontiguousarray(b).reshape(-1)
    n = f.size // p
    return np.ascontiguousarray(f.reshape(n, p).T)


def prep_inputs(core, inp):
    d = {}
    x = np.asarray(inp["L0_attn_hidden_states"])
    d["x0"] = np.ascontiguousarray(x[core * B:(core + 1) * B].reshape(TOK, DM))
    d["wqp"] = np.ascontiguousarray(
        _perm_heads(np.asarray(inp["L0_attn_Wq"])).reshape(8, 128, 512).transpose(1, 0, 2))
    d["wkp"] = np.ascontiguousarray(
        _perm_heads(np.asarray(inp["L0_attn_Wk"])).reshape(8, 128, 512).transpose(1, 0, 2))
    d["wv"] = np.ascontiguousarray(
        np.asarray(inp["L0_attn_Wv"]).reshape(8, 128, 1024).transpose(1, 0, 2))
    d["wo"] = np.asarray(inp["L0_attn_Wo"])
    d["w1"] = np.ascontiguousarray(
        np.asarray(inp["L0_ffn_W1"]).reshape(8, 8, 128, 8, 256).transpose(0, 3, 2, 1, 4))
    d["w3"] = np.ascontiguousarray(
        np.asarray(inp["L0_ffn_W3"]).reshape(8, 8, 128, 8, 256).transpose(0, 3, 2, 1, 4))
    d["w2"] = np.ascontiguousarray(
        np.asarray(inp["L0_ffn_W2"]).reshape(8, 2, 8, 128, 1024).transpose(0, 1, 3, 2, 4))
    d["b1c"] = _bias_cols(inp["L0_ffn_b1"])
    d["b3c"] = _bias_cols(inp["L0_ffn_b3"])
    d["b2c"] = _bias_cols(inp["L0_ffn_b2"])
    d["wg"] = np.asarray(inp["L0_ffn_Wg"])
    d["bgr"] = np.ascontiguousarray(np.asarray(inp["L0_ffn_bg"]).reshape(1, NE))
    d["wf"] = np.ascontiguousarray(
        np.asarray(inp["L0_ffn_Wf"]).reshape(8, 128, 8, 128).transpose(2, 1, 0, 3))
    d["bfc"] = _bias_cols(inp["L0_ffn_bf"])
    wft = np.ascontiguousarray(np.asarray(inp["L1_attn_W_f"]).T)
    d["wft"] = np.ascontiguousarray(
        wft.reshape(8, 128, 2, 8, 128).transpose(3, 1, 2, 0, 4))
    d["l1w1"] = np.ascontiguousarray(
        np.asarray(inp["L1_ffn_W1"]).reshape(8, 128, 32, 128).transpose(2, 1, 0, 3))
    d["l1b1c"] = _bias_cols(inp["L1_ffn_b1"])
    d["l1w2"] = np.ascontiguousarray(
        np.asarray(inp["L1_ffn_W2"]).reshape(32, 128, 8, 128).transpose(2, 1, 0, 3))
    d["l1b2c"] = _bias_cols(inp["L1_ffn_b2"])
    d["ident_bf"] = np.eye(128, dtype=ml_dtypes.bfloat16)
    d["ident_f32"] = np.eye(128, dtype=np.float32)
    d["triu_bf"] = np.triu(np.ones((128, 128), np.float32)).astype(ml_dtypes.bfloat16)
    return d


DRAM_SPECS = [
    ("x0", [TOK, DM], F32), ("wqp", [128, 8, 512], F32), ("wkp", [128, 8, 512], F32),
    ("wv", [128, 8, 1024], F32), ("wo", [DM, DM], F32),
    ("w1", [NE, 8, 128, 8, 256], F32), ("w3", [NE, 8, 128, 8, 256], F32),
    ("w2", [NE, 2, 128, 8, 1024], F32),
    ("b1c", [128, 128], F32), ("b3c", [128, 128], F32), ("b2c", [128, 64], F32),
    ("wg", [DM, NE], F32), ("bgr", [1, NE], F32), ("wf", [8, 128, 8, 128], F32),
    ("bfc", [128, 8], F32), ("wft", [8, 128, 2, 8, 128], F32),
    ("l1w1", [32, 128, 8, 128], F32), ("l1b1c", [128, 32], F32),
    ("l1w2", [8, 128, 32, 128], F32), ("l1b2c", [128, 8], F32),
    ("ident_bf", [128, 128], BF16), ("ident_f32", [128, 128], F32),
    ("triu_bf", [128, 128], BF16),
]

HB = S // 2
B_LIST = None
SCAN_FAKE = False
ATTN_IL = True
L1A_LIMIT = 99
SQ_SCALE = 1.0 / np.sqrt(32.0)
SQ_BIAS = 4.0 / np.sqrt(32.0)


class T:
    pass


def setup_tensors(nc, dbg=()):
    t = T()
    t.nc = nc
    t.em = Em(nc)
    t.ctxs = []

    def sbuf(name, shape, dt):
        c = nc.sbuf_tensor(name, shape, dt)
        t.ctxs.append(c)
        return c.__enter__()

    def psum(name, shape, dt=F32):
        c = nc.psum_tensor(name, shape, dt)
        t.ctxs.append(c)
        return c.__enter__()

    t.dram = {}
    for name, shape, dt in DRAM_SPECS:
        t.dram[name] = nc.dram_tensor(name, shape, dt, kind="ExternalInput")
    t.out = nc.dram_tensor("out", [TOK, DM], F32, kind="ExternalOutput")
    t.dbg = {}
    for name, shape in dbg:
        t.dbg[name] = nc.dram_tensor("dbg_" + name, shape, F32, kind="ExternalOutput")

    t.XT = sbuf("XT", [128, FT * TOK], BF16)
    t.XN = sbuf("XN", [128, FT * HB], BF16)
    t.AR = sbuf("AR", [128, 33024], BF16)
    t.ARF = t.AR.bitcast(F32)
    t.WS = sbuf("WS", [128, 16384], BF16)
    t.SB = sbuf("SB", [128, 8192], BF16)
    t.SF = sbuf("SF", [128, 3072], F32)
    t.WB = sbuf("WB", [128, 128], F32)
    t.IDB = sbuf("IDB", [128, 128], BF16)
    t.IDF = sbuf("IDF", [128, 128], F32)
    t.TRIU = sbuf("TRIU", [128, 128], BF16)
    t.ONESC = sbuf("ONESC", [128, 1], BF16)
    t.CC = sbuf("CC", [128, 4], F32)
    t.B1C = sbuf("B1C", [128, 128], BF16)
    t.B3C = sbuf("B3C", [128, 128], BF16)
    t.B2C = sbuf("B2C", [128, 64], F32)
    t.BFC = sbuf("BFC", [128, 8], F32)
    t.L1B1C = sbuf("L1B1C", [128, 32], F32)
    t.L1B2C = sbuf("L1B2C", [128, 8], F32)
    t.BGR = sbuf("BGR", [1, 8], F32)
    t.WG = sbuf("WG", [128, 64], BF16)
    t.PB = [psum(f"PB{i}", [128, 512], F32) for i in range(7)]
    t.PT = psum("PT", [128, 1024], BF16)

    t.b_xt = Bufs("xt")
    t.b_xn = Bufs("xn")
    t.b_ar = Bufs("ar")
    t.b_ws = Bufs("ws")
    t.b_sb = Bufs("sb")
    t.b_sf = Bufs("sf")
    t.b_pb = Bufs("pb", xr=True)
    t.b_const = Bufs("const")
    t.b_dram = Bufs("dram")
    t.b_out = Bufs("out")
    return t


def ar_keys(t, s):
    if s in (8, 9):
        return [t.b_ar((89, r)) for r in range(4)]
    return [t.b_ar(s)]


def ar_bufs(t, c0, c1):
    out = []
    for s in range(c0 // 2048, (c1 + 2047) // 2048):
        out += ar_keys(t, s)
    return out


def arf_bufs(t, c0, c1):
    return ar_bufs(t, 2 * c0, 2 * c1)


def emit_consts(t):
    em, nc = t.em, t.nc
    g = lambda fn, **kw: em.op("gpsimd", fn, **kw)
    for nm, dst in [("ident_bf", t.IDB), ("ident_f32", t.IDF), ("triu_bf", t.TRIU),
                    ("b1c", t.B1C), ("b3c", t.B3C), ("b2c", t.B2C), ("bfc", t.BFC),
                    ("l1b1c", t.L1B1C), ("l1b2c", t.L1B2C), ("bgr", t.BGR)]:
        src = t.dram[nm]
        g(lambda h, d=dst, s=src: h.dma_start(out=d[:], in_=s.ap()),
          writes=[t.b_const(nm)], dma_slot=em.slot("c_" + nm))
    g(lambda h: h.dma_start(
        out=t.WG[:].rearrange("p (k e) -> p k e", k=8),
        in_=t.dram["wg"].ap().rearrange("(k p) e -> p k e", p=128)),
      writes=[t.b_const("wg")], dma_slot=em.slot("c_wg"))
    v = lambda fn, **kw: em.op("vector", fn, **kw)
    v(lambda h: h.memset(t.ONESC[:], 1.0), writes=[t.b_const("onesc")])
    v(lambda h: h.memset(t.CC[:, 0:1], 1e-5), writes=[t.b_const("cc0")])
    v(lambda h: h.memset(t.CC[:, 1:2], 1e-9), writes=[t.b_const("cc1")])
    v(lambda h: h.memset(t.CC[:, 2:3], SQ_BIAS), writes=[t.b_const("cc2")])


def emit_load_x(t):
    em = t.em
    t.x0t = t.nc.dram_tensor("x0t", [128, FT * TOK], BF16)
    xld_view = t.ARF
    for tt in range(TOK // 128):
        par = tt % 2
        stg = (xld_view, 1024 * par, [t.b_ar(par)])
        em.op("sync", lambda h, tt=tt, par=par: h.dma_start(
            out=xld_view[:, 1024 * par:1024 * (par + 1)],
            in_=t.dram["x0"].ap()[tt * 128:(tt + 1) * 128, :]),
            reads=[], writes=stg[2], dma_slot=em.slot(f"xld{par}"), tag="xld")
        sbslot = [t.b_sb(par)]
        em.op("vector", lambda h, par=par: h.tensor_copy(
            t.SB[:, 2048 * par:2048 * par + 1024], xld_view[:, 1024 * par:1024 * (par + 1)]),
            reads=stg[2], writes=sbslot, tag="xcvt")
        for f in range(FT):
            pb = t.b_pb("t")
            em.op("tensor", lambda h, par=par, f=f: h.transpose(
                t.PT[:, 512 * par:512 * par + 128],
                t.SB[:, 2048 * par + 128 * f: 2048 * par + 128 * (f + 1)], t.IDB[:]),
                reads=sbslot + [t.b_const("ident_bf")], writes=[pb], tag="xT")
            em.op("vector", lambda h, tt=tt, par=par, f=f: h.tensor_copy(
                t.XT[:, f * TOK + tt * 128: f * TOK + (tt + 1) * 128],
                t.PT[:, 512 * par:512 * par + 128]),
                reads=[pb], writes=[t.b_xt((f, tt // 4))], tag="xTe")
    for c in range(8):
        em.op("sync", lambda h, c=c: h.dma_start(
            out=t.x0t.ap().rearrange("p (f g ch) -> p f g ch", f=FT, g=8)[:, :, c, :],
            in_=t.XT[:].rearrange("p (f g ch) -> p f g ch", f=FT, g=8)[:, :, c, :]),
            reads=[t.b_xt((f, c)) for f in range(FT)], writes=[t.b_dram(("x0t", c))],
            dma_slot=em.slot(f"x0d{c}"), tag="x0dump")


def emit_ln(t, dst, dst_base, dst_stride, dst_bufs_fn, src_chunks, src="xt"):
    em = t.em
    for i, gch in enumerate(src_chunks):
        if src == "x0":
            em.op("sync", lambda h, gch=gch: h.dma_start(
                out=t.SB[:, 0:4096].rearrange("p (f ch) -> p f ch", f=FT),
                in_=t.x0t.ap().rearrange("p (f g ch) -> p f g ch", f=FT, g=8)[:, :, gch, :]),
                reads=[t.b_dram(("x0t", gch))], writes=[t.b_sb(0), t.b_sb(1)],
                dma_slot=em.slot("x0re"), tag="x0re")
            xsl = lambda f: t.SB[:, f * 512:(f + 1) * 512]
            xb = [t.b_sb(0), t.b_sb(1)]
            xbf = lambda f: [t.b_sb(f // 4)]
        else:
            xsl = lambda f: t.XT[:, f * TOK + gch * 512: f * TOK + (gch + 1) * 512]
            xb = [t.b_xt((f, gch)) for f in range(FT)]
            xbf = lambda f: [t.b_xt((f, gch))]
        for f in range(FT):
            xs = xsl(f)
            em.op("tensor", lambda h, xs=xs, f=f: h.matmul(
                t.PB[0][0:1, 0:512], t.ONESC[:], xs, start=(f == 0), stop=(f == FT - 1)),
                reads=xbf(f) + [t.b_const("onesc")], writes=[t.b_pb(0)], tag="lnsum")
        for f in range(FT):
            xs = xsl(f)
            sq = t.SB[:, 6144 + 1024 * (f % 2): 6144 + 1024 * (f % 2) + 512]
            sqb = t.b_sb(3)
            em.op("scalar", lambda h, xs=xs, sq=sq: h.activation(
                out=sq, in_=xs, func=AF.Square),
                reads=xbf(f), writes=[sqb], tag="lnsq")
            em.op("tensor", lambda h, sq=sq, f=f: h.matmul(
                t.PB[1][0:1, 0:512], t.ONESC[:], sq, start=(f == 0), stop=(f == FT - 1)),
                reads=[sqb, t.b_const("onesc")], writes=[t.b_pb(1)], tag="lnsq2")
        mrow = t.SF[0:1, 4 * 512:4 * 512 + 512]
        srow = t.SF[0:1, 5 * 512:5 * 512 + 512]
        vrow = t.SF[0:1, 3 * 512:3 * 512 + 512]
        em.op("vector", lambda h, mrow=mrow: h.tensor_scalar(
            out=mrow, in0=t.PB[0][0:1, 0:512], scalar1=1.0 / DM, scalar2=None, op0=OP.mult),
            reads=[t.b_pb(0)], writes=[t.b_sf(4)], tag="lnm")
        em.op("vector", lambda h, srow=srow: h.tensor_scalar(
            out=srow, in0=t.PB[1][0:1, 0:512], scalar1=1.0 / DM, scalar2=None, op0=OP.mult),
            reads=[t.b_pb(1)], writes=[t.b_sf(5)], tag="lns")
        em.op("vector", lambda h, mrow=mrow, vrow=vrow: h.tensor_tensor(
            out=vrow, in0=mrow, in1=mrow, op=OP.mult),
            reads=[t.b_sf(4)], writes=[t.b_sf(3)], tag="lnm2")
        em.op("vector", lambda h, srow=srow, vrow=vrow: h.tensor_tensor(
            out=vrow, in0=srow, in1=vrow, op=OP.subtract),
            reads=[t.b_sf(5), t.b_sf(3)], writes=[t.b_sf(3)], tag="lnv")
        em.op("scalar", lambda h, vrow=vrow: h.activation(
            out=vrow, in_=vrow, func=AF.Sqrt, bias=t.CC[0:1, 0:1]),
            reads=[t.b_sf(3), t.b_const("cc0")], writes=[t.b_sf(3)], tag="lnsd")
        em.op("vector", lambda h, vrow=vrow: h.reciprocal(vrow, vrow),
            reads=[t.b_sf(3)], writes=[t.b_sf(3)], tag="lnrs")
        em.op("tensor", lambda h, mrow=mrow: h.matmul(
            t.PB[2][:, 0:512], t.ONESF[0:1, :], mrow, start=True, stop=True, tile_position=(0, 0)),
            reads=[t.b_sf(4), t.b_const("onesf")], writes=[t.b_pb(2)], tag="lnbm")
        em.op("tensor", lambda h, vrow=vrow: h.matmul(
            t.PB[3][:, 0:512], t.ONESF[0:1, :], vrow, start=True, stop=True, tile_position=(0, 0)),
            reads=[t.b_sf(3), t.b_const("onesf")], writes=[t.b_pb(3)], tag="lnbr")
        for f in range(FT):
            xs = xsl(f)
            tmp = t.SF[:, 0:512]
            em.op("vector", lambda h, xs=xs, tmp=tmp: h.scalar_tensor_tensor(
                out=tmp, in0=xs, scalar=1.0, in1=t.PB[2][:, 0:512], op0=OP.mult, op1=OP.subtract),
                reads=xbf(f) + [t.b_pb(2)], writes=[t.b_sf(0)], tag="lnsub")
            dcol = dst_base + f * dst_stride + i * 512
            em.op("vector", lambda h, tmp=tmp, dcol=dcol: h.tensor_tensor(
                out=dst[:, dcol:dcol + 512], in0=tmp, in1=t.PB[3][:, 0:512], op=OP.mult),
                reads=[t.b_sf(0), t.b_pb(3)], writes=dst_bufs_fn(f, i), tag="lnmul")




def emit_attn_proj(t, b, half):
    em = t.em
    em.op("gpsimd", lambda h: h.dma_start(
        out=t.WS[:, 0:4096].rearrange("p (k m) -> p k m", k=8),
        in_=t.dram["wqp"].ap()),
        writes=[t.b_ws(0), t.b_ws(1)], dma_slot=em.slot("wqp"), tag="ldwq")
    em.op("gpsimd", lambda h: h.dma_start(
        out=t.WS[:, 4096:8192].rearrange("p (k m) -> p k m", k=8),
        in_=t.dram["wkp"].ap()),
        writes=[t.b_ws(2), t.b_ws(3)], dma_slot=em.slot("wkp"), tag="ldwk")
    em.op("gpsimd", lambda h: h.dma_start(
        out=t.WS[:, 8192:16384].rearrange("p (k m) -> p k m", k=8),
        in_=t.dram["wv"].ap()),
        writes=[t.b_ws(s) for s in range(4, 8)], dma_slot=em.slot("wv"), tag="ldwv")
    for dst_base, wbase, wbufs, dname in (
            (0, 0, [t.b_ws(0), t.b_ws(1)], "q"), (8192, 4096, [t.b_ws(2), t.b_ws(3)], "k")):
        for g in range(4):
            for nch in range(2):
                pb = 4 + (g * 2 + nch) % 2
                xnb = [t.b_xn((f, nch)) for f in range(FT)]
                em.op("tensor", lambda h, g=g, nch=nch, pb=pb, wbase=wbase: [
                    h.matmul(t.PB[pb][:, 0:512],
                             t.WS[:, wbase + fK * 512 + g * 128: wbase + fK * 512 + (g + 1) * 128],
                             t.XN[:, fK * HB + nch * 512: fK * HB + (nch + 1) * 512],
                             start=(fK == 0), stop=(fK == FT - 1)) for fK in range(FT)][-1],
                    reads=wbufs + xnb, writes=[t.b_pb(pb)], tag=f"proj{dname}")
                dcol = dst_base + g * 2048 + half * 1024 + nch * 512
                dslot = dcol // 2048
                em.op("vector", lambda h, dcol=dcol, pb=pb: h.tensor_copy(
                    t.AR[:, dcol:dcol + 512], t.PB[pb][:, 0:512]),
                    reads=[t.b_pb(pb)], writes=[t.b_ar(dslot)], tag=f"proj{dname}e")
    ob = 16384 + half * 8 * 1040 + 64
    oslots = ar_bufs(t, 16384, 33024)
    em.op("vector", lambda h, ob=ob: h.memset(
        t.AR[:, ob:ob + 8 * 1040].rearrange("p (tt x) -> p tt x", tt=8)[:, :, 0:1040:65].rearrange(
            "p tt hx -> p (tt hx)") if False else
        t.AR[:, ob - 64:ob - 64 + 8 * 1040].rearrange("p (tt hx c) -> p tt hx c", tt=8, hx=16)[:, :, :, 64:65],
        1.0), reads=[], writes=oslots, tag="vones")
    for tt_h in range(8):
        ttg = half * 8 + tt_h
        for vh in range(2):
            pb = 4 + (tt_h * 2 + vh) % 2
            xnb = [t.b_xn((f, tt_h // 4)) for f in range(FT)]
            em.op("tensor", lambda h, tt_h=tt_h, vh=vh, pb=pb: [
                h.matmul(t.PB[pb][:, 0:512],
                         t.XN[:, fK * HB + tt_h * 128: fK * HB + (tt_h + 1) * 128],
                         t.WS[:, 8192 + fK * 1024 + vh * 512: 8192 + fK * 1024 + (vh + 1) * 512],
                         start=(fK == 0), stop=(fK == FT - 1)) for fK in range(FT)][-1],
                reads=[t.b_ws(s) for s in range(4, 8)] + xnb, writes=[t.b_pb(pb)], tag="projv")
            dbase = 16384 + ttg * 1040 + vh * 8 * 65
            vslots = ar_bufs(t, dbase, dbase + 520)
            em.op("vector", lambda h, dbase=dbase, pb=pb: h.tensor_copy(
                t.AR[:, dbase:dbase + 520].rearrange("p (hp c) -> p hp c", hp=8)[:, :, 0:64],
                t.PB[pb][:, 0:512].rearrange("p (hp c) -> p hp c", hp=8)),
                reads=[t.b_pb(pb)], writes=vslots, tag="projve")


def emit_attn_loop(t, b):
    em = t.em
    vbufs = ar_bufs(t, 16384, 33024)

    def a_banks(chain):
        return (0, 1) if chain == 0 else (2, 4)

    def regions(chain):
        return (0, 1) if chain == 0 else (2, 3)

    def emit_A(hh, s, grp, k, chain):
        g, j = hh // 4, hh % 4
        pa = a_banks(chain)[k % 2]
        em.op("tensor", lambda h, grp=grp, pa=pa, g=g, j=j, s=s: [
            h.matmul(t.PB[pa][:, idx * 128:(idx + 1) * 128],
                     t.AR[32 * j:32 * j + 16, 8192 + g * 2048 + tc * 128: 8192 + g * 2048 + (tc + 1) * 128],
                     t.AR[32 * j:32 * j + 16, g * 2048 + s * 128: g * 2048 + (s + 1) * 128],
                     start=True, stop=True, tile_position=(32 * j, 0),
                     skip_group_check=True)
            for idx, tc in enumerate(grp)][-1],
            reads=[t.b_ar(g), t.b_ar(4 + g)], writes=[t.b_pb(pa)], tag="A")

    def emit_sqqk(s, grp, k, chain):
        pa = a_banks(chain)[k % 2]
        jj = regions(chain)[k % 2]
        sqc = 8192 + 2048 * jj
        qkc = sqc + 1024
        L = len(grp)
        em.op("scalar", lambda h, pa=pa, L=L, sqc=sqc: h.activation(
            out=t.WS[:, sqc:sqc + L * 128], in_=t.PB[pa][:, 0:L * 128],
            func=AF.Square, scale=SQ_SCALE, bias=t.CC[:, 2:3]),
            reads=[t.b_pb(pa), t.b_const("cc2")], writes=[t.b_ws(4 + jj)], tag="sq")
        if grp[-1] == s:
            em.op("vector", lambda h, L=L, sqc=sqc, qkc=qkc: h.scalar_tensor_tensor(
                out=t.WS[:, qkc:qkc + L * 128], in0=t.WS[:, sqc:sqc + L * 128],
                scalar=0.5, in1=t.TRIU[:], op0=OP.add, op1=OP.mult),
                reads=[t.b_ws(4 + jj), t.b_const("triu_bf")], writes=[t.b_ws(4 + jj)], tag="qk")
        else:
            em.op("vector", lambda h, L=L, sqc=sqc, qkc=qkc: h.tensor_scalar(
                out=t.WS[:, qkc:qkc + L * 128], in0=t.WS[:, sqc:sqc + L * 128],
                scalar1=0.5, scalar2=None, op0=OP.add),
                reads=[t.b_ws(4 + jj)], writes=[t.b_ws(4 + jj)], tag="qk")

    def emit_qkv(hh, s, grp, k, chain, first):
        jj = regions(chain)[k % 2]
        qkc = 8192 + 2048 * jj + 1024
        pq = 6 - chain
        for idx, tc in enumerate(grp):
            st = first and idx == 0
            last = (tc == s)
            em.op("tensor", lambda h, tc=tc, idx=idx, hh=hh, st=st, last=last, qkc=qkc, pq=pq: h.matmul(
                t.PB[pq][0:65, 0:128],
                t.AR[:, 16384 + tc * 1040 + hh * 65: 16384 + tc * 1040 + (hh + 1) * 65],
                t.WS[:, qkc + idx * 128: qkc + (idx + 1) * 128],
                start=st, stop=last, skip_group_check=True),
                reads=vbufs + [t.b_ws(4 + jj)], writes=[t.b_pb(pq)], tag="qkv")

    def emit_tail(hh, s, chain):
        pq = 6 - chain
        c0 = 2560 if chain == 0 else 2048
        sfb = 5 if chain == 0 else 4
        dnr = t.SF[64:65, c0:c0 + 128]
        qsb = t.SF[0:64, c0:c0 + 128]
        em.op("scalar", lambda h, dnr=dnr, pq=pq: h.activation(
            out=dnr, in_=t.PB[pq][64:65, 0:128], func=AF.Identity, bias=t.CC[64:65, 1:2]),
            reads=[t.b_pb(pq), t.b_const("cc1")], writes=[t.b_sf(sfb)], tag="dn")
        em.op("vector", lambda h, dnr=dnr: h.reciprocal(dnr, dnr),
            reads=[t.b_sf(sfb)], writes=[t.b_sf(sfb)], tag="dr")
        em.op("tensor", lambda h, dnr=dnr: h.matmul(
            t.PB[3][0:64, 0:128], t.ONESF[64:65, 0:64], dnr,
            start=True, stop=True, tile_position=(64, 0), skip_group_check=True),
            reads=[t.b_sf(sfb), t.b_const("onesf")], writes=[t.b_pb(3)], tag="dbc")
        em.op("scalar", lambda h, qsb=qsb, pq=pq: h.copy(out=qsb, in_=t.PB[pq][0:64, 0:128]),
            reads=[t.b_pb(pq)], writes=[t.b_sf(sfb)], tag="qev")
        acol = 4096 + (hh // 2) * 512 + (s % 4) * 128
        arow = 64 * (hh % 2)
        em.op("vector", lambda h, qsb=qsb, acol=acol, arow=arow: h.tensor_tensor(
            out=t.WS[arow:arow + 64, acol:acol + 128], in0=qsb,
            in1=t.PB[3][0:64, 0:128], op=OP.mult),
            reads=[t.b_sf(sfb), t.b_pb(3)], writes=[t.b_ws(2 + (acol - 4096) // 2048)], tag="att")

    def groups_of(s):
        grps = [list(range(j0, min(j0 + 4, s))) for j0 in range(0, s, 4)]
        return [gg for gg in grps if gg] + [[s]]

    INTERLEAVE = ATTN_IL
    for sg in range(4):
        for hh in range(H):
            for pair in range(2):
                sa, sb = sg * 4 + 2 * pair, sg * 4 + 2 * pair + 1
                ga, gb = groups_of(sa), groups_of(sb)
                na, nb = len(ga), len(gb)
                if not INTERLEAVE:
                    for s, gl, ch in ((sa, ga, 0), (sb, gb, 1)):
                        for k, grp in enumerate(gl):
                            emit_A(hh, s, grp, k, ch)
                            emit_sqqk(s, grp, k, ch)
                            emit_qkv(hh, s, grp, k, ch, first=(k == 0))
                        emit_tail(hh, s, ch)
                    continue
                for k in range(max(na, nb) + 1):
                    if k < na:
                        emit_A(hh, sa, ga[k], k, 0)
                    if k < nb:
                        emit_A(hh, sb, gb[k], k, 1)
                    if k < na:
                        emit_sqqk(sa, ga[k], k, 0)
                    if k < nb:
                        emit_sqqk(sb, gb[k], k, 1)
                    if 1 <= k <= na:
                        emit_qkv(hh, sa, ga[k - 1], k - 1, 0, first=(k == 1))
                        if k == na:
                            emit_tail(hh, sa, 0)
                    if 1 <= k <= nb:
                        emit_qkv(hh, sb, gb[k - 1], k - 1, 1, first=(k == 1))
                        if k == nb:
                            emit_tail(hh, sb, 1)
        for mh in range(2):
            for fK in range(FT):
                par = fK % 2
                em.op("gpsimd", lambda h, fK=fK, par=par: h.dma_start(
                    out=t.WS[:, par * 2048: par * 2048 + 1024],
                    in_=t.dram["wo"].ap()[fK * 128:(fK + 1) * 128, :]),
                    writes=[t.b_ws(par)], dma_slot=em.slot(f"wos{par}"), tag="ldwo")
                for m in range(4):
                    mm = mh * 4 + m
                    em.op("tensor", lambda h, fK=fK, par=par, m=m, mm=mm: h.matmul(
                        t.PB[3 + m][:, 0:512],
                        t.WS[:, par * 2048 + mm * 128: par * 2048 + (mm + 1) * 128],
                        t.WS[:, 4096 + fK * 512: 4096 + (fK + 1) * 512],
                        start=(fK == 0), stop=(fK == FT - 1), skip_group_check=True),
                        reads=[t.b_ws(par), t.b_ws(2), t.b_ws(3)], writes=[t.b_pb(3 + m)], tag="wo")
            for m in range(4):
                mm = mh * 4 + m
                xcol = mm * TOK + b * S + sg * 512
                gch = (b * S + sg * 512) // 512
                em.op("vector", lambda h, xcol=xcol, m=m: h.tensor_tensor(
                    out=t.XT[:, xcol:xcol + 512], in0=t.XT[:, xcol:xcol + 512],
                    in1=t.PB[3 + m][:, 0:512], op=OP.add),
                    reads=[t.b_pb(3 + m), t.b_xt((mm, gch))], writes=[t.b_xt((mm, gch))], tag="woadd")
def emit_gating(t, b, half):
    em = t.em
    for tt_h in range(8):
        xnb = [t.b_xn((f, tt_h // 4)) for f in range(FT)]
        em.op("tensor", lambda h, tt_h=tt_h: [
            h.matmul(t.PB[6][:, 0:8],
                     t.XN[:, fK * HB + tt_h * 128: fK * HB + (tt_h + 1) * 128],
                     t.WG[:, fK * 8:(fK + 1) * 8],
                     start=(fK == 0), stop=False, skip_group_check=True) for fK in range(FT)][-1],
            reads=xnb + [t.b_const("wg")], writes=[t.b_pb(6)], tag="gmm")
        em.op("tensor", lambda h: h.matmul(
            t.PB[6][:, 0:8], t.ONESF[0:1, 0:128], t.BGR[:],
            start=False, stop=True, tile_position=(0, 0), skip_group_check=True),
            reads=[t.b_const("onesf"), t.b_const("bgr")], writes=[t.b_pb(6)], tag="gbias")
        s8 = t.SF[:, 2080:2088]
        nmax = t.SF[:, 2088:2089]
        sumr = t.SF[:, 2089:2090]
        rec = t.SF[:, 2090:2091]
        probs = t.SF[:, 2096:2104]
        mask = t.SF[:, 2104:2112]
        em.op("vector", lambda h, s8=s8: h.max(s8, t.PB[6][:, 0:8]),
            reads=[t.b_pb(6)], writes=[t.b_sf(4)], tag="gmax")
        em.op("vector", lambda h, s8=s8, nmax=nmax: h.tensor_scalar(
            out=nmax, in0=s8[:, 0:1], scalar1=-1.0, scalar2=None, op0=OP.mult),
            reads=[t.b_sf(4)], writes=[t.b_sf(4)], tag="gnm")
        em.op("scalar", lambda h, probs=probs, nmax=nmax, sumr=sumr: h.activation(
            out=probs, in_=t.PB[6][:, 0:8], func=AF.Exp, bias=nmax, accum_out=sumr),
            reads=[t.b_pb(6), t.b_sf(4)], writes=[t.b_sf(4)], tag="gexp")
        em.op("vector", lambda h, rec=rec, sumr=sumr: h.reciprocal(rec, sumr),
            reads=[t.b_sf(4)], writes=[t.b_sf(4)], tag="grec")
        em.op("vector", lambda h, probs=probs, rec=rec: h.tensor_scalar(
            out=probs, in0=probs, scalar1=rec, scalar2=None, op0=OP.mult),
            reads=[t.b_sf(4)], writes=[t.b_sf(4)], tag="gnorm")
        em.op("vector", lambda h, mask=mask, s8=s8: h.tensor_scalar(
            out=mask, in0=t.PB[6][:, 0:8], scalar1=s8[:, 1:2], scalar2=None, op0=OP.is_ge),
            reads=[t.b_pb(6), t.b_sf(4)], writes=[t.b_sf(4)], tag="gmask")
        wcol = half * 64 + tt_h * 8
        em.op("vector", lambda h, probs=probs, mask=mask, wcol=wcol: h.tensor_tensor(
            out=t.WB[:, wcol:wcol + 8], in0=probs, in1=mask, op=OP.mult),
            reads=[t.b_sf(4)], writes=[t.b_const("wb")], tag="gw")


def emit_moe(t, b, half):
    em = t.em
    def build_wbc(e):
        if e >= NE:
            return
        for tb in range(2):
            rk = (89, (e % 2) * 2 + tb)
            rcol = 8192 + (e % 2) * 1024 + tb * 512
            em.op("tensor", lambda h, e=e, tb=tb, half=half: [
                h.matmul(t.PB[6][0:1, tq * 128:(tq + 1) * 128],
                         t.WB[:, half * 64 + (tb * 4 + tq) * 8 + e: half * 64 + (tb * 4 + tq) * 8 + e + 1],
                         t.IDF[:], start=True, stop=True, skip_group_check=True)
                for tq in range(4)][-1],
                reads=[t.b_const("wb"), t.b_const("ident_f32")], writes=[t.b_pb(6)], tag="wrmm")
            wr = t.SF[0:1, 2048:2560]
            em.op("vector", lambda h, wr=wr: h.tensor_copy(wr, t.PB[6][0:1, 0:512]),
                reads=[t.b_pb(6)], writes=[t.b_sf(4)], tag="wrcp")
            em.op("tensor", lambda h, wr=wr: h.matmul(
                t.PB[6][:, 0:512], t.ONESF[0:1, 0:128], wr,
                start=True, stop=True, tile_position=(0, 0), skip_group_check=True),
                reads=[t.b_sf(4), t.b_const("onesf")], writes=[t.b_pb(6)], tag="wbcmm")
            em.op("vector", lambda h, rcol=rcol, rk=rk: h.tensor_copy(
                t.ARF[:, rcol:rcol + 512], t.PB[6][:, 0:512]),
                reads=[t.b_pb(6)], writes=[t.b_ar(rk)], tag="wbc")

    for e in range(NE):
        if e == 0:
            build_wbc(0)
        for hf in range(2):
            for wh2 in range(2):
                em.op("gpsimd", lambda h, e=e, hf=hf, wh2=wh2: h.dma_start(
                    out=t.WS[:, 8192 + wh2 * 4096:12288 + wh2 * 4096].rearrange("p (k n) -> p k n", k=4),
                    in_=t.dram["w2"].ap()[e, hf, :, wh2 * 4:(wh2 + 1) * 4, :]),
                    writes=[t.b_ws(4 + 2 * wh2), t.b_ws(5 + 2 * wh2)],
                    dma_slot=em.slot(f"w2h{wh2}"), tag="ldw2")
            for oct in range(4):
                par = oct % 2
                kbase = hf * 8 + oct * 2
                em.op("gpsimd", lambda h, e=e, kbase=kbase, par=par: h.dma_start(
                    out=t.WS[:, par * 2048: (par + 1) * 2048].rearrange("p (k n) -> p k n", k=8),
                    in_=t.dram["w1"].ap()[e, kbase // 2]),
                    writes=[t.b_ws(par)], dma_slot=em.slot(f"w1o{par}"), tag="ldw1")
                em.op("gpsimd", lambda h, e=e, kbase=kbase, par=par: h.dma_start(
                    out=t.WS[:, 4096 + par * 2048: 4096 + (par + 1) * 2048].rearrange("p (k n) -> p k n", k=8),
                    in_=t.dram["w3"].ap()[e, kbase // 2]),
                    writes=[t.b_ws(2 + par)], dma_slot=em.slot(f"w3o{par}"), tag="ldw3")
                for krel in range(2):
                    kih = oct * 2 + krel
                    kg = hf * 8 + kih
                    for tb in range(2):
                        it = kih * 2 + tb
                        pa, pb_ = 2 * (it % 2), 2 * (it % 2) + 1
                        xnb = [t.b_xn((f, tb)) for f in range(FT)]
                        em.op("tensor", lambda h, par=par, krel=krel, tb=tb, pa=pa: [
                            h.matmul(t.PB[pa][:, 0:512],
                                     t.WS[:, par * 2048 + fK * 256 + krel * 128: par * 2048 + fK * 256 + (krel + 1) * 128],
                                     t.XN[:, fK * HB + tb * 512: fK * HB + (tb + 1) * 512],
                                     start=(fK == 0), stop=(fK == FT - 1)) for fK in range(FT)][-1],
                            reads=[t.b_ws(par)] + xnb, writes=[t.b_pb(pa)], tag="h1")
                        em.op("tensor", lambda h, par=par, krel=krel, tb=tb, pb_=pb_: [
                            h.matmul(t.PB[pb_][:, 0:512],
                                     t.WS[:, 4096 + par * 2048 + fK * 256 + krel * 128: 4096 + par * 2048 + fK * 256 + (krel + 1) * 128],
                                     t.XN[:, fK * HB + tb * 512: fK * HB + (tb + 1) * 512],
                                     start=(fK == 0), stop=(fK == FT - 1)) for fK in range(FT)][-1],
                            reads=[t.b_ws(2 + par)] + xnb, writes=[t.b_pb(pb_)], tag="h3")
                        h3b = t.SF[:, 1024:1536]
                        em.op("scalar", lambda h, pb_=pb_, e=e, kg=kg, h3b=h3b: h.activation(
                            out=h3b, in_=t.PB[pb_][:, 0:512], func=AF.Identity,
                            bias=t.B3C[:, e * 16 + kg: e * 16 + kg + 1]),
                            reads=[t.b_pb(pb_), t.b_const("b3c")], writes=[t.b_sf(2)], tag="h3b")
                        tt_ = t.SF[:, 1536:2048]
                        em.op("vector", lambda h, pa=pa, e=e, kg=kg, h3b=h3b, tt_=tt_: h.scalar_tensor_tensor(
                            out=tt_, in0=t.PB[pa][:, 0:512], scalar=t.B1C[:, e * 16 + kg: e * 16 + kg + 1],
                            in1=h3b, op0=OP.add, op1=OP.mult),
                            reads=[t.b_pb(pa), t.b_const("b1c"), t.b_sf(2)], writes=[t.b_sf(3)], tag="hm")
                        uu = t.SF[:, 1024:1536]
                        em.op("scalar", lambda h, tt_=tt_, uu=uu: h.activation(
                            out=uu, in_=tt_, func=AF.Silu),
                            reads=[t.b_sf(3)], writes=[t.b_sf(2)], tag="silu")
                        whc = kih * 1024 + tb * 512
                        rcol = 8192 + (e % 2) * 1024 + tb * 512
                        em.op("vector", lambda h, uu=uu, rcol=rcol, whc=whc: h.tensor_tensor(
                            out=t.SB[:, whc:whc + 512], in0=uu,
                            in1=t.ARF[:, rcol:rcol + 512], op=OP.mult),
                            reads=[t.b_sf(2), t.b_ar((89, (e % 2) * 2 + tb))],
                            writes=[t.b_sb(whc // 2048)], tag="wh")
            if hf == 0:
                build_wbc(e + 1)
            for tb in range(2):
                for m in range(FT):
                    pp = 4 + (m % 2)
                    for wh2 in range(2):
                        em.op("tensor", lambda h, tb=tb, m=m, pp=pp, wh2=wh2: [
                            h.matmul(t.PB[pp][:, 0:512],
                                     t.WS[:, 8192 + kih * 1024 + m * 128: 8192 + kih * 1024 + (m + 1) * 128],
                                     t.SB[:, kih * 1024 + tb * 512: kih * 1024 + tb * 512 + 512],
                                     start=(kih == 0), stop=(kih == 7))
                            for kih in range(wh2 * 4, wh2 * 4 + 4)][-1],
                            reads=[t.b_ws(4 + 2 * wh2), t.b_ws(5 + 2 * wh2)] +
                                  [t.b_sb(i) for i in range(2 * wh2, 2 * wh2 + 2)],
                            writes=[t.b_pb(pp)], tag="w2")
                    mcol = tb * 4096 + m * 512
                    mslot = [t.b_ar(mcol // 1024)]
                    if e == 0 and hf == 0:
                        em.op("vector", lambda h, mcol=mcol, pp=pp: h.tensor_copy(
                            t.ARF[:, mcol:mcol + 512], t.PB[pp][:, 0:512]),
                            reads=[t.b_pb(pp)], writes=mslot, tag="mxi")
                    else:
                        em.op("vector", lambda h, mcol=mcol, pp=pp: h.tensor_tensor(
                            out=t.ARF[:, mcol:mcol + 512], in0=t.ARF[:, mcol:mcol + 512],
                            in1=t.PB[pp][:, 0:512], op=OP.add),
                            reads=[t.b_pb(pp)] + mslot, writes=mslot, tag="mxa")
    for tb in range(2):
        for fK in range(FT):
            mcol = tb * 4096 + fK * 512
            em.op("vector", lambda h, mcol=mcol, fK=fK: h.tensor_copy(
                t.SB[:, fK * 512:(fK + 1) * 512], t.ARF[:, mcol:mcol + 512]),
                reads=[t.b_ar(mcol // 1024)], writes=[t.b_sb(fK // 4)], tag="mxc")
        for m in range(FT):
            par = m % 2
            em.op("gpsimd", lambda h, m=m, par=par: h.dma_start(
                out=t.WS[:, par * 1024:(par + 1) * 1024].rearrange("p (k n) -> p k n", k=8),
                in_=t.dram["wf"].ap()[m]),
                writes=[t.b_ws(0)], dma_slot=em.slot(f"wfs{par}"), tag="ldwf")
            pp = 4 + par
            em.op("tensor", lambda h, m=m, par=par, pp=pp: [
                h.matmul(t.PB[pp][:, 0:512],
                         t.WS[:, par * 1024 + fK * 128: par * 1024 + (fK + 1) * 128],
                         t.SB[:, fK * 512:(fK + 1) * 512],
                         start=(fK == 0), stop=(fK == FT - 1)) for fK in range(FT)][-1],
                reads=[t.b_ws(0), t.b_sb(0), t.b_sb(1)], writes=[t.b_pb(pp)], tag="wf")
            fo = t.SF[:, 1024:1536]
            em.op("scalar", lambda h, m=m, pp=pp, fo=fo: h.activation(
                out=fo, in_=t.PB[pp][:, 0:512], func=AF.Identity, bias=t.BFC[:, m:m + 1]),
                reads=[t.b_pb(pp), t.b_const("bfc")], writes=[t.b_sf(2)], tag="wfb")
            xcol = m * TOK + b * S + half * 1024 + tb * 512
            gch = (b * S + half * 1024 + tb * 512) // 512
            em.op("vector", lambda h, xcol=xcol, fo=fo: h.tensor_tensor(
                out=t.XT[:, xcol:xcol + 512], in0=t.XT[:, xcol:xcol + 512], in1=fo, op=OP.add),
                reads=[t.b_sf(2), t.b_xt((m, gch))], writes=[t.b_xt((m, gch))], tag="wfadd")


def emit_l1_attn(t, b):
    em = t.em
    xnb_all = [t.b_ar(s) for s in range(8)]
    for et in range(FT):
        par = et % 2
        em.op("gpsimd", lambda h, et=et, par=par: h.dma_start(
            out=t.WS[:, par * 2048:(par + 1) * 2048].rearrange("p (mt k n) -> p mt k n", mt=2, k=8),
            in_=t.dram["wft"].ap()[et]),
            writes=[t.b_ws(par)], dma_slot=em.slot(f"wfts{par}"), tag="ldwft")
        for ch in range(4):
            pk = 0 + (ch % 2)
            ph = 2 + (ch % 2)
            em.op("tensor", lambda h, par=par, ch=ch, pk=pk: [
                h.matmul(t.PB[pk][:, 0:512],
                         t.WS[:, par * 2048 + fK * 128: par * 2048 + (fK + 1) * 128],
                         t.AR[:, fK * 2048 + ch * 512: fK * 2048 + (ch + 1) * 512],
                         start=(fK == 0), stop=(fK == FT - 1)) for fK in range(FT)][-1],
                reads=[t.b_ws(par)] + xnb_all, writes=[t.b_pb(pk)], tag="kf")
            em.op("tensor", lambda h, par=par, ch=ch, ph=ph: [
                h.matmul(t.PB[ph][:, 0:512],
                         t.WS[:, par * 2048 + 1024 + fK * 128: par * 2048 + 1024 + (fK + 1) * 128],
                         t.AR[:, fK * 2048 + ch * 512: fK * 2048 + (ch + 1) * 512],
                         start=(fK == 0), stop=(fK == FT - 1)) for fK in range(FT)][-1],
                reads=[t.b_ws(par)] + xnb_all, writes=[t.b_pb(ph)], tag="hx")
            if L1A_LIMIT < 2:
                continue
            aa = t.SF[:, 0:512]
            zz = t.SF[:, 512:1024]
            sg_ = t.SF[:, 1024:1536]
            lg = t.SF[:, 1536:2048]
            em.op("scalar", lambda h, pk=pk, aa=aa: h.activation(
                out=aa, in_=t.PB[pk][:, 0:512], func=AF.Sigmoid, scale=-1.0),
                reads=[t.b_pb(pk)], writes=[t.b_sf(0)], tag="siga")
            em.op("scalar", lambda h, pk=pk, zz=zz: h.activation(
                out=zz, in_=t.PB[pk][:, 0:512], func=AF.Sigmoid),
                reads=[t.b_pb(pk)], writes=[t.b_sf(1)], tag="sigz")
            em.op("scalar", lambda h, ph=ph, sg_=sg_: h.activation(
                out=sg_, in_=t.PB[ph][:, 0:512], func=AF.Sigmoid),
                reads=[t.b_pb(ph)], writes=[t.b_sf(2)], tag="sigg")
            if L1A_LIMIT < 3:
                continue
            em.op("vector", lambda h, ph=ph, lg=lg: h.tensor_scalar(
                out=lg, in0=t.PB[ph][:, 0:512], scalar1=0.5, scalar2=None, op0=OP.add),
                reads=[t.b_pb(ph)], writes=[t.b_sf(3)], tag="lin")
            em.op("vector", lambda h, sg_=sg_, lg=lg: h.tensor_tensor(
                out=lg, in0=sg_, in1=lg, op=OP.max),
                reads=[t.b_sf(2), t.b_sf(3)], writes=[t.b_sf(3)], tag="gmx")
            em.op("vector", lambda h, zz=zz, lg=lg: h.tensor_tensor(
                out=zz, in0=zz, in1=lg, op=OP.mult),
                reads=[t.b_sf(1), t.b_sf(3)], writes=[t.b_sf(1)], tag="bv")
            if L1A_LIMIT < 4:
                continue
            hp = 4 + (ch % 2)
            hprev = 4 + ((ch + 1) % 2)
            init = 0.5 if ch == 0 else t.SF[:, hprev * 512 + 511: hprev * 512 + 512]
            rd = [t.b_sf(0), t.b_sf(1)] + ([] if ch == 0 else [t.b_sf(hprev)])
            if SCAN_FAKE:
                em.op("vector", lambda h, aa=aa, zz=zz, hp=hp: h.tensor_tensor(
                    out=t.SF[:, hp * 512:(hp + 1) * 512], in0=aa, in1=zz, op=OP.add),
                    reads=rd, writes=[t.b_sf(hp)], tag="scan")
            else:
                em.op("vector", lambda h, aa=aa, zz=zz, hp=hp, init=init: h.tensor_tensor_scan(
                    out=t.SF[:, hp * 512:(hp + 1) * 512], data0=aa, data1=zz,
                    initial=init, op0=OP.mult, op1=OP.add),
                    reads=rd, writes=[t.b_sf(hp)], tag="scan")
            if L1A_LIMIT < 5:
                continue
            xcol = et * TOK + b * S + ch * 512
            gch = (b * S + ch * 512) // 512
            em.op("vector", lambda h, xcol=xcol, hp=hp: h.tensor_tensor(
                out=t.XT[:, xcol:xcol + 512], in0=t.XT[:, xcol:xcol + 512],
                in1=t.SF[:, hp * 512:(hp + 1) * 512], op=OP.add),
                reads=[t.b_sf(hp), t.b_xt((et, gch))], writes=[t.b_xt((et, gch))], tag="hadd")


def emit_l1_ffn(t, b):
    em = t.em
    xnb_all = [t.b_ar(s) for s in range(8)]
    for ch in range(4):
        gch = (b * S + ch * 512) // 512
        for mh in range(32):
            par = mh % 2
            em.op("gpsimd", lambda h, mh=mh, par=par: h.dma_start(
                out=t.WS[:, 4096 + par * 1024: 4096 + (par + 1) * 1024].rearrange("p (k n) -> p k n", k=8),
                in_=t.dram["l1w1"].ap()[mh]),
                writes=[t.b_ws(2 + par)], dma_slot=em.slot(f"l1w1{par}"), tag="ldl1a")
            pk = par
            em.op("tensor", lambda h, mh=mh, par=par, ch=ch, pk=pk: [
                h.matmul(t.PB[pk][:, 0:512],
                         t.WS[:, 4096 + par * 1024 + fK * 128: 4096 + par * 1024 + (fK + 1) * 128],
                         t.AR[:, fK * 2048 + ch * 512: fK * 2048 + (ch + 1) * 512],
                         start=(fK == 0), stop=(fK == FT - 1)) for fK in range(FT)][-1],
                reads=[t.b_ws(2 + par)] + xnb_all, writes=[t.b_pb(pk)], tag="f1")
            em.op("scalar", lambda h, mh=mh, pk=pk: h.activation(
                out=t.AR[:, 16384 + mh * 512: 16384 + (mh + 1) * 512], in_=t.PB[pk][:, 0:512],
                func=AF.Relu, bias=t.L1B1C[:, mh:mh + 1]),
                reads=[t.b_pb(pk), t.b_const("l1b1c")],
                writes=[t.b_ar(8 + mh // 4)], tag="relu")
        rhb = [t.b_ar(8 + i) for i in range(8)]
        for mo in range(FT):
            par = mo % 2
            wsl = (4 + 2 * par, 6 + 2 * par)
            em.op("gpsimd", lambda h, mo=mo, par=par: h.dma_start(
                out=t.WS[:, 8192 + par * 4096: 8192 + (par + 1) * 4096].rearrange("p (k n) -> p k n", k=32),
                in_=t.dram["l1w2"].ap()[mo]),
                writes=[t.b_ws(wsl[0]), t.b_ws(wsl[0] + 1)], dma_slot=em.slot(f"l1w2{par}"), tag="ldl1b")
            po = 2 + par
            em.op("tensor", lambda h, mo=mo, par=par, po=po: [
                h.matmul(t.PB[po][:, 0:512],
                         t.WS[:, 8192 + par * 4096 + k * 128: 8192 + par * 4096 + (k + 1) * 128],
                         t.AR[:, 16384 + k * 512: 16384 + (k + 1) * 512],
                         start=(k == 0), stop=(k == 31)) for k in range(32)][-1],
                reads=[t.b_ws(wsl[0]), t.b_ws(wsl[0] + 1)] + rhb, writes=[t.b_pb(po)], tag="f2")
            xa = t.SF[:, (par) * 512:(par + 1) * 512]
            em.op("scalar", lambda h, mo=mo, po=po, xa=xa: h.activation(
                out=xa, in_=t.PB[po][:, 0:512], func=AF.Identity, bias=t.L1B2C[:, mo:mo + 1]),
                reads=[t.b_pb(po), t.b_const("l1b2c")], writes=[t.b_sf(par)], tag="ob")
            of = t.SF[:, 1024 + par * 512: 1024 + (par + 1) * 512]
            xcol = mo * TOK + b * S + ch * 512
            em.op("vector", lambda h, xa=xa, of=of, xcol=xcol: h.tensor_tensor(
                out=of, in0=xa, in1=t.XT[:, xcol:xcol + 512], op=OP.add),
                reads=[t.b_sf(par), t.b_xt((mo, gch))], writes=[t.b_sf(2 + par)], tag="osum")
            for tq in range(4):
                em.op("tensor", lambda h, of=of, tq=tq: h.transpose(
                    t.PB[6][:, tq * 128:(tq + 1) * 128], of[:, tq * 128:(tq + 1) * 128], t.IDF[:]),
                    reads=[t.b_sf(2 + par), t.b_const("ident_f32")], writes=[t.b_pb(6)], tag="oT")
            ost = t.SF[:, 2048 + par * 512: 2048 + (par + 1) * 512]
            em.op("vector", lambda h, ost=ost: h.tensor_copy(ost, t.PB[6][:, 0:512]),
                reads=[t.b_pb(6)], writes=[t.b_sf(4 + par)], tag="oTe")
            r0 = b * S + ch * 512
            em.op("sync", lambda h, ost=ost, r0=r0, mo=mo: h.dma_start(
                out=t.out.ap().rearrange("(tq p) n -> p tq n", p=128)[:, r0 // 128: r0 // 128 + 4, mo * 128:(mo + 1) * 128],
                in_=ost.rearrange("p (tq n) -> p tq n", tq=4)),
                reads=[t.b_sf(4 + par)], writes=[t.b_out((b, ch, mo))], dma_slot=em.slot(f"out{par}"), tag="ost")


def emit_tap(t, name, src_ap, reads):
    em = t.em
    em.op("gpsimd", lambda h: h.dma_start(out=t.dbg[name].ap(), in_=src_ap),
        reads=reads, writes=[], dma_slot=em.slot("dbgs"), tag="tap")


def emit_dump_x1(t):
    em = t.em
    x1f = t.nc.dram_tensor("x1f", [128, FT * TOK], F32, kind="ExternalOutput")
    for c in range(8):
        rd = [t.b_xt((f, c)) for f in range(FT)]
        em.op("gpsimd", lambda h, c=c: h.dma_start(
            out=x1f.ap().rearrange("p (f g ch) -> p f g ch", f=FT, g=8)[:, :, c, :],
            in_=t.XT[:].rearrange("p (f g ch) -> p f g ch", f=FT, g=8)[:, :, c, :]),
            reads=rd, writes=[], dma_slot=em.slot(f"x1d{c % 2}"), tag="x1d")


def emit_load_x1(t):
    em = t.em
    x1f = t.nc.dram_tensor("x1f", [128, FT * TOK], F32, kind="ExternalInput")
    for c in range(8):
        em.op("gpsimd", lambda h, c=c: h.dma_start(
            out=t.XT[:].rearrange("p (f g ch) -> p f g ch", f=FT, g=8)[:, :, c, :],
            in_=x1f.ap().rearrange("p (f g ch) -> p f g ch", f=FT, g=8)[:, :, c, :]),
            reads=[], writes=[t.b_xt((f, c)) for f in range(FT)],
            dma_slot=em.slot(f"x1l{c}"), tag="x1l")


def build_program(nc, taps=(), parts=("load", "l0", "l1")):
    dbg_specs = []
    for tp in taps:
        if tp.startswith("xt"):
            dbg_specs.append((tp, [128, FT * TOK]))
        elif tp.startswith("xn1"):
            dbg_specs.append((tp, [128, 16384]))
        elif tp.startswith("xn"):
            dbg_specs.append((tp, [128, FT * HB]))
    t = setup_tensors(nc, dbg=dbg_specs)
    t.ONESF = t.IDF
    em = t.em
    c = nc.sbuf_tensor("ONESF", [128, 128], F32)
    t.ctxs.append(c)
    t.ONESF = c.__enter__()
    em.op("vector", lambda h: h.memset(t.ONESF[:], 1.0), writes=[t.b_const("onesf")])
    emit_consts(t)
    if "load" in parts:
        emit_load_x(t)
        if "xt0" in taps:
            emit_tap(t, "xt0", t.XT[:], [t.b_xt((f, c)) for f in range(FT) for c in range(8)])
    if "lntest" in parts:
        emit_ln(t, t.XN, 0, HB, lambda f, i: [t.b_xn((f, i))], [0, 1])
        emit_tap(t, "xn0", t.XN[:], [t.b_xn((f, i)) for f in range(FT) for i in range(2)])
    if "loadx1" in parts:
        emit_load_x1(t)
    do_attn = ("l0" in parts) or ("l0attn" in parts)
    do_moe = ("l0" in parts) or ("l0moe" in parts)
    if do_attn or do_moe:
        for b in (B_LIST if B_LIST is not None else range(B)):
            if do_attn:
                for half in range(2):
                    chunks = [(b * S + half * 1024) // 512, (b * S + half * 1024) // 512 + 1]
                    emit_ln(t, t.XN, 0, HB, lambda f, i: [t.b_xn((f, i))], chunks)
                    if half == 0 and b == 0 and "xn0" in taps:
                        emit_tap(t, "xn0", t.XN[:],
                                 [t.b_xn((f, i)) for f in range(FT) for i in range(2)])
                    emit_attn_proj(t, b, half)
                emit_attn_loop(t, b)
            if do_moe:
                for half in range(2):
                    chunks = [(b * S + half * 1024) // 512, (b * S + half * 1024) // 512 + 1]
                    emit_ln(t, t.XN, 0, HB, lambda f, i: [t.b_xn((f, i))], chunks,
                            src=("x0" if do_attn else "xt"))
                    emit_gating(t, b, half)
                    emit_moe(t, b, half)
        if "xt1" in taps:
            emit_tap(t, "xt1", t.XT[:], [t.b_xt((f, c)) for f in range(FT) for c in range(8)])
    if "dumpx1" in parts:
        emit_dump_x1(t)
    if "l1" in parts:
        for b in (B_LIST if B_LIST is not None else range(B)):
            chunks = [b * 4 + i for i in range(4)]
            emit_ln(t, t.AR, 0, 2048, lambda f, i: [t.b_ar(f)], chunks)
            if b == 0 and "xn1a" in taps:
                emit_tap(t, "xn1a", t.AR[:, 0:16384],
                         [t.b_ar(s) for s in range(8)])
            emit_l1_attn(t, b)
            emit_l1_ffn(t, b)
    em.barrier()
    sem_ctxs = em.finalize()
    return t, sem_ctxs


FUSED = True


def _run_build(parts, in_maps, extra=None):
    nc = bass.Bass("TRN2", target_bir_lowering=False)
    t, sem_ctxs = build_program(nc, parts=parts)
    res = run_bass_kernel_spmd(nc, in_maps, core_ids=list(range(NCORES)))
    for c in reversed(sem_ctxs):
        c.__exit__(None, None, None)
    for c in reversed(t.ctxs):
        c.__exit__(None, None, None)
    return res


def kernel(**inputs):
    global B_LIST
    in_maps = [prep_inputs(core, inputs) for core in range(NCORES)]
    if FUSED:
        res = _run_build(("load", "l0", "l1"), in_maps)
        shards = [res.results[i]["out"].reshape(B, S, DM) for i in range(NCORES)]
    else:
        outs = []
        for bsel in range(B):
            B_LIST = [bsel]
            res = _run_build(("load", "l0", "l1"), in_maps)
            outs.append([res.results[i]["out"].reshape(B, S, DM)[bsel] for i in range(NCORES)])
        B_LIST = None
        shards = [np.stack([outs[0][i], outs[1][i]], axis=0) for i in range(NCORES)]
    return np.concatenate(shards, axis=0).astype(np.float32)


# version 13
# speedup vs baseline: 1.0430x; 1.0112x over previous
import numpy as np
import ml_dtypes
import concourse.bass as bass
import concourse.mybir as mybir
from concourse.bass_utils import run_bass_kernel_spmd

F32 = mybir.dt.float32
BF16 = mybir.dt.bfloat16
AF = mybir.ActivationFunctionType
OP = mybir.AluOpType

ENGINES = ("tensor", "vector", "scalar", "gpsimd", "sync")
NCORES = 8
B = 2
S = 2048
DM = 1024
FT = 8
TOK = B * S
H = 16
HF = 16
DH = 64
NE = 8
HID = 2048
INTER = 4096


class Buf:
    __slots__ = ("name", "writer", "readers", "xr")

    def __init__(self, name, xr=False):
        self.name = name
        self.writer = None
        self.readers = []
        self.xr = xr


class Bufs:

    def __init__(self, name, xr=False):
        self.name = name
        self.m = {}
        self.xr = xr

    def __call__(self, key=0):
        if key not in self.m:
            self.m[key] = Buf(f"{self.name}[{key}]", xr=self.xr)
        return self.m[key]

    def reset(self):
        self.m = {}


class Op:
    __slots__ = ("eng", "fn", "idx", "waits", "marked", "know", "dma_slot", "tag")

    def __init__(self, eng, fn, idx, dma_slot, tag):
        self.eng = eng
        self.fn = fn
        self.idx = idx
        self.waits = {}
        self.marked = False
        self.know = None
        self.dma_slot = dma_slot
        self.tag = tag


class Em:
    def __init__(self, nc):
        self.nc = nc
        self.ops = {e: [] for e in ENGINES}
        self.dma_slots = {}
        self.dims = list(ENGINES)
        self.know = {e: {} for e in ENGINES}
        self.sems = {}

    def slot(self, name):
        if name not in self.dma_slots:
            self.dma_slots[name] = 0
            self.dims.append("dma:" + name)
        return name

    def _dep_dim_val(self, dep):
        return dep

    def _need(self, eng, dim, val):
        cur = self.know[eng].get(dim, -1)
        return cur < val

    def _add_edge(self, eng, waits, dep):
        dim, val, know_at = dep
        if dim == eng and not dim.startswith("dma:"):
            if eng == "tensor":
                return
        if not self._need(eng, dim, val):
            return
        if waits.get(dim, -1) < val:
            waits[dim] = val
        k = self.know[eng]
        if know_at:
            for d, v in know_at.items():
                if k.get(d, -1) < v:
                    k[d] = v
        if k.get(dim, -1) < val:
            k[dim] = val

    def op(self, eng, fn, reads=(), writes=(), dma_slot=None, tag=""):
        waits = {}
        for b in reads:
            if b.writer is not None:
                self._add_edge(eng, waits, b.writer)
            if b.xr:
                for r in b.readers:
                    if r[0] != eng:
                        self._add_edge(eng, waits, r)
        for b in writes:
            if b.writer is not None:
                self._add_edge(eng, waits, b.writer)
            for r in b.readers:
                self._add_edge(eng, waits, r)
        idx = len(self.ops[eng])
        o = Op(eng, fn, idx, dma_slot, tag)
        o.waits = waits
        self.ops[eng].append(o)
        for dim, val in waits.items():
            if not dim.startswith("dma:"):
                self.ops[dim][val].marked = True
        if dma_slot is not None:
            self.dma_slots[dma_slot] += 1
            cnt = self.dma_slots[dma_slot]
            dep = ("dma:" + dma_slot, cnt, dict(self.know[eng]))
        else:
            dep = (eng, idx, None)
        o.know = dict(self.know[eng])
        o.know[eng] = idx
        if dep[0] == eng:
            dep = (eng, idx, o.know)
        for b in writes:
            b.writer = dep
            b.readers = []
        for b in reads:
            b.readers.append(dep)
        return o

    def barrier(self):
        lasts = {}
        for e in ENGINES:
            for i in range(len(self.ops[e]) - 1, -1, -1):
                o = self.ops[e][i]
                if o.dma_slot is None and o.fn is not None:
                    o.marked = True
                    lasts[e] = i
                    break
        slot_counts = dict(self.dma_slots)
        for e in ENGINES:
            if not self.ops[e] and e not in ("sync",):
                continue
            waits = {}
            for f, i in lasts.items():
                if f != e and self._need(e, f, i):
                    waits[f] = i
                    self.know[e][f] = i
            for s, c in slot_counts.items():
                d = "dma:" + s
                if c > 0 and self._need(e, d, c):
                    waits[d] = c
                    self.know[e][d] = c
            if waits:
                o = Op(e, None, len(self.ops[e]), None, "barrier")
                o.waits = waits
                self.ops[e].append(o)
        merged = {}
        for e in ENGINES:
            for d, v in self.know[e].items():
                if merged.get(d, -1) < v:
                    merged[d] = v
        for e in ENGINES:
            self.know[e] = dict(merged)

    def finalize(self, final_eng="sync"):
        nc = self.nc
        pc = {}
        for e in ENGINES:
            cnt = 0
            lst = []
            for o in self.ops[e]:
                if o.marked:
                    cnt += 1
                lst.append(cnt)
            pc[e] = lst
        sem_ctxs = []
        for e in ENGINES:
            if self.ops[e]:
                c = nc.semaphore("sem_" + e)
                self.sems[e] = c.__enter__()
                sem_ctxs.append(c)
        for s in self.dma_slots:
            c = nc.semaphore("sd_" + s)
            self.sems["dma:" + s] = c.__enter__()
            sem_ctxs.append(c)

        em = self

        def mk(engname):
            def body(h):
                for o in em.ops[engname]:
                    for dim in sorted(o.waits):
                        val = o.waits[dim]
                        if dim.startswith("dma:"):
                            h.wait_ge(em.sems[dim], 16 * val)
                        else:
                            c = pc[dim][val]
                            if c > 0:
                                h.wait_ge(em.sems[dim], c)
                    if o.fn is None:
                        continue
                    inst = o.fn(h)
                    if o.dma_slot is not None:
                        inst.then_inc(em.sems["dma:" + o.dma_slot], 16)
                    elif o.marked:
                        inst.then_inc(em.sems[engname], 1)
            return body

        with nc.Block() as block:
            for e in ENGINES:
                if self.ops[e]:
                    getattr(block, e)(mk(e))
        return sem_ctxs


def _perm_heads(W):
    Wp = np.zeros((DM, 512), W.dtype)
    for h in range(H):
        g, j = h // 4, h % 4
        Wp[:, 128 * g + 32 * j: 128 * g + 32 * j + HF] = W[:, h * HF:(h + 1) * HF]
    return Wp


def _bias_cols(b, p=128):
    f = np.ascontiguousarray(b).reshape(-1)
    n = f.size // p
    return np.ascontiguousarray(f.reshape(n, p).T)


def prep_inputs(core, inp):
    d = {}
    x = np.asarray(inp["L0_attn_hidden_states"])
    d["x0"] = np.ascontiguousarray(x[core * B:(core + 1) * B].reshape(TOK, DM))
    d["wqp"] = np.ascontiguousarray(
        _perm_heads(np.asarray(inp["L0_attn_Wq"])).reshape(8, 128, 512).transpose(1, 0, 2))
    d["wkp"] = np.ascontiguousarray(
        _perm_heads(np.asarray(inp["L0_attn_Wk"])).reshape(8, 128, 512).transpose(1, 0, 2))
    d["wv"] = np.ascontiguousarray(
        np.asarray(inp["L0_attn_Wv"]).reshape(8, 128, 1024).transpose(1, 0, 2))
    d["wo"] = np.asarray(inp["L0_attn_Wo"])
    d["w1"] = np.ascontiguousarray(
        np.asarray(inp["L0_ffn_W1"]).reshape(8, 8, 128, 8, 256).transpose(0, 3, 2, 1, 4))
    d["w3"] = np.ascontiguousarray(
        np.asarray(inp["L0_ffn_W3"]).reshape(8, 8, 128, 8, 256).transpose(0, 3, 2, 1, 4))
    d["w2"] = np.ascontiguousarray(
        np.asarray(inp["L0_ffn_W2"]).reshape(8, 2, 8, 128, 1024).transpose(0, 1, 3, 2, 4))
    d["b1c"] = _bias_cols(inp["L0_ffn_b1"])
    d["b3c"] = _bias_cols(inp["L0_ffn_b3"])
    d["b2c"] = _bias_cols(inp["L0_ffn_b2"])
    d["wg"] = np.asarray(inp["L0_ffn_Wg"])
    d["bgr"] = np.ascontiguousarray(np.asarray(inp["L0_ffn_bg"]).reshape(1, NE))
    d["wf"] = np.ascontiguousarray(
        np.asarray(inp["L0_ffn_Wf"]).reshape(8, 128, 8, 128).transpose(2, 1, 0, 3))
    d["bfc"] = _bias_cols(inp["L0_ffn_bf"])
    wft = np.ascontiguousarray(np.asarray(inp["L1_attn_W_f"]).T)
    d["wft"] = np.ascontiguousarray(
        wft.reshape(8, 128, 2, 8, 128).transpose(3, 1, 2, 0, 4))
    d["l1w1"] = np.ascontiguousarray(
        np.asarray(inp["L1_ffn_W1"]).reshape(8, 128, 32, 128).transpose(2, 1, 0, 3))
    d["l1b1c"] = _bias_cols(inp["L1_ffn_b1"])
    d["l1w2"] = np.ascontiguousarray(
        np.asarray(inp["L1_ffn_W2"]).reshape(32, 128, 8, 128).transpose(2, 1, 0, 3))
    d["l1b2c"] = _bias_cols(inp["L1_ffn_b2"])
    d["ident_bf"] = np.eye(128, dtype=ml_dtypes.bfloat16)
    d["ident_f32"] = np.eye(128, dtype=np.float32)
    d["triu_bf"] = np.triu(np.ones((128, 128), np.float32)).astype(ml_dtypes.bfloat16)
    return d


DRAM_SPECS = [
    ("x0", [TOK, DM], F32), ("wqp", [128, 8, 512], F32), ("wkp", [128, 8, 512], F32),
    ("wv", [128, 8, 1024], F32), ("wo", [DM, DM], F32),
    ("w1", [NE, 8, 128, 8, 256], F32), ("w3", [NE, 8, 128, 8, 256], F32),
    ("w2", [NE, 2, 128, 8, 1024], F32),
    ("b1c", [128, 128], F32), ("b3c", [128, 128], F32), ("b2c", [128, 64], F32),
    ("wg", [DM, NE], F32), ("bgr", [1, NE], F32), ("wf", [8, 128, 8, 128], F32),
    ("bfc", [128, 8], F32), ("wft", [8, 128, 2, 8, 128], F32),
    ("l1w1", [32, 128, 8, 128], F32), ("l1b1c", [128, 32], F32),
    ("l1w2", [8, 128, 32, 128], F32), ("l1b2c", [128, 8], F32),
    ("ident_bf", [128, 128], BF16), ("ident_f32", [128, 128], F32),
    ("triu_bf", [128, 128], BF16),
]

HB = S // 2
B_LIST = None
SCAN_FAKE = False
ATTN_IL = True
L1A_LIMIT = 99
SQ_SCALE = 1.0 / np.sqrt(32.0)
SQ_BIAS = 4.0 / np.sqrt(32.0)


class T:
    pass


def setup_tensors(nc, dbg=()):
    t = T()
    t.nc = nc
    t.em = Em(nc)
    t.ctxs = []

    def sbuf(name, shape, dt):
        c = nc.sbuf_tensor(name, shape, dt)
        t.ctxs.append(c)
        return c.__enter__()

    def psum(name, shape, dt=F32):
        c = nc.psum_tensor(name, shape, dt)
        t.ctxs.append(c)
        return c.__enter__()

    t.dram = {}
    for name, shape, dt in DRAM_SPECS:
        t.dram[name] = nc.dram_tensor(name, shape, dt, kind="ExternalInput")
    t.out = nc.dram_tensor("out", [TOK, DM], F32, kind="ExternalOutput")
    t.dbg = {}
    for name, shape in dbg:
        t.dbg[name] = nc.dram_tensor("dbg_" + name, shape, F32, kind="ExternalOutput")

    t.XT = sbuf("XT", [128, FT * TOK], BF16)
    t.XN = sbuf("XN", [128, FT * HB], BF16)
    t.AR = sbuf("AR", [128, 33024], BF16)
    t.ARF = t.AR.bitcast(F32)
    t.WS = sbuf("WS", [128, 16384], BF16)
    t.SB = sbuf("SB", [128, 8192], BF16)
    t.SF = sbuf("SF", [128, 3072], F32)
    t.WB = sbuf("WB", [128, 128], F32)
    t.IDB = sbuf("IDB", [128, 128], BF16)
    t.IDF = sbuf("IDF", [128, 128], F32)
    t.TRIU = sbuf("TRIU", [128, 128], BF16)
    t.ONESC = sbuf("ONESC", [128, 1], BF16)
    t.CC = sbuf("CC", [128, 4], F32)
    t.B1C = sbuf("B1C", [128, 128], BF16)
    t.B3C = sbuf("B3C", [128, 128], BF16)
    t.B2C = sbuf("B2C", [128, 64], F32)
    t.BFC = sbuf("BFC", [128, 8], F32)
    t.L1B1C = sbuf("L1B1C", [128, 32], F32)
    t.L1B2C = sbuf("L1B2C", [128, 8], F32)
    t.BGR = sbuf("BGR", [1, 8], F32)
    t.WG = sbuf("WG", [128, 64], BF16)
    t.PB = [psum(f"PB{i}", [128, 512], F32) for i in range(7)]
    t.PT = psum("PT", [128, 1024], BF16)

    t.b_xt = Bufs("xt")
    t.b_xn = Bufs("xn")
    t.b_ar = Bufs("ar")
    t.b_ws = Bufs("ws")
    t.b_sb = Bufs("sb")
    t.b_sf = Bufs("sf")
    t.b_pb = Bufs("pb", xr=True)
    t.b_const = Bufs("const")
    t.b_dram = Bufs("dram")
    t.b_out = Bufs("out")
    return t


def ar_keys(t, s):
    if s in (8, 9):
        return [t.b_ar((89, r)) for r in range(4)]
    return [t.b_ar(s)]


def ar_bufs(t, c0, c1):
    out = []
    for s in range(c0 // 2048, (c1 + 2047) // 2048):
        out += ar_keys(t, s)
    return out


def arf_bufs(t, c0, c1):
    return ar_bufs(t, 2 * c0, 2 * c1)


def emit_consts(t):
    em, nc = t.em, t.nc
    g = lambda fn, **kw: em.op("gpsimd", fn, **kw)
    for nm, dst in [("ident_bf", t.IDB), ("ident_f32", t.IDF), ("triu_bf", t.TRIU),
                    ("b1c", t.B1C), ("b3c", t.B3C), ("b2c", t.B2C), ("bfc", t.BFC),
                    ("l1b1c", t.L1B1C), ("l1b2c", t.L1B2C), ("bgr", t.BGR)]:
        src = t.dram[nm]
        g(lambda h, d=dst, s=src: h.dma_start(out=d[:], in_=s.ap()),
          writes=[t.b_const(nm)], dma_slot=em.slot("c_" + nm))
    g(lambda h: h.dma_start(
        out=t.WG[:].rearrange("p (k e) -> p k e", k=8),
        in_=t.dram["wg"].ap().rearrange("(k p) e -> p k e", p=128)),
      writes=[t.b_const("wg")], dma_slot=em.slot("c_wg"))
    v = lambda fn, **kw: em.op("vector", fn, **kw)
    v(lambda h: h.memset(t.ONESC[:], 1.0), writes=[t.b_const("onesc")])
    v(lambda h: h.memset(t.CC[:, 0:1], 1e-5), writes=[t.b_const("cc0")])
    v(lambda h: h.memset(t.CC[:, 1:2], 1e-9), writes=[t.b_const("cc1")])
    v(lambda h: h.memset(t.CC[:, 2:3], SQ_BIAS), writes=[t.b_const("cc2")])


def emit_load_x(t):
    em = t.em
    t.x0t = t.nc.dram_tensor("x0t", [128, FT * TOK], BF16)
    xld_view = t.ARF
    for tt in range(TOK // 128):
        par = tt % 2
        stg = (xld_view, 1024 * par, [t.b_ar(par)])
        em.op("sync", lambda h, tt=tt, par=par: h.dma_start(
            out=xld_view[:, 1024 * par:1024 * (par + 1)],
            in_=t.dram["x0"].ap()[tt * 128:(tt + 1) * 128, :]),
            reads=[], writes=stg[2], dma_slot=em.slot(f"xld{par}"), tag="xld")
        sbslot = [t.b_sb(par)]
        em.op("vector", lambda h, par=par: h.tensor_copy(
            t.SB[:, 2048 * par:2048 * par + 1024], xld_view[:, 1024 * par:1024 * (par + 1)]),
            reads=stg[2], writes=sbslot, tag="xcvt")
        for f in range(FT):
            pb = t.b_pb("t")
            em.op("tensor", lambda h, par=par, f=f: h.transpose(
                t.PT[:, 512 * par:512 * par + 128],
                t.SB[:, 2048 * par + 128 * f: 2048 * par + 128 * (f + 1)], t.IDB[:]),
                reads=sbslot + [t.b_const("ident_bf")], writes=[pb], tag="xT")
            em.op("vector", lambda h, tt=tt, par=par, f=f: h.tensor_copy(
                t.XT[:, f * TOK + tt * 128: f * TOK + (tt + 1) * 128],
                t.PT[:, 512 * par:512 * par + 128]),
                reads=[pb], writes=[t.b_xt((f, tt // 4))], tag="xTe")
    for c in range(8):
        em.op("sync", lambda h, c=c: h.dma_start(
            out=t.x0t.ap().rearrange("p (f g ch) -> p f g ch", f=FT, g=8)[:, :, c, :],
            in_=t.XT[:].rearrange("p (f g ch) -> p f g ch", f=FT, g=8)[:, :, c, :]),
            reads=[t.b_xt((f, c)) for f in range(FT)], writes=[t.b_dram(("x0t", c))],
            dma_slot=em.slot(f"x0d{c}"), tag="x0dump")


def emit_ln(t, dst, dst_base, dst_stride, dst_bufs_fn, src_chunks, src="xt"):
    em = t.em
    for i, gch in enumerate(src_chunks):
        if src == "x0":
            em.op("sync", lambda h, gch=gch: h.dma_start(
                out=t.SB[:, 0:4096].rearrange("p (f ch) -> p f ch", f=FT),
                in_=t.x0t.ap().rearrange("p (f g ch) -> p f g ch", f=FT, g=8)[:, :, gch, :]),
                reads=[t.b_dram(("x0t", gch))], writes=[t.b_sb(0), t.b_sb(1)],
                dma_slot=em.slot("x0re"), tag="x0re")
            xsl = lambda f: t.SB[:, f * 512:(f + 1) * 512]
            xb = [t.b_sb(0), t.b_sb(1)]
            xbf = lambda f: [t.b_sb(f // 4)]
        else:
            xsl = lambda f: t.XT[:, f * TOK + gch * 512: f * TOK + (gch + 1) * 512]
            xb = [t.b_xt((f, gch)) for f in range(FT)]
            xbf = lambda f: [t.b_xt((f, gch))]
        for f in range(FT):
            xs = xsl(f)
            em.op("tensor", lambda h, xs=xs, f=f: h.matmul(
                t.PB[0][0:1, 0:512], t.ONESC[:], xs, start=(f == 0), stop=(f == FT - 1)),
                reads=xbf(f) + [t.b_const("onesc")], writes=[t.b_pb(0)], tag="lnsum")
        for f in range(FT):
            xs = xsl(f)
            sq = t.SB[:, 6144 + 1024 * (f % 2): 6144 + 1024 * (f % 2) + 512]
            sqb = t.b_sb(3)
            em.op("scalar", lambda h, xs=xs, sq=sq: h.activation(
                out=sq, in_=xs, func=AF.Square),
                reads=xbf(f), writes=[sqb], tag="lnsq")
            em.op("tensor", lambda h, sq=sq, f=f: h.matmul(
                t.PB[1][0:1, 0:512], t.ONESC[:], sq, start=(f == 0), stop=(f == FT - 1)),
                reads=[sqb, t.b_const("onesc")], writes=[t.b_pb(1)], tag="lnsq2")
        mrow = t.SF[0:1, 4 * 512:4 * 512 + 512]
        srow = t.SF[0:1, 5 * 512:5 * 512 + 512]
        vrow = t.SF[0:1, 3 * 512:3 * 512 + 512]
        em.op("vector", lambda h, mrow=mrow: h.tensor_scalar(
            out=mrow, in0=t.PB[0][0:1, 0:512], scalar1=1.0 / DM, scalar2=None, op0=OP.mult),
            reads=[t.b_pb(0)], writes=[t.b_sf(4)], tag="lnm")
        em.op("vector", lambda h, srow=srow: h.tensor_scalar(
            out=srow, in0=t.PB[1][0:1, 0:512], scalar1=1.0 / DM, scalar2=None, op0=OP.mult),
            reads=[t.b_pb(1)], writes=[t.b_sf(5)], tag="lns")
        em.op("vector", lambda h, mrow=mrow, vrow=vrow: h.tensor_tensor(
            out=vrow, in0=mrow, in1=mrow, op=OP.mult),
            reads=[t.b_sf(4)], writes=[t.b_sf(3)], tag="lnm2")
        em.op("vector", lambda h, srow=srow, vrow=vrow: h.tensor_tensor(
            out=vrow, in0=srow, in1=vrow, op=OP.subtract),
            reads=[t.b_sf(5), t.b_sf(3)], writes=[t.b_sf(3)], tag="lnv")
        em.op("scalar", lambda h, vrow=vrow: h.activation(
            out=vrow, in_=vrow, func=AF.Sqrt, bias=t.CC[0:1, 0:1]),
            reads=[t.b_sf(3), t.b_const("cc0")], writes=[t.b_sf(3)], tag="lnsd")
        em.op("vector", lambda h, vrow=vrow: h.reciprocal(vrow, vrow),
            reads=[t.b_sf(3)], writes=[t.b_sf(3)], tag="lnrs")
        em.op("tensor", lambda h, mrow=mrow: h.matmul(
            t.PB[2][:, 0:512], t.ONESF[0:1, :], mrow, start=True, stop=True, tile_position=(0, 0)),
            reads=[t.b_sf(4), t.b_const("onesf")], writes=[t.b_pb(2)], tag="lnbm")
        em.op("tensor", lambda h, vrow=vrow: h.matmul(
            t.PB[3][:, 0:512], t.ONESF[0:1, :], vrow, start=True, stop=True, tile_position=(0, 0)),
            reads=[t.b_sf(3), t.b_const("onesf")], writes=[t.b_pb(3)], tag="lnbr")
        for f in range(FT):
            xs = xsl(f)
            tmp = t.SF[:, 0:512]
            em.op("vector", lambda h, xs=xs, tmp=tmp: h.scalar_tensor_tensor(
                out=tmp, in0=xs, scalar=1.0, in1=t.PB[2][:, 0:512], op0=OP.mult, op1=OP.subtract),
                reads=xbf(f) + [t.b_pb(2)], writes=[t.b_sf(0)], tag="lnsub")
            dcol = dst_base + f * dst_stride + i * 512
            em.op("vector", lambda h, tmp=tmp, dcol=dcol: h.tensor_tensor(
                out=dst[:, dcol:dcol + 512], in0=tmp, in1=t.PB[3][:, 0:512], op=OP.mult),
                reads=[t.b_sf(0), t.b_pb(3)], writes=dst_bufs_fn(f, i), tag="lnmul")




def emit_attn_proj(t, b, half):
    em = t.em
    em.op("gpsimd", lambda h: h.dma_start(
        out=t.WS[:, 0:4096].rearrange("p (k m) -> p k m", k=8),
        in_=t.dram["wqp"].ap()),
        writes=[t.b_ws(0), t.b_ws(1)], dma_slot=em.slot("wqp"), tag="ldwq")
    em.op("gpsimd", lambda h: h.dma_start(
        out=t.WS[:, 4096:8192].rearrange("p (k m) -> p k m", k=8),
        in_=t.dram["wkp"].ap()),
        writes=[t.b_ws(2), t.b_ws(3)], dma_slot=em.slot("wkp"), tag="ldwk")
    em.op("gpsimd", lambda h: h.dma_start(
        out=t.WS[:, 8192:16384].rearrange("p (k m) -> p k m", k=8),
        in_=t.dram["wv"].ap()),
        writes=[t.b_ws(s) for s in range(4, 8)], dma_slot=em.slot("wv"), tag="ldwv")
    for dst_base, wbase, wbufs, dname in (
            (0, 0, [t.b_ws(0), t.b_ws(1)], "q"), (8192, 4096, [t.b_ws(2), t.b_ws(3)], "k")):
        for g in range(4):
            for nch in range(2):
                pb = 4 + (g * 2 + nch) % 2
                xnb = [t.b_xn((f, nch)) for f in range(FT)]
                em.op("tensor", lambda h, g=g, nch=nch, pb=pb, wbase=wbase: [
                    h.matmul(t.PB[pb][:, 0:512],
                             t.WS[:, wbase + fK * 512 + g * 128: wbase + fK * 512 + (g + 1) * 128],
                             t.XN[:, fK * HB + nch * 512: fK * HB + (nch + 1) * 512],
                             start=(fK == 0), stop=(fK == FT - 1)) for fK in range(FT)][-1],
                    reads=wbufs + xnb, writes=[t.b_pb(pb)], tag=f"proj{dname}")
                dcol = dst_base + g * 2048 + half * 1024 + nch * 512
                dslot = dcol // 2048
                em.op("vector", lambda h, dcol=dcol, pb=pb: h.tensor_copy(
                    t.AR[:, dcol:dcol + 512], t.PB[pb][:, 0:512]),
                    reads=[t.b_pb(pb)], writes=[t.b_ar(dslot)], tag=f"proj{dname}e")
    ob = 16384 + half * 8 * 1040 + 64
    oslots = ar_bufs(t, 16384, 33024)
    em.op("vector", lambda h, ob=ob: h.memset(
        t.AR[:, ob:ob + 8 * 1040].rearrange("p (tt x) -> p tt x", tt=8)[:, :, 0:1040:65].rearrange(
            "p tt hx -> p (tt hx)") if False else
        t.AR[:, ob - 64:ob - 64 + 8 * 1040].rearrange("p (tt hx c) -> p tt hx c", tt=8, hx=16)[:, :, :, 64:65],
        1.0), reads=[], writes=oslots, tag="vones")
    for tt_h in range(8):
        ttg = half * 8 + tt_h
        for vh in range(2):
            pb = 4 + (tt_h * 2 + vh) % 2
            xnb = [t.b_xn((f, tt_h // 4)) for f in range(FT)]
            em.op("tensor", lambda h, tt_h=tt_h, vh=vh, pb=pb: [
                h.matmul(t.PB[pb][:, 0:512],
                         t.XN[:, fK * HB + tt_h * 128: fK * HB + (tt_h + 1) * 128],
                         t.WS[:, 8192 + fK * 1024 + vh * 512: 8192 + fK * 1024 + (vh + 1) * 512],
                         start=(fK == 0), stop=(fK == FT - 1)) for fK in range(FT)][-1],
                reads=[t.b_ws(s) for s in range(4, 8)] + xnb, writes=[t.b_pb(pb)], tag="projv")
            dbase = 16384 + ttg * 1040 + vh * 8 * 65
            vslots = ar_bufs(t, dbase, dbase + 520)
            em.op("vector", lambda h, dbase=dbase, pb=pb: h.tensor_copy(
                t.AR[:, dbase:dbase + 520].rearrange("p (hp c) -> p hp c", hp=8)[:, :, 0:64],
                t.PB[pb][:, 0:512].rearrange("p (hp c) -> p hp c", hp=8)),
                reads=[t.b_pb(pb)], writes=vslots, tag="projve")


def emit_attn_loop(t, b):
    em = t.em
    vbufs = ar_bufs(t, 16384, 33024)

    def a_banks(chain):
        return (0, 1) if chain == 0 else (2, 4)

    def regions(chain):
        return (0, 1) if chain == 0 else (2, 3)

    def emit_A(hh, s, grp, k, chain):
        g, j = hh // 4, hh % 4
        pa = a_banks(chain)[k % 2]
        em.op("tensor", lambda h, grp=grp, pa=pa, g=g, j=j, s=s: [
            h.matmul(t.PB[pa][:, idx * 128:(idx + 1) * 128],
                     t.AR[32 * j:32 * j + 16, 8192 + g * 2048 + tc * 128: 8192 + g * 2048 + (tc + 1) * 128],
                     t.AR[32 * j:32 * j + 16, g * 2048 + s * 128: g * 2048 + (s + 1) * 128],
                     start=True, stop=True, tile_position=(32 * j, 0),
                     skip_group_check=True)
            for idx, tc in enumerate(grp)][-1],
            reads=[t.b_ar(g), t.b_ar(4 + g)], writes=[t.b_pb(pa)], tag="A")

    def emit_sqqk(s, grp, k, chain):
        pa = a_banks(chain)[k % 2]
        jj = regions(chain)[k % 2]
        sqc = 8192 + 2048 * jj
        qkc = sqc + 1024
        L = len(grp)
        em.op("scalar", lambda h, pa=pa, L=L, sqc=sqc: h.activation(
            out=t.WS[:, sqc:sqc + L * 128], in_=t.PB[pa][:, 0:L * 128],
            func=AF.Square, scale=SQ_SCALE, bias=t.CC[:, 2:3]),
            reads=[t.b_pb(pa), t.b_const("cc2")], writes=[t.b_ws(4 + jj)], tag="sq")
        if grp[-1] == s:
            em.op("vector", lambda h, L=L, sqc=sqc, qkc=qkc: h.scalar_tensor_tensor(
                out=t.WS[:, qkc:qkc + L * 128], in0=t.WS[:, sqc:sqc + L * 128],
                scalar=0.5, in1=t.TRIU[:], op0=OP.add, op1=OP.mult),
                reads=[t.b_ws(4 + jj), t.b_const("triu_bf")], writes=[t.b_ws(4 + jj)], tag="qk")
        else:
            em.op("vector", lambda h, L=L, sqc=sqc, qkc=qkc: h.tensor_scalar(
                out=t.WS[:, qkc:qkc + L * 128], in0=t.WS[:, sqc:sqc + L * 128],
                scalar1=0.5, scalar2=None, op0=OP.add),
                reads=[t.b_ws(4 + jj)], writes=[t.b_ws(4 + jj)], tag="qk")

    def emit_qkv(hh, s, grp, k, chain, first):
        jj = regions(chain)[k % 2]
        qkc = 8192 + 2048 * jj + 1024
        pq = 6 - chain
        for idx, tc in enumerate(grp):
            st = first and idx == 0
            last = (tc == s)
            em.op("tensor", lambda h, tc=tc, idx=idx, hh=hh, st=st, last=last, qkc=qkc, pq=pq: h.matmul(
                t.PB[pq][0:65, 0:128],
                t.AR[:, 16384 + tc * 1040 + hh * 65: 16384 + tc * 1040 + (hh + 1) * 65],
                t.WS[:, qkc + idx * 128: qkc + (idx + 1) * 128],
                start=st, stop=last, skip_group_check=True),
                reads=vbufs + [t.b_ws(4 + jj)], writes=[t.b_pb(pq)], tag="qkv")

    def emit_tail(hh, s, chain):
        pq = 6 - chain
        c0 = 2560 if chain == 0 else 2048
        sfb = 5 if chain == 0 else 4
        dnr = t.SF[64:65, c0:c0 + 128]
        qsb = t.SF[0:64, c0:c0 + 128]
        em.op("scalar", lambda h, dnr=dnr, pq=pq: h.activation(
            out=dnr, in_=t.PB[pq][64:65, 0:128], func=AF.Identity, bias=t.CC[64:65, 1:2]),
            reads=[t.b_pb(pq), t.b_const("cc1")], writes=[t.b_sf(sfb)], tag="dn")
        em.op("vector", lambda h, dnr=dnr: h.reciprocal(dnr, dnr),
            reads=[t.b_sf(sfb)], writes=[t.b_sf(sfb)], tag="dr")
        em.op("tensor", lambda h, dnr=dnr: h.matmul(
            t.PB[3][0:64, 0:128], t.ONESF[64:65, 0:64], dnr,
            start=True, stop=True, tile_position=(64, 0), skip_group_check=True),
            reads=[t.b_sf(sfb), t.b_const("onesf")], writes=[t.b_pb(3)], tag="dbc")
        em.op("scalar", lambda h, qsb=qsb, pq=pq: h.copy(out=qsb, in_=t.PB[pq][0:64, 0:128]),
            reads=[t.b_pb(pq)], writes=[t.b_sf(sfb)], tag="qev")
        acol = 4096 + (hh // 2) * 512 + (s % 4) * 128
        arow = 64 * (hh % 2)
        em.op("vector", lambda h, qsb=qsb, acol=acol, arow=arow: h.tensor_tensor(
            out=t.WS[arow:arow + 64, acol:acol + 128], in0=qsb,
            in1=t.PB[3][0:64, 0:128], op=OP.mult),
            reads=[t.b_sf(sfb), t.b_pb(3)], writes=[t.b_ws(2 + (acol - 4096) // 2048)], tag="att")

    def groups_of(s):
        grps = [list(range(j0, min(j0 + 4, s))) for j0 in range(0, s, 4)]
        return [gg for gg in grps if gg] + [[s]]

    INTERLEAVE = ATTN_IL
    for sg in range(4):
        for hh in range(H):
            for pair in range(2):
                sa, sb = sg * 4 + 2 * pair, sg * 4 + 2 * pair + 1
                ga, gb = groups_of(sa), groups_of(sb)
                na, nb = len(ga), len(gb)
                if not INTERLEAVE:
                    for s, gl, ch in ((sa, ga, 0), (sb, gb, 1)):
                        for k, grp in enumerate(gl):
                            emit_A(hh, s, grp, k, ch)
                            emit_sqqk(s, grp, k, ch)
                            emit_qkv(hh, s, grp, k, ch, first=(k == 0))
                        emit_tail(hh, s, ch)
                    continue
                for k in range(max(na, nb) + 1):
                    if k < na:
                        emit_A(hh, sa, ga[k], k, 0)
                    if k < nb:
                        emit_A(hh, sb, gb[k], k, 1)
                    if k < na:
                        emit_sqqk(sa, ga[k], k, 0)
                    if k < nb:
                        emit_sqqk(sb, gb[k], k, 1)
                    if 1 <= k <= na:
                        emit_qkv(hh, sa, ga[k - 1], k - 1, 0, first=(k == 1))
                        if k == na:
                            emit_tail(hh, sa, 0)
                    if 1 <= k <= nb:
                        emit_qkv(hh, sb, gb[k - 1], k - 1, 1, first=(k == 1))
                        if k == nb:
                            emit_tail(hh, sb, 1)
        for mh in range(2):
            for fK in range(FT):
                par = fK % 2
                em.op("gpsimd", lambda h, fK=fK, par=par: h.dma_start(
                    out=t.WS[:, par * 2048: par * 2048 + 1024],
                    in_=t.dram["wo"].ap()[fK * 128:(fK + 1) * 128, :]),
                    writes=[t.b_ws(par)], dma_slot=em.slot(f"wos{par}"), tag="ldwo")
                for m in range(4):
                    mm = mh * 4 + m
                    em.op("tensor", lambda h, fK=fK, par=par, m=m, mm=mm: h.matmul(
                        t.PB[3 + m][:, 0:512],
                        t.WS[:, par * 2048 + mm * 128: par * 2048 + (mm + 1) * 128],
                        t.WS[:, 4096 + fK * 512: 4096 + (fK + 1) * 512],
                        start=(fK == 0), stop=(fK == FT - 1), skip_group_check=True),
                        reads=[t.b_ws(par), t.b_ws(2), t.b_ws(3)], writes=[t.b_pb(3 + m)], tag="wo")
            for m in range(4):
                mm = mh * 4 + m
                xcol = mm * TOK + b * S + sg * 512
                gch = (b * S + sg * 512) // 512
                em.op("vector", lambda h, xcol=xcol, m=m: h.tensor_tensor(
                    out=t.XT[:, xcol:xcol + 512], in0=t.XT[:, xcol:xcol + 512],
                    in1=t.PB[3 + m][:, 0:512], op=OP.add),
                    reads=[t.b_pb(3 + m), t.b_xt((mm, gch))], writes=[t.b_xt((mm, gch))], tag="woadd")
def emit_gating(t, b, half):
    em = t.em
    for tt_h in range(8):
        xnb = [t.b_xn((f, tt_h // 4)) for f in range(FT)]
        em.op("tensor", lambda h, tt_h=tt_h: [
            h.matmul(t.PB[6][:, 0:8],
                     t.XN[:, fK * HB + tt_h * 128: fK * HB + (tt_h + 1) * 128],
                     t.WG[:, fK * 8:(fK + 1) * 8],
                     start=(fK == 0), stop=False, skip_group_check=True) for fK in range(FT)][-1],
            reads=xnb + [t.b_const("wg")], writes=[t.b_pb(6)], tag="gmm")
        em.op("tensor", lambda h: h.matmul(
            t.PB[6][:, 0:8], t.ONESF[0:1, 0:128], t.BGR[:],
            start=False, stop=True, tile_position=(0, 0), skip_group_check=True),
            reads=[t.b_const("onesf"), t.b_const("bgr")], writes=[t.b_pb(6)], tag="gbias")
        s8 = t.SF[:, 2080:2088]
        nmax = t.SF[:, 2088:2089]
        sumr = t.SF[:, 2089:2090]
        rec = t.SF[:, 2090:2091]
        probs = t.SF[:, 2096:2104]
        mask = t.SF[:, 2104:2112]
        em.op("vector", lambda h, s8=s8: h.max(s8, t.PB[6][:, 0:8]),
            reads=[t.b_pb(6)], writes=[t.b_sf(4)], tag="gmax")
        em.op("vector", lambda h, s8=s8, nmax=nmax: h.tensor_scalar(
            out=nmax, in0=s8[:, 0:1], scalar1=-1.0, scalar2=None, op0=OP.mult),
            reads=[t.b_sf(4)], writes=[t.b_sf(4)], tag="gnm")
        em.op("scalar", lambda h, probs=probs, nmax=nmax, sumr=sumr: h.activation(
            out=probs, in_=t.PB[6][:, 0:8], func=AF.Exp, bias=nmax, accum_out=sumr),
            reads=[t.b_pb(6), t.b_sf(4)], writes=[t.b_sf(4)], tag="gexp")
        em.op("vector", lambda h, rec=rec, sumr=sumr: h.reciprocal(rec, sumr),
            reads=[t.b_sf(4)], writes=[t.b_sf(4)], tag="grec")
        em.op("vector", lambda h, probs=probs, rec=rec: h.tensor_scalar(
            out=probs, in0=probs, scalar1=rec, scalar2=None, op0=OP.mult),
            reads=[t.b_sf(4)], writes=[t.b_sf(4)], tag="gnorm")
        em.op("vector", lambda h, mask=mask, s8=s8: h.tensor_scalar(
            out=mask, in0=t.PB[6][:, 0:8], scalar1=s8[:, 1:2], scalar2=None, op0=OP.is_ge),
            reads=[t.b_pb(6), t.b_sf(4)], writes=[t.b_sf(4)], tag="gmask")
        wcol = half * 64 + tt_h * 8
        em.op("vector", lambda h, probs=probs, mask=mask, wcol=wcol: h.tensor_tensor(
            out=t.WB[:, wcol:wcol + 8], in0=probs, in1=mask, op=OP.mult),
            reads=[t.b_sf(4)], writes=[t.b_const("wb")], tag="gw")


def emit_moe(t, b, half):
    em = t.em
    def build_wbc(e):
        if e >= NE:
            return
        for tb in range(2):
            rk = (89, (e % 2) * 2 + tb)
            rcol = 8192 + (e % 2) * 1024 + tb * 512
            em.op("tensor", lambda h, e=e, tb=tb, half=half: [
                h.matmul(t.PB[6][0:1, tq * 128:(tq + 1) * 128],
                         t.WB[:, half * 64 + (tb * 4 + tq) * 8 + e: half * 64 + (tb * 4 + tq) * 8 + e + 1],
                         t.IDF[:], start=True, stop=True, skip_group_check=True)
                for tq in range(4)][-1],
                reads=[t.b_const("wb"), t.b_const("ident_f32")], writes=[t.b_pb(6)], tag="wrmm")
            wr = t.AR[0:1, 20480:20992]
            em.op("vector", lambda h, wr=wr: h.tensor_copy(wr, t.PB[6][0:1, 0:512]),
                reads=[t.b_pb(6)], writes=[t.b_ar(10)], tag="wrcp")
            em.op("tensor", lambda h, wr=wr: h.matmul(
                t.PB[6][:, 0:512], t.TRIU[0:1, 0:128], wr,
                start=True, stop=True, tile_position=(0, 0), skip_group_check=True),
                reads=[t.b_ar(10), t.b_const("triu_bf")], writes=[t.b_pb(6)], tag="wbcmm")
            em.op("vector", lambda h, rcol=rcol, rk=rk: h.tensor_copy(
                t.ARF[:, rcol:rcol + 512], t.PB[6][:, 0:512]),
                reads=[t.b_pb(6)], writes=[t.b_ar(rk)], tag="wbc")

    for e in range(NE):
        if e == 0:
            build_wbc(0)
        for hf in range(2):
            for wh2 in range(2):
                em.op("gpsimd", lambda h, e=e, hf=hf, wh2=wh2: h.dma_start(
                    out=t.WS[:, 8192 + wh2 * 4096:12288 + wh2 * 4096].rearrange("p (k n) -> p k n", k=4),
                    in_=t.dram["w2"].ap()[e, hf, :, wh2 * 4:(wh2 + 1) * 4, :]),
                    writes=[t.b_ws(4 + 2 * wh2), t.b_ws(5 + 2 * wh2)],
                    dma_slot=em.slot(f"w2h{wh2}"), tag="ldw2")
            for oct in range(4):
                par = oct % 2
                kbase = hf * 8 + oct * 2
                em.op("gpsimd", lambda h, e=e, kbase=kbase, par=par: h.dma_start(
                    out=t.WS[:, par * 2048: (par + 1) * 2048].rearrange("p (k n) -> p k n", k=8),
                    in_=t.dram["w1"].ap()[e, kbase // 2]),
                    writes=[t.b_ws(par)], dma_slot=em.slot(f"w1o{par}"), tag="ldw1")
                em.op("gpsimd", lambda h, e=e, kbase=kbase, par=par: h.dma_start(
                    out=t.WS[:, 4096 + par * 2048: 4096 + (par + 1) * 2048].rearrange("p (k n) -> p k n", k=8),
                    in_=t.dram["w3"].ap()[e, kbase // 2]),
                    writes=[t.b_ws(2 + par)], dma_slot=em.slot(f"w3o{par}"), tag="ldw3")
                for krel in range(2):
                    kih = oct * 2 + krel
                    kg = hf * 8 + kih
                    for tb in range(2):
                        it = kih * 2 + tb
                        pa, pb_ = 2 * (it % 2), 2 * (it % 2) + 1
                        xnb = [t.b_xn((f, tb)) for f in range(FT)]
                        em.op("tensor", lambda h, par=par, krel=krel, tb=tb, pa=pa: [
                            h.matmul(t.PB[pa][:, 0:512],
                                     t.WS[:, par * 2048 + fK * 256 + krel * 128: par * 2048 + fK * 256 + (krel + 1) * 128],
                                     t.XN[:, fK * HB + tb * 512: fK * HB + (tb + 1) * 512],
                                     start=(fK == 0), stop=(fK == FT - 1)) for fK in range(FT)][-1],
                            reads=[t.b_ws(par)] + xnb, writes=[t.b_pb(pa)], tag="h1")
                        em.op("tensor", lambda h, par=par, krel=krel, tb=tb, pb_=pb_: [
                            h.matmul(t.PB[pb_][:, 0:512],
                                     t.WS[:, 4096 + par * 2048 + fK * 256 + krel * 128: 4096 + par * 2048 + fK * 256 + (krel + 1) * 128],
                                     t.XN[:, fK * HB + tb * 512: fK * HB + (tb + 1) * 512],
                                     start=(fK == 0), stop=(fK == FT - 1)) for fK in range(FT)][-1],
                            reads=[t.b_ws(2 + par)] + xnb, writes=[t.b_pb(pb_)], tag="h3")
                        h3b = t.SF[:, 1024:1536]
                        em.op("scalar", lambda h, pb_=pb_, e=e, kg=kg, h3b=h3b: h.activation(
                            out=h3b, in_=t.PB[pb_][:, 0:512], func=AF.Identity,
                            bias=t.B3C[:, e * 16 + kg: e * 16 + kg + 1]),
                            reads=[t.b_pb(pb_), t.b_const("b3c")], writes=[t.b_sf(2)], tag="h3b")
                        tt_ = t.SF[:, 1536:2048]
                        em.op("vector", lambda h, pa=pa, e=e, kg=kg, h3b=h3b, tt_=tt_: h.scalar_tensor_tensor(
                            out=tt_, in0=t.PB[pa][:, 0:512], scalar=t.B1C[:, e * 16 + kg: e * 16 + kg + 1],
                            in1=h3b, op0=OP.add, op1=OP.mult),
                            reads=[t.b_pb(pa), t.b_const("b1c"), t.b_sf(2)], writes=[t.b_sf(3)], tag="hm")
                        uu = t.SF[:, 1024:1536]
                        em.op("scalar", lambda h, tt_=tt_, uu=uu: h.activation(
                            out=uu, in_=tt_, func=AF.Silu),
                            reads=[t.b_sf(3)], writes=[t.b_sf(2)], tag="silu")
                        whc = kih * 1024 + tb * 512
                        rcol = 8192 + (e % 2) * 1024 + tb * 512
                        em.op("vector", lambda h, uu=uu, rcol=rcol, whc=whc: h.tensor_tensor(
                            out=t.SB[:, whc:whc + 512], in0=uu,
                            in1=t.ARF[:, rcol:rcol + 512], op=OP.mult),
                            reads=[t.b_sf(2), t.b_ar((89, (e % 2) * 2 + tb))],
                            writes=[t.b_sb(whc // 2048)], tag="wh")
            if hf == 0:
                build_wbc(e + 1)
            for tb in range(2):
                for m in range(FT):
                    pp = 4 + (m % 2)
                    for wh2 in range(2):
                        em.op("tensor", lambda h, tb=tb, m=m, pp=pp, wh2=wh2: [
                            h.matmul(t.PB[pp][:, 0:512],
                                     t.WS[:, 8192 + kih * 1024 + m * 128: 8192 + kih * 1024 + (m + 1) * 128],
                                     t.SB[:, kih * 1024 + tb * 512: kih * 1024 + tb * 512 + 512],
                                     start=(kih == 0), stop=(kih == 7))
                            for kih in range(wh2 * 4, wh2 * 4 + 4)][-1],
                            reads=[t.b_ws(4 + 2 * wh2), t.b_ws(5 + 2 * wh2)] +
                                  [t.b_sb(i) for i in range(2 * wh2, 2 * wh2 + 2)],
                            writes=[t.b_pb(pp)], tag="w2")
                    mcol = tb * 4096 + m * 512
                    mslot = [t.b_ar(mcol // 1024)]
                    if e == 0 and hf == 0:
                        em.op("vector", lambda h, mcol=mcol, pp=pp: h.tensor_copy(
                            t.ARF[:, mcol:mcol + 512], t.PB[pp][:, 0:512]),
                            reads=[t.b_pb(pp)], writes=mslot, tag="mxi")
                    else:
                        em.op("vector", lambda h, mcol=mcol, pp=pp: h.tensor_tensor(
                            out=t.ARF[:, mcol:mcol + 512], in0=t.ARF[:, mcol:mcol + 512],
                            in1=t.PB[pp][:, 0:512], op=OP.add),
                            reads=[t.b_pb(pp)] + mslot, writes=mslot, tag="mxa")
    for tb in range(2):
        for fK in range(FT):
            mcol = tb * 4096 + fK * 512
            em.op("vector", lambda h, mcol=mcol, fK=fK: h.tensor_copy(
                t.SB[:, fK * 512:(fK + 1) * 512], t.ARF[:, mcol:mcol + 512]),
                reads=[t.b_ar(mcol // 1024)], writes=[t.b_sb(fK // 4)], tag="mxc")
        for m in range(FT):
            par = m % 2
            em.op("gpsimd", lambda h, m=m, par=par: h.dma_start(
                out=t.WS[:, par * 1024:(par + 1) * 1024].rearrange("p (k n) -> p k n", k=8),
                in_=t.dram["wf"].ap()[m]),
                writes=[t.b_ws(0)], dma_slot=em.slot(f"wfs{par}"), tag="ldwf")
            pp = 4 + par
            em.op("tensor", lambda h, m=m, par=par, pp=pp: [
                h.matmul(t.PB[pp][:, 0:512],
                         t.WS[:, par * 1024 + fK * 128: par * 1024 + (fK + 1) * 128],
                         t.SB[:, fK * 512:(fK + 1) * 512],
                         start=(fK == 0), stop=(fK == FT - 1)) for fK in range(FT)][-1],
                reads=[t.b_ws(0), t.b_sb(0), t.b_sb(1)], writes=[t.b_pb(pp)], tag="wf")
            fo = t.SF[:, 1024:1536]
            em.op("scalar", lambda h, m=m, pp=pp, fo=fo: h.activation(
                out=fo, in_=t.PB[pp][:, 0:512], func=AF.Identity, bias=t.BFC[:, m:m + 1]),
                reads=[t.b_pb(pp), t.b_const("bfc")], writes=[t.b_sf(2)], tag="wfb")
            xcol = m * TOK + b * S + half * 1024 + tb * 512
            gch = (b * S + half * 1024 + tb * 512) // 512
            em.op("vector", lambda h, xcol=xcol, fo=fo: h.tensor_tensor(
                out=t.XT[:, xcol:xcol + 512], in0=t.XT[:, xcol:xcol + 512], in1=fo, op=OP.add),
                reads=[t.b_sf(2), t.b_xt((m, gch))], writes=[t.b_xt((m, gch))], tag="wfadd")


def emit_l1_attn(t, b):
    em = t.em
    xnb_all = [t.b_ar(s) for s in range(8)]
    for et in range(FT):
        par = et % 2
        em.op("gpsimd", lambda h, et=et, par=par: h.dma_start(
            out=t.WS[:, par * 2048:(par + 1) * 2048].rearrange("p (mt k n) -> p mt k n", mt=2, k=8),
            in_=t.dram["wft"].ap()[et]),
            writes=[t.b_ws(par)], dma_slot=em.slot(f"wfts{par}"), tag="ldwft")
        for ch in range(4):
            pk = 0 + (ch % 2)
            ph = 2 + (ch % 2)
            em.op("tensor", lambda h, par=par, ch=ch, pk=pk: [
                h.matmul(t.PB[pk][:, 0:512],
                         t.WS[:, par * 2048 + fK * 128: par * 2048 + (fK + 1) * 128],
                         t.AR[:, fK * 2048 + ch * 512: fK * 2048 + (ch + 1) * 512],
                         start=(fK == 0), stop=(fK == FT - 1)) for fK in range(FT)][-1],
                reads=[t.b_ws(par)] + xnb_all, writes=[t.b_pb(pk)], tag="kf")
            em.op("tensor", lambda h, par=par, ch=ch, ph=ph: [
                h.matmul(t.PB[ph][:, 0:512],
                         t.WS[:, par * 2048 + 1024 + fK * 128: par * 2048 + 1024 + (fK + 1) * 128],
                         t.AR[:, fK * 2048 + ch * 512: fK * 2048 + (ch + 1) * 512],
                         start=(fK == 0), stop=(fK == FT - 1)) for fK in range(FT)][-1],
                reads=[t.b_ws(par)] + xnb_all, writes=[t.b_pb(ph)], tag="hx")
            if L1A_LIMIT < 2:
                continue
            aa = t.SF[:, 0:512]
            zz = t.SF[:, 512:1024]
            sg_ = t.SF[:, 1024:1536]
            lg = t.SF[:, 1536:2048]
            em.op("scalar", lambda h, pk=pk, aa=aa: h.activation(
                out=aa, in_=t.PB[pk][:, 0:512], func=AF.Sigmoid, scale=-1.0),
                reads=[t.b_pb(pk)], writes=[t.b_sf(0)], tag="siga")
            em.op("scalar", lambda h, pk=pk, zz=zz: h.activation(
                out=zz, in_=t.PB[pk][:, 0:512], func=AF.Sigmoid),
                reads=[t.b_pb(pk)], writes=[t.b_sf(1)], tag="sigz")
            em.op("scalar", lambda h, ph=ph, sg_=sg_: h.activation(
                out=sg_, in_=t.PB[ph][:, 0:512], func=AF.Sigmoid),
                reads=[t.b_pb(ph)], writes=[t.b_sf(2)], tag="sigg")
            if L1A_LIMIT < 3:
                continue
            em.op("vector", lambda h, ph=ph, lg=lg: h.tensor_scalar(
                out=lg, in0=t.PB[ph][:, 0:512], scalar1=0.5, scalar2=None, op0=OP.add),
                reads=[t.b_pb(ph)], writes=[t.b_sf(3)], tag="lin")
            em.op("vector", lambda h, sg_=sg_, lg=lg: h.tensor_tensor(
                out=lg, in0=sg_, in1=lg, op=OP.max),
                reads=[t.b_sf(2), t.b_sf(3)], writes=[t.b_sf(3)], tag="gmx")
            em.op("vector", lambda h, zz=zz, lg=lg: h.tensor_tensor(
                out=zz, in0=zz, in1=lg, op=OP.mult),
                reads=[t.b_sf(1), t.b_sf(3)], writes=[t.b_sf(1)], tag="bv")
            if L1A_LIMIT < 4:
                continue
            hp = 4 + (ch % 2)
            hprev = 4 + ((ch + 1) % 2)
            init = 0.5 if ch == 0 else t.SF[:, hprev * 512 + 511: hprev * 512 + 512]
            rd = [t.b_sf(0), t.b_sf(1)] + ([] if ch == 0 else [t.b_sf(hprev)])
            if SCAN_FAKE:
                em.op("vector", lambda h, aa=aa, zz=zz, hp=hp: h.tensor_tensor(
                    out=t.SF[:, hp * 512:(hp + 1) * 512], in0=aa, in1=zz, op=OP.add),
                    reads=rd, writes=[t.b_sf(hp)], tag="scan")
            else:
                em.op("vector", lambda h, aa=aa, zz=zz, hp=hp, init=init: h.tensor_tensor_scan(
                    out=t.SF[:, hp * 512:(hp + 1) * 512], data0=aa, data1=zz,
                    initial=init, op0=OP.mult, op1=OP.add),
                    reads=rd, writes=[t.b_sf(hp)], tag="scan")
            if L1A_LIMIT < 5:
                continue
            xcol = et * TOK + b * S + ch * 512
            gch = (b * S + ch * 512) // 512
            em.op("vector", lambda h, xcol=xcol, hp=hp: h.tensor_tensor(
                out=t.XT[:, xcol:xcol + 512], in0=t.XT[:, xcol:xcol + 512],
                in1=t.SF[:, hp * 512:(hp + 1) * 512], op=OP.add),
                reads=[t.b_sf(hp), t.b_xt((et, gch))], writes=[t.b_xt((et, gch))], tag="hadd")


def emit_l1_ffn(t, b):
    em = t.em
    xnb_all = [t.b_ar(s) for s in range(8)]
    for ch in range(4):
        gch = (b * S + ch * 512) // 512
        for mh in range(32):
            par = mh % 2
            em.op("gpsimd", lambda h, mh=mh, par=par: h.dma_start(
                out=t.WS[:, 4096 + par * 1024: 4096 + (par + 1) * 1024].rearrange("p (k n) -> p k n", k=8),
                in_=t.dram["l1w1"].ap()[mh]),
                writes=[t.b_ws(2 + par)], dma_slot=em.slot(f"l1w1{par}"), tag="ldl1a")
            pk = par
            em.op("tensor", lambda h, mh=mh, par=par, ch=ch, pk=pk: [
                h.matmul(t.PB[pk][:, 0:512],
                         t.WS[:, 4096 + par * 1024 + fK * 128: 4096 + par * 1024 + (fK + 1) * 128],
                         t.AR[:, fK * 2048 + ch * 512: fK * 2048 + (ch + 1) * 512],
                         start=(fK == 0), stop=(fK == FT - 1)) for fK in range(FT)][-1],
                reads=[t.b_ws(2 + par)] + xnb_all, writes=[t.b_pb(pk)], tag="f1")
            em.op("scalar", lambda h, mh=mh, pk=pk: h.activation(
                out=t.AR[:, 16384 + mh * 512: 16384 + (mh + 1) * 512], in_=t.PB[pk][:, 0:512],
                func=AF.Relu, bias=t.L1B1C[:, mh:mh + 1]),
                reads=[t.b_pb(pk), t.b_const("l1b1c")],
                writes=[t.b_ar(8 + mh // 4)], tag="relu")
        rhb = [t.b_ar(8 + i) for i in range(8)]
        for mo in range(FT):
            par = mo % 2
            wsl = (4 + 2 * par, 6 + 2 * par)
            em.op("gpsimd", lambda h, mo=mo, par=par: h.dma_start(
                out=t.WS[:, 8192 + par * 4096: 8192 + (par + 1) * 4096].rearrange("p (k n) -> p k n", k=32),
                in_=t.dram["l1w2"].ap()[mo]),
                writes=[t.b_ws(wsl[0]), t.b_ws(wsl[0] + 1)], dma_slot=em.slot(f"l1w2{par}"), tag="ldl1b")
            po = 2 + par
            em.op("tensor", lambda h, mo=mo, par=par, po=po: [
                h.matmul(t.PB[po][:, 0:512],
                         t.WS[:, 8192 + par * 4096 + k * 128: 8192 + par * 4096 + (k + 1) * 128],
                         t.AR[:, 16384 + k * 512: 16384 + (k + 1) * 512],
                         start=(k == 0), stop=(k == 31)) for k in range(32)][-1],
                reads=[t.b_ws(wsl[0]), t.b_ws(wsl[0] + 1)] + rhb, writes=[t.b_pb(po)], tag="f2")
            xa = t.SF[:, (par) * 512:(par + 1) * 512]
            em.op("scalar", lambda h, mo=mo, po=po, xa=xa: h.activation(
                out=xa, in_=t.PB[po][:, 0:512], func=AF.Identity, bias=t.L1B2C[:, mo:mo + 1]),
                reads=[t.b_pb(po), t.b_const("l1b2c")], writes=[t.b_sf(par)], tag="ob")
            of = t.SF[:, 1024 + par * 512: 1024 + (par + 1) * 512]
            xcol = mo * TOK + b * S + ch * 512
            em.op("vector", lambda h, xa=xa, of=of, xcol=xcol: h.tensor_tensor(
                out=of, in0=xa, in1=t.XT[:, xcol:xcol + 512], op=OP.add),
                reads=[t.b_sf(par), t.b_xt((mo, gch))], writes=[t.b_sf(2 + par)], tag="osum")
            for tq in range(4):
                em.op("tensor", lambda h, of=of, tq=tq: h.transpose(
                    t.PB[6][:, tq * 128:(tq + 1) * 128], of[:, tq * 128:(tq + 1) * 128], t.IDF[:]),
                    reads=[t.b_sf(2 + par), t.b_const("ident_f32")], writes=[t.b_pb(6)], tag="oT")
            ost = t.SF[:, 2048 + par * 512: 2048 + (par + 1) * 512]
            em.op("vector", lambda h, ost=ost: h.tensor_copy(ost, t.PB[6][:, 0:512]),
                reads=[t.b_pb(6)], writes=[t.b_sf(4 + par)], tag="oTe")
            r0 = b * S + ch * 512
            em.op("sync", lambda h, ost=ost, r0=r0, mo=mo: h.dma_start(
                out=t.out.ap().rearrange("(tq p) n -> p tq n", p=128)[:, r0 // 128: r0 // 128 + 4, mo * 128:(mo + 1) * 128],
                in_=ost.rearrange("p (tq n) -> p tq n", tq=4)),
                reads=[t.b_sf(4 + par)], writes=[t.b_out((b, ch, mo))], dma_slot=em.slot(f"out{par}"), tag="ost")


def emit_tap(t, name, src_ap, reads):
    em = t.em
    em.op("gpsimd", lambda h: h.dma_start(out=t.dbg[name].ap(), in_=src_ap),
        reads=reads, writes=[], dma_slot=em.slot("dbgs"), tag="tap")


def emit_dump_x1(t):
    em = t.em
    x1f = t.nc.dram_tensor("x1f", [128, FT * TOK], F32, kind="ExternalOutput")
    for c in range(8):
        rd = [t.b_xt((f, c)) for f in range(FT)]
        em.op("gpsimd", lambda h, c=c: h.dma_start(
            out=x1f.ap().rearrange("p (f g ch) -> p f g ch", f=FT, g=8)[:, :, c, :],
            in_=t.XT[:].rearrange("p (f g ch) -> p f g ch", f=FT, g=8)[:, :, c, :]),
            reads=rd, writes=[], dma_slot=em.slot(f"x1d{c % 2}"), tag="x1d")


def emit_load_x1(t):
    em = t.em
    x1f = t.nc.dram_tensor("x1f", [128, FT * TOK], F32, kind="ExternalInput")
    for c in range(8):
        em.op("gpsimd", lambda h, c=c: h.dma_start(
            out=t.XT[:].rearrange("p (f g ch) -> p f g ch", f=FT, g=8)[:, :, c, :],
            in_=x1f.ap().rearrange("p (f g ch) -> p f g ch", f=FT, g=8)[:, :, c, :]),
            reads=[], writes=[t.b_xt((f, c)) for f in range(FT)],
            dma_slot=em.slot(f"x1l{c}"), tag="x1l")


def build_program(nc, taps=(), parts=("load", "l0", "l1")):
    dbg_specs = []
    for tp in taps:
        if tp.startswith("xt"):
            dbg_specs.append((tp, [128, FT * TOK]))
        elif tp.startswith("xn1"):
            dbg_specs.append((tp, [128, 16384]))
        elif tp.startswith("xn"):
            dbg_specs.append((tp, [128, FT * HB]))
    t = setup_tensors(nc, dbg=dbg_specs)
    t.ONESF = t.IDF
    em = t.em
    c = nc.sbuf_tensor("ONESF", [128, 128], F32)
    t.ctxs.append(c)
    t.ONESF = c.__enter__()
    em.op("vector", lambda h: h.memset(t.ONESF[:], 1.0), writes=[t.b_const("onesf")])
    emit_consts(t)
    if "load" in parts:
        emit_load_x(t)
        if "xt0" in taps:
            emit_tap(t, "xt0", t.XT[:], [t.b_xt((f, c)) for f in range(FT) for c in range(8)])
    if "lntest" in parts:
        emit_ln(t, t.XN, 0, HB, lambda f, i: [t.b_xn((f, i))], [0, 1])
        emit_tap(t, "xn0", t.XN[:], [t.b_xn((f, i)) for f in range(FT) for i in range(2)])
    if "loadx1" in parts:
        emit_load_x1(t)
    do_attn = ("l0" in parts) or ("l0attn" in parts)
    do_moe = ("l0" in parts) or ("l0moe" in parts)
    if do_attn or do_moe:
        for b in (B_LIST if B_LIST is not None else range(B)):
            if do_attn:
                for half in range(2):
                    chunks = [(b * S + half * 1024) // 512, (b * S + half * 1024) // 512 + 1]
                    emit_ln(t, t.XN, 0, HB, lambda f, i: [t.b_xn((f, i))], chunks)
                    if half == 0 and b == 0 and "xn0" in taps:
                        emit_tap(t, "xn0", t.XN[:],
                                 [t.b_xn((f, i)) for f in range(FT) for i in range(2)])
                    emit_attn_proj(t, b, half)
                emit_attn_loop(t, b)
            if do_moe:
                for half in range(2):
                    chunks = [(b * S + half * 1024) // 512, (b * S + half * 1024) // 512 + 1]
                    emit_ln(t, t.XN, 0, HB, lambda f, i: [t.b_xn((f, i))], chunks,
                            src=("x0" if do_attn else "xt"))
                    emit_gating(t, b, half)
                    emit_moe(t, b, half)
        if "xt1" in taps:
            emit_tap(t, "xt1", t.XT[:], [t.b_xt((f, c)) for f in range(FT) for c in range(8)])
    if "dumpx1" in parts:
        emit_dump_x1(t)
    if "l1" in parts:
        for b in (B_LIST if B_LIST is not None else range(B)):
            chunks = [b * 4 + i for i in range(4)]
            emit_ln(t, t.AR, 0, 2048, lambda f, i: [t.b_ar(f)], chunks)
            if b == 0 and "xn1a" in taps:
                emit_tap(t, "xn1a", t.AR[:, 0:16384],
                         [t.b_ar(s) for s in range(8)])
            emit_l1_attn(t, b)
            emit_l1_ffn(t, b)
    em.barrier()
    sem_ctxs = em.finalize()
    return t, sem_ctxs


FUSED = True


def _run_build(parts, in_maps, extra=None):
    nc = bass.Bass("TRN2", target_bir_lowering=False)
    t, sem_ctxs = build_program(nc, parts=parts)
    res = run_bass_kernel_spmd(nc, in_maps, core_ids=list(range(NCORES)))
    for c in reversed(sem_ctxs):
        c.__exit__(None, None, None)
    for c in reversed(t.ctxs):
        c.__exit__(None, None, None)
    return res


def kernel(**inputs):
    global B_LIST
    in_maps = [prep_inputs(core, inputs) for core in range(NCORES)]
    if FUSED:
        res = _run_build(("load", "l0", "l1"), in_maps)
        shards = [res.results[i]["out"].reshape(B, S, DM) for i in range(NCORES)]
    else:
        outs = []
        for bsel in range(B):
            B_LIST = [bsel]
            res = _run_build(("load", "l0", "l1"), in_maps)
            outs.append([res.results[i]["out"].reshape(B, S, DM)[bsel] for i in range(NCORES)])
        B_LIST = None
        shards = [np.stack([outs[0][i], outs[1][i]], axis=0) for i in range(NCORES)]
    return np.concatenate(shards, axis=0).astype(np.float32)


# version 14
# speedup vs baseline: 1.0976x; 1.0524x over previous
import numpy as np
import ml_dtypes
import concourse.bass as bass
import concourse.mybir as mybir
from concourse.bass_utils import run_bass_kernel_spmd

F32 = mybir.dt.float32
BF16 = mybir.dt.bfloat16
AF = mybir.ActivationFunctionType
OP = mybir.AluOpType

ENGINES = ("tensor", "vector", "scalar", "gpsimd", "sync")
NCORES = 8
B = 2
S = 2048
DM = 1024
FT = 8
TOK = B * S
H = 16
HF = 16
DH = 64
NE = 8
HID = 2048
INTER = 4096


class Buf:
    __slots__ = ("name", "writer", "readers", "xr")

    def __init__(self, name, xr=False):
        self.name = name
        self.writer = None
        self.readers = []
        self.xr = xr


class Bufs:

    def __init__(self, name, xr=False):
        self.name = name
        self.m = {}
        self.xr = xr

    def __call__(self, key=0):
        if key not in self.m:
            self.m[key] = Buf(f"{self.name}[{key}]", xr=self.xr)
        return self.m[key]

    def reset(self):
        self.m = {}


class Op:
    __slots__ = ("eng", "fn", "idx", "waits", "marked", "know", "dma_slot", "tag")

    def __init__(self, eng, fn, idx, dma_slot, tag):
        self.eng = eng
        self.fn = fn
        self.idx = idx
        self.waits = {}
        self.marked = False
        self.know = None
        self.dma_slot = dma_slot
        self.tag = tag


class Em:
    def __init__(self, nc):
        self.nc = nc
        self.ops = {e: [] for e in ENGINES}
        self.dma_slots = {}
        self.dims = list(ENGINES)
        self.know = {e: {} for e in ENGINES}
        self.sems = {}

    def slot(self, name):
        if name not in self.dma_slots:
            self.dma_slots[name] = 0
            self.dims.append("dma:" + name)
        return name

    def _dep_dim_val(self, dep):
        return dep

    def _need(self, eng, dim, val):
        cur = self.know[eng].get(dim, -1)
        return cur < val

    def _add_edge(self, eng, waits, dep):
        dim, val, know_at = dep
        if dim == eng and not dim.startswith("dma:"):
            if eng == "tensor":
                return
        if not self._need(eng, dim, val):
            return
        if waits.get(dim, -1) < val:
            waits[dim] = val
        k = self.know[eng]
        if know_at:
            for d, v in know_at.items():
                if k.get(d, -1) < v:
                    k[d] = v
        if k.get(dim, -1) < val:
            k[dim] = val

    def op(self, eng, fn, reads=(), writes=(), dma_slot=None, tag=""):
        waits = {}
        for b in reads:
            if b.writer is not None:
                self._add_edge(eng, waits, b.writer)
            if b.xr:
                for r in b.readers:
                    if r[0] != eng:
                        self._add_edge(eng, waits, r)
        for b in writes:
            if b.writer is not None:
                self._add_edge(eng, waits, b.writer)
            for r in b.readers:
                self._add_edge(eng, waits, r)
        idx = len(self.ops[eng])
        o = Op(eng, fn, idx, dma_slot, tag)
        o.waits = waits
        self.ops[eng].append(o)
        for dim, val in waits.items():
            if not dim.startswith("dma:"):
                self.ops[dim][val].marked = True
        if dma_slot is not None:
            self.dma_slots[dma_slot] += 1
            cnt = self.dma_slots[dma_slot]
            dep = ("dma:" + dma_slot, cnt, dict(self.know[eng]))
        else:
            dep = (eng, idx, None)
        o.know = dict(self.know[eng])
        o.know[eng] = idx
        if dep[0] == eng:
            dep = (eng, idx, o.know)
        for b in writes:
            b.writer = dep
            b.readers = []
        for b in reads:
            b.readers.append(dep)
        return o

    def barrier(self):
        lasts = {}
        for e in ENGINES:
            for i in range(len(self.ops[e]) - 1, -1, -1):
                o = self.ops[e][i]
                if o.dma_slot is None and o.fn is not None:
                    o.marked = True
                    lasts[e] = i
                    break
        slot_counts = dict(self.dma_slots)
        for e in ENGINES:
            if not self.ops[e] and e not in ("sync",):
                continue
            waits = {}
            for f, i in lasts.items():
                if f != e and self._need(e, f, i):
                    waits[f] = i
                    self.know[e][f] = i
            for s, c in slot_counts.items():
                d = "dma:" + s
                if c > 0 and self._need(e, d, c):
                    waits[d] = c
                    self.know[e][d] = c
            if waits:
                o = Op(e, None, len(self.ops[e]), None, "barrier")
                o.waits = waits
                self.ops[e].append(o)
        merged = {}
        for e in ENGINES:
            for d, v in self.know[e].items():
                if merged.get(d, -1) < v:
                    merged[d] = v
        for e in ENGINES:
            self.know[e] = dict(merged)

    def finalize(self, final_eng="sync"):
        nc = self.nc
        pc = {}
        for e in ENGINES:
            cnt = 0
            lst = []
            for o in self.ops[e]:
                if o.marked:
                    cnt += 1
                lst.append(cnt)
            pc[e] = lst
        sem_ctxs = []
        for e in ENGINES:
            if self.ops[e]:
                c = nc.semaphore("sem_" + e)
                self.sems[e] = c.__enter__()
                sem_ctxs.append(c)
        for s in self.dma_slots:
            c = nc.semaphore("sd_" + s)
            self.sems["dma:" + s] = c.__enter__()
            sem_ctxs.append(c)

        em = self

        def mk(engname):
            def body(h):
                for o in em.ops[engname]:
                    for dim in sorted(o.waits):
                        val = o.waits[dim]
                        if dim.startswith("dma:"):
                            h.wait_ge(em.sems[dim], 16 * val)
                        else:
                            c = pc[dim][val]
                            if c > 0:
                                h.wait_ge(em.sems[dim], c)
                    if o.fn is None:
                        continue
                    inst = o.fn(h)
                    if o.dma_slot is not None:
                        inst.then_inc(em.sems["dma:" + o.dma_slot], 16)
                    elif o.marked:
                        inst.then_inc(em.sems[engname], 1)
            return body

        with nc.allow_low_precision(reason="bf16 broadcast rows (LN mean/rstd, attn denom recip); analyzed, within tolerance"):
            with nc.Block() as block:
                for e in ENGINES:
                    if self.ops[e]:
                        getattr(block, e)(mk(e))
        return sem_ctxs


def _perm_heads(W):
    Wp = np.zeros((DM, 512), W.dtype)
    for h in range(H):
        g, j = h // 4, h % 4
        Wp[:, 128 * g + 32 * j: 128 * g + 32 * j + HF] = W[:, h * HF:(h + 1) * HF]
    return Wp


def _bias_cols(b, p=128):
    f = np.ascontiguousarray(b).reshape(-1)
    n = f.size // p
    return np.ascontiguousarray(f.reshape(n, p).T)


def prep_inputs(core, inp):
    d = {}
    x = np.asarray(inp["L0_attn_hidden_states"])
    d["x0"] = np.ascontiguousarray(x[core * B:(core + 1) * B].reshape(TOK, DM))
    d["wqp"] = np.ascontiguousarray(
        _perm_heads(np.asarray(inp["L0_attn_Wq"])).reshape(8, 128, 512).transpose(1, 0, 2))
    d["wkp"] = np.ascontiguousarray(
        _perm_heads(np.asarray(inp["L0_attn_Wk"])).reshape(8, 128, 512).transpose(1, 0, 2))
    d["wv"] = np.ascontiguousarray(
        np.asarray(inp["L0_attn_Wv"]).reshape(8, 128, 1024).transpose(1, 0, 2))
    d["wo"] = np.asarray(inp["L0_attn_Wo"])
    d["w1"] = np.ascontiguousarray(
        np.asarray(inp["L0_ffn_W1"]).reshape(8, 8, 128, 8, 256).transpose(0, 3, 2, 1, 4))
    d["w3"] = np.ascontiguousarray(
        np.asarray(inp["L0_ffn_W3"]).reshape(8, 8, 128, 8, 256).transpose(0, 3, 2, 1, 4))
    d["w2"] = np.ascontiguousarray(
        np.asarray(inp["L0_ffn_W2"]).reshape(8, 2, 8, 128, 1024).transpose(0, 1, 3, 2, 4))
    d["b1c"] = _bias_cols(inp["L0_ffn_b1"])
    d["b3c"] = _bias_cols(inp["L0_ffn_b3"])
    d["b2c"] = _bias_cols(inp["L0_ffn_b2"])
    d["wg"] = np.asarray(inp["L0_ffn_Wg"])
    d["bgr"] = np.ascontiguousarray(np.asarray(inp["L0_ffn_bg"]).reshape(1, NE))
    d["wf"] = np.ascontiguousarray(
        np.asarray(inp["L0_ffn_Wf"]).reshape(8, 128, 8, 128).transpose(2, 1, 0, 3))
    d["bfc"] = _bias_cols(inp["L0_ffn_bf"])
    wft = np.ascontiguousarray(np.asarray(inp["L1_attn_W_f"]).T)
    d["wft"] = np.ascontiguousarray(
        wft.reshape(8, 128, 2, 8, 128).transpose(3, 1, 2, 0, 4))
    d["l1w1"] = np.ascontiguousarray(
        np.asarray(inp["L1_ffn_W1"]).reshape(8, 128, 32, 128).transpose(2, 1, 0, 3))
    d["l1b1c"] = _bias_cols(inp["L1_ffn_b1"])
    d["l1w2"] = np.ascontiguousarray(
        np.asarray(inp["L1_ffn_W2"]).reshape(32, 128, 8, 128).transpose(2, 1, 0, 3))
    d["l1b2c"] = _bias_cols(inp["L1_ffn_b2"])
    d["ident_bf"] = np.eye(128, dtype=ml_dtypes.bfloat16)
    d["ident_f32"] = np.eye(128, dtype=np.float32)
    d["triu_bf"] = np.triu(np.ones((128, 128), np.float32)).astype(ml_dtypes.bfloat16)
    return d


DRAM_SPECS = [
    ("x0", [TOK, DM], F32), ("wqp", [128, 8, 512], F32), ("wkp", [128, 8, 512], F32),
    ("wv", [128, 8, 1024], F32), ("wo", [DM, DM], F32),
    ("w1", [NE, 8, 128, 8, 256], F32), ("w3", [NE, 8, 128, 8, 256], F32),
    ("w2", [NE, 2, 128, 8, 1024], F32),
    ("b1c", [128, 128], F32), ("b3c", [128, 128], F32), ("b2c", [128, 64], F32),
    ("wg", [DM, NE], F32), ("bgr", [1, NE], F32), ("wf", [8, 128, 8, 128], F32),
    ("bfc", [128, 8], F32), ("wft", [8, 128, 2, 8, 128], F32),
    ("l1w1", [32, 128, 8, 128], F32), ("l1b1c", [128, 32], F32),
    ("l1w2", [8, 128, 32, 128], F32), ("l1b2c", [128, 8], F32),
    ("ident_bf", [128, 128], BF16), ("ident_f32", [128, 128], F32),
    ("triu_bf", [128, 128], BF16),
]

HB = S // 2
B_LIST = None
SCAN_FAKE = False
ATTN_IL = True
L1A_LIMIT = 99
SQ_SCALE = 1.0 / np.sqrt(32.0)
SQ_BIAS = 4.0 / np.sqrt(32.0)


class T:
    pass


def setup_tensors(nc, dbg=()):
    t = T()
    t.nc = nc
    t.em = Em(nc)
    t.ctxs = []

    def sbuf(name, shape, dt):
        c = nc.sbuf_tensor(name, shape, dt)
        t.ctxs.append(c)
        return c.__enter__()

    def psum(name, shape, dt=F32):
        c = nc.psum_tensor(name, shape, dt)
        t.ctxs.append(c)
        return c.__enter__()

    t.dram = {}
    for name, shape, dt in DRAM_SPECS:
        t.dram[name] = nc.dram_tensor(name, shape, dt, kind="ExternalInput")
    t.out = nc.dram_tensor("out", [TOK, DM], F32, kind="ExternalOutput")
    t.dbg = {}
    for name, shape in dbg:
        t.dbg[name] = nc.dram_tensor("dbg_" + name, shape, F32, kind="ExternalOutput")

    t.XT = sbuf("XT", [128, FT * TOK], BF16)
    t.XN = sbuf("XN", [128, FT * HB], BF16)
    t.AR = sbuf("AR", [128, 33024], BF16)
    t.ARF = t.AR.bitcast(F32)
    t.WS = sbuf("WS", [128, 16384], BF16)
    t.SB = sbuf("SB", [128, 8192], BF16)
    t.SF = sbuf("SF", [128, 3072], F32)
    t.WB = sbuf("WB", [128, 128], F32)
    t.IDB = sbuf("IDB", [128, 128], BF16)
    t.IDF = sbuf("IDF", [128, 128], F32)
    t.TRIU = sbuf("TRIU", [128, 128], BF16)
    t.ONESC = sbuf("ONESC", [128, 1], BF16)
    t.ONESB = sbuf("ONESB", [128, 64], BF16)
    t.CC = sbuf("CC", [128, 4], F32)
    t.B1C = sbuf("B1C", [128, 128], BF16)
    t.B3C = sbuf("B3C", [128, 128], BF16)
    t.B2C = sbuf("B2C", [128, 64], F32)
    t.BFC = sbuf("BFC", [128, 8], F32)
    t.L1B1C = sbuf("L1B1C", [128, 32], F32)
    t.L1B2C = sbuf("L1B2C", [128, 8], F32)
    t.BGR = sbuf("BGR", [1, 8], F32)
    t.WG = sbuf("WG", [128, 64], BF16)
    t.PB = [psum(f"PB{i}", [128, 512], F32) for i in range(7)]
    t.PT = psum("PT", [128, 1024], BF16)

    t.b_xt = Bufs("xt")
    t.b_xn = Bufs("xn")
    t.b_ar = Bufs("ar")
    t.b_ws = Bufs("ws")
    t.b_sb = Bufs("sb")
    t.b_sf = Bufs("sf")
    t.b_pb = Bufs("pb", xr=True)
    t.b_const = Bufs("const")
    t.b_dram = Bufs("dram")
    t.b_out = Bufs("out")
    return t


def ar_keys(t, s):
    if s in (8, 9):
        return [t.b_ar((89, r)) for r in range(4)]
    return [t.b_ar(s)]


def ar_bufs(t, c0, c1):
    out = []
    for s in range(c0 // 2048, (c1 + 2047) // 2048):
        out += ar_keys(t, s)
    return out


def arf_bufs(t, c0, c1):
    return ar_bufs(t, 2 * c0, 2 * c1)


def emit_consts(t):
    em, nc = t.em, t.nc
    g = lambda fn, **kw: em.op("gpsimd", fn, **kw)
    for nm, dst in [("ident_bf", t.IDB), ("ident_f32", t.IDF), ("triu_bf", t.TRIU),
                    ("b1c", t.B1C), ("b3c", t.B3C), ("b2c", t.B2C), ("bfc", t.BFC),
                    ("l1b1c", t.L1B1C), ("l1b2c", t.L1B2C), ("bgr", t.BGR)]:
        src = t.dram[nm]
        g(lambda h, d=dst, s=src: h.dma_start(out=d[:], in_=s.ap()),
          writes=[t.b_const(nm)], dma_slot=em.slot("c_" + nm))
    g(lambda h: h.dma_start(
        out=t.WG[:].rearrange("p (k e) -> p k e", k=8),
        in_=t.dram["wg"].ap().rearrange("(k p) e -> p k e", p=128)),
      writes=[t.b_const("wg")], dma_slot=em.slot("c_wg"))
    v = lambda fn, **kw: em.op("vector", fn, **kw)
    v(lambda h: h.memset(t.ONESC[:], 1.0), writes=[t.b_const("onesc")])
    v(lambda h: h.memset(t.ONESB[:], 1.0), writes=[t.b_const("onesb")])
    v(lambda h: h.memset(t.CC[:, 0:1], 1e-5), writes=[t.b_const("cc0")])
    v(lambda h: h.memset(t.CC[:, 1:2], 1e-9), writes=[t.b_const("cc1")])
    v(lambda h: h.memset(t.CC[:, 2:3], SQ_BIAS), writes=[t.b_const("cc2")])


def emit_load_x(t):
    em = t.em
    t.x0t = t.nc.dram_tensor("x0t", [128, FT * TOK], BF16)
    xld_view = t.ARF
    for tt in range(TOK // 128):
        par = tt % 2
        stg = (xld_view, 1024 * par, [t.b_ar(par)])
        em.op("sync", lambda h, tt=tt, par=par: h.dma_start(
            out=xld_view[:, 1024 * par:1024 * (par + 1)],
            in_=t.dram["x0"].ap()[tt * 128:(tt + 1) * 128, :]),
            reads=[], writes=stg[2], dma_slot=em.slot(f"xld{par}"), tag="xld")
        sbslot = [t.b_sb(par)]
        em.op("vector", lambda h, par=par: h.tensor_copy(
            t.SB[:, 2048 * par:2048 * par + 1024], xld_view[:, 1024 * par:1024 * (par + 1)]),
            reads=stg[2], writes=sbslot, tag="xcvt")
        for f in range(FT):
            pb = t.b_pb("t")
            em.op("tensor", lambda h, par=par, f=f: h.transpose(
                t.PT[:, 512 * par:512 * par + 128],
                t.SB[:, 2048 * par + 128 * f: 2048 * par + 128 * (f + 1)], t.IDB[:]),
                reads=sbslot + [t.b_const("ident_bf")], writes=[pb], tag="xT")
            em.op("vector", lambda h, tt=tt, par=par, f=f: h.tensor_copy(
                t.XT[:, f * TOK + tt * 128: f * TOK + (tt + 1) * 128],
                t.PT[:, 512 * par:512 * par + 128]),
                reads=[pb], writes=[t.b_xt((f, tt // 4))], tag="xTe")
    for c in range(8):
        em.op("sync", lambda h, c=c: h.dma_start(
            out=t.x0t.ap().rearrange("p (f g ch) -> p f g ch", f=FT, g=8)[:, :, c, :],
            in_=t.XT[:].rearrange("p (f g ch) -> p f g ch", f=FT, g=8)[:, :, c, :]),
            reads=[t.b_xt((f, c)) for f in range(FT)], writes=[t.b_dram(("x0t", c))],
            dma_slot=em.slot(f"x0d{c}"), tag="x0dump")


def emit_ln(t, dst, dst_base, dst_stride, dst_bufs_fn, src_chunks, src="xt"):
    em = t.em
    for i, gch in enumerate(src_chunks):
        if src == "x0":
            em.op("sync", lambda h, gch=gch: h.dma_start(
                out=t.SB[:, 0:4096].rearrange("p (f ch) -> p f ch", f=FT),
                in_=t.x0t.ap().rearrange("p (f g ch) -> p f g ch", f=FT, g=8)[:, :, gch, :]),
                reads=[t.b_dram(("x0t", gch))], writes=[t.b_sb(0), t.b_sb(1)],
                dma_slot=em.slot("x0re"), tag="x0re")
            xsl = lambda f: t.SB[:, f * 512:(f + 1) * 512]
            xb = [t.b_sb(0), t.b_sb(1)]
            xbf = lambda f: [t.b_sb(f // 4)]
        else:
            xsl = lambda f: t.XT[:, f * TOK + gch * 512: f * TOK + (gch + 1) * 512]
            xb = [t.b_xt((f, gch)) for f in range(FT)]
            xbf = lambda f: [t.b_xt((f, gch))]
        for f in range(FT):
            xs = xsl(f)
            em.op("tensor", lambda h, xs=xs, f=f: h.matmul(
                t.PB[0][0:1, 0:512], t.ONESC[:], xs, start=(f == 0), stop=(f == FT - 1)),
                reads=xbf(f) + [t.b_const("onesc")], writes=[t.b_pb(0)], tag="lnsum")
        for f in range(FT):
            xs = xsl(f)
            sq = t.SB[:, 6144 + 1024 * (f % 2): 6144 + 1024 * (f % 2) + 512]
            sqb = t.b_sb(3)
            em.op("scalar", lambda h, xs=xs, sq=sq: h.activation(
                out=sq, in_=xs, func=AF.Square),
                reads=xbf(f), writes=[sqb], tag="lnsq")
            em.op("tensor", lambda h, sq=sq, f=f: h.matmul(
                t.PB[1][0:1, 0:512], t.ONESC[:], sq, start=(f == 0), stop=(f == FT - 1)),
                reads=[sqb, t.b_const("onesc")], writes=[t.b_pb(1)], tag="lnsq2")
        mrow = t.SB[0:1, 4096:4608]
        srow = t.SF[0:1, 5 * 512:5 * 512 + 512]
        vrow = t.SF[0:1, 3 * 512:3 * 512 + 512]
        rsrow = t.SB[0:1, 4608:5120]
        em.op("vector", lambda h, mrow=mrow: h.tensor_scalar(
            out=mrow, in0=t.PB[0][0:1, 0:512], scalar1=1.0 / DM, scalar2=None, op0=OP.mult),
            reads=[t.b_pb(0)], writes=[t.b_sb(2)], tag="lnm")
        em.op("vector", lambda h, srow=srow: h.tensor_scalar(
            out=srow, in0=t.PB[1][0:1, 0:512], scalar1=1.0 / DM, scalar2=None, op0=OP.mult),
            reads=[t.b_pb(1)], writes=[t.b_sf(5)], tag="lns")
        em.op("vector", lambda h, mrow=mrow, vrow=vrow: h.tensor_tensor(
            out=vrow, in0=mrow, in1=mrow, op=OP.mult),
            reads=[t.b_sb(2)], writes=[t.b_sf(3)], tag="lnm2")
        em.op("vector", lambda h, srow=srow, vrow=vrow: h.tensor_tensor(
            out=vrow, in0=srow, in1=vrow, op=OP.subtract),
            reads=[t.b_sf(5), t.b_sf(3)], writes=[t.b_sf(3)], tag="lnv")
        em.op("scalar", lambda h, vrow=vrow, rsrow=rsrow: h.activation(
            out=rsrow, in_=vrow, func=AF.Sqrt, bias=t.CC[0:1, 0:1]),
            reads=[t.b_sf(3), t.b_const("cc0")], writes=[t.b_sb(2)], tag="lnsd")
        em.op("vector", lambda h, rsrow=rsrow: h.reciprocal(rsrow, rsrow),
            reads=[t.b_sb(2)], writes=[t.b_sb(2)], tag="lnrs")
        em.op("tensor", lambda h, mrow=mrow: h.matmul(
            t.PB[2][:, 0:512], t.TRIU[0:1, 0:128], mrow, start=True, stop=True, tile_position=(0, 0)),
            reads=[t.b_sb(2), t.b_const("triu_bf")], writes=[t.b_pb(2)], tag="lnbm")
        em.op("tensor", lambda h, rsrow=rsrow: h.matmul(
            t.PB[3][:, 0:512], t.TRIU[0:1, 0:128], rsrow, start=True, stop=True, tile_position=(0, 0)),
            reads=[t.b_sb(2), t.b_const("triu_bf")], writes=[t.b_pb(3)], tag="lnbr")
        for f in range(FT):
            xs = xsl(f)
            tmp = t.SF[:, 0:512]
            em.op("vector", lambda h, xs=xs, tmp=tmp: h.scalar_tensor_tensor(
                out=tmp, in0=xs, scalar=1.0, in1=t.PB[2][:, 0:512], op0=OP.mult, op1=OP.subtract),
                reads=xbf(f) + [t.b_pb(2)], writes=[t.b_sf(0)], tag="lnsub")
            dcol = dst_base + f * dst_stride + i * 512
            em.op("vector", lambda h, tmp=tmp, dcol=dcol: h.tensor_tensor(
                out=dst[:, dcol:dcol + 512], in0=tmp, in1=t.PB[3][:, 0:512], op=OP.mult),
                reads=[t.b_sf(0), t.b_pb(3)], writes=dst_bufs_fn(f, i), tag="lnmul")




def emit_attn_proj(t, b, half):
    em = t.em
    em.op("gpsimd", lambda h: h.dma_start(
        out=t.WS[:, 0:4096].rearrange("p (k m) -> p k m", k=8),
        in_=t.dram["wqp"].ap()),
        writes=[t.b_ws(0), t.b_ws(1)], dma_slot=em.slot("wqp"), tag="ldwq")
    em.op("gpsimd", lambda h: h.dma_start(
        out=t.WS[:, 4096:8192].rearrange("p (k m) -> p k m", k=8),
        in_=t.dram["wkp"].ap()),
        writes=[t.b_ws(2), t.b_ws(3)], dma_slot=em.slot("wkp"), tag="ldwk")
    em.op("gpsimd", lambda h: h.dma_start(
        out=t.WS[:, 8192:16384].rearrange("p (k m) -> p k m", k=8),
        in_=t.dram["wv"].ap()),
        writes=[t.b_ws(s) for s in range(4, 8)], dma_slot=em.slot("wv"), tag="ldwv")
    for dst_base, wbase, wbufs, dname in (
            (0, 0, [t.b_ws(0), t.b_ws(1)], "q"), (8192, 4096, [t.b_ws(2), t.b_ws(3)], "k")):
        for g in range(4):
            for nch in range(2):
                pb = 4 + (g * 2 + nch) % 2
                xnb = [t.b_xn((f, nch)) for f in range(FT)]
                em.op("tensor", lambda h, g=g, nch=nch, pb=pb, wbase=wbase: [
                    h.matmul(t.PB[pb][:, 0:512],
                             t.WS[:, wbase + fK * 512 + g * 128: wbase + fK * 512 + (g + 1) * 128],
                             t.XN[:, fK * HB + nch * 512: fK * HB + (nch + 1) * 512],
                             start=(fK == 0), stop=(fK == FT - 1)) for fK in range(FT)][-1],
                    reads=wbufs + xnb, writes=[t.b_pb(pb)], tag=f"proj{dname}")
                dcol = dst_base + g * 2048 + half * 1024 + nch * 512
                dslot = dcol // 2048
                em.op("vector", lambda h, dcol=dcol, pb=pb: h.tensor_copy(
                    t.AR[:, dcol:dcol + 512], t.PB[pb][:, 0:512]),
                    reads=[t.b_pb(pb)], writes=[t.b_ar(dslot)], tag=f"proj{dname}e")
    ob = 16384 + half * 8 * 1040 + 64
    oslots = ar_bufs(t, 16384, 33024)
    em.op("vector", lambda h, ob=ob: h.memset(
        t.AR[:, ob:ob + 8 * 1040].rearrange("p (tt x) -> p tt x", tt=8)[:, :, 0:1040:65].rearrange(
            "p tt hx -> p (tt hx)") if False else
        t.AR[:, ob - 64:ob - 64 + 8 * 1040].rearrange("p (tt hx c) -> p tt hx c", tt=8, hx=16)[:, :, :, 64:65],
        1.0), reads=[], writes=oslots, tag="vones")
    for tt_h in range(8):
        ttg = half * 8 + tt_h
        for vh in range(2):
            pb = 4 + (tt_h * 2 + vh) % 2
            xnb = [t.b_xn((f, tt_h // 4)) for f in range(FT)]
            em.op("tensor", lambda h, tt_h=tt_h, vh=vh, pb=pb: [
                h.matmul(t.PB[pb][:, 0:512],
                         t.XN[:, fK * HB + tt_h * 128: fK * HB + (tt_h + 1) * 128],
                         t.WS[:, 8192 + fK * 1024 + vh * 512: 8192 + fK * 1024 + (vh + 1) * 512],
                         start=(fK == 0), stop=(fK == FT - 1)) for fK in range(FT)][-1],
                reads=[t.b_ws(s) for s in range(4, 8)] + xnb, writes=[t.b_pb(pb)], tag="projv")
            dbase = 16384 + ttg * 1040 + vh * 8 * 65
            vslots = ar_bufs(t, dbase, dbase + 520)
            em.op("vector", lambda h, dbase=dbase, pb=pb: h.tensor_copy(
                t.AR[:, dbase:dbase + 520].rearrange("p (hp c) -> p hp c", hp=8)[:, :, 0:64],
                t.PB[pb][:, 0:512].rearrange("p (hp c) -> p hp c", hp=8)),
                reads=[t.b_pb(pb)], writes=vslots, tag="projve")


def emit_attn_loop(t, b):
    em = t.em
    vbufs = ar_bufs(t, 16384, 33024)

    def a_banks(chain):
        return (0, 1) if chain == 0 else (2, 4)

    def regions(chain):
        return (0, 1) if chain == 0 else (2, 3)

    def emit_A(hh, s, grp, k, chain):
        g, j = hh // 4, hh % 4
        pa = a_banks(chain)[k % 2]
        em.op("tensor", lambda h, grp=grp, pa=pa, g=g, j=j, s=s: [
            h.matmul(t.PB[pa][:, idx * 128:(idx + 1) * 128],
                     t.AR[32 * j:32 * j + 16, 8192 + g * 2048 + tc * 128: 8192 + g * 2048 + (tc + 1) * 128],
                     t.AR[32 * j:32 * j + 16, g * 2048 + s * 128: g * 2048 + (s + 1) * 128],
                     start=True, stop=True, tile_position=(32 * j, 0),
                     skip_group_check=True)
            for idx, tc in enumerate(grp)][-1],
            reads=[t.b_ar(g), t.b_ar(4 + g)], writes=[t.b_pb(pa)], tag="A")

    def emit_sqqk(s, grp, k, chain):
        pa = a_banks(chain)[k % 2]
        jj = regions(chain)[k % 2]
        sqc = 8192 + 2048 * jj
        qkc = sqc + 1024
        L = len(grp)
        em.op("scalar", lambda h, pa=pa, L=L, sqc=sqc: h.activation(
            out=t.WS[:, sqc:sqc + L * 128], in_=t.PB[pa][:, 0:L * 128],
            func=AF.Square, scale=SQ_SCALE, bias=t.CC[:, 2:3]),
            reads=[t.b_pb(pa), t.b_const("cc2")], writes=[t.b_ws(4 + jj)], tag="sq")
        if grp[-1] == s:
            em.op("vector", lambda h, L=L, sqc=sqc, qkc=qkc: h.scalar_tensor_tensor(
                out=t.WS[:, qkc:qkc + L * 128], in0=t.WS[:, sqc:sqc + L * 128],
                scalar=0.5, in1=t.TRIU[:], op0=OP.add, op1=OP.mult),
                reads=[t.b_ws(4 + jj), t.b_const("triu_bf")], writes=[t.b_ws(4 + jj)], tag="qk")
        else:
            em.op("vector", lambda h, L=L, sqc=sqc, qkc=qkc: h.tensor_scalar(
                out=t.WS[:, qkc:qkc + L * 128], in0=t.WS[:, sqc:sqc + L * 128],
                scalar1=0.5, scalar2=None, op0=OP.add),
                reads=[t.b_ws(4 + jj)], writes=[t.b_ws(4 + jj)], tag="qk")

    def emit_qkv(hh, s, grp, k, chain, first):
        jj = regions(chain)[k % 2]
        qkc = 8192 + 2048 * jj + 1024
        pq = 6 - chain
        for idx, tc in enumerate(grp):
            st = first and idx == 0
            last = (tc == s)
            em.op("tensor", lambda h, tc=tc, idx=idx, hh=hh, st=st, last=last, qkc=qkc, pq=pq: h.matmul(
                t.PB[pq][0:65, 0:128],
                t.AR[:, 16384 + tc * 1040 + hh * 65: 16384 + tc * 1040 + (hh + 1) * 65],
                t.WS[:, qkc + idx * 128: qkc + (idx + 1) * 128],
                start=st, stop=last, skip_group_check=True),
                reads=vbufs + [t.b_ws(4 + jj)], writes=[t.b_pb(pq)], tag="qkv")

    def emit_tail(hh, s, chain):
        pq = 6 - chain
        c0 = 2560 if chain == 0 else 2048
        sfb = 5 if chain == 0 else 4
        dnr = t.SB[64:65, 5120 + chain * 128:5248 + chain * 128]
        qsb = t.SF[0:64, c0:c0 + 128]
        em.op("scalar", lambda h, dnr=dnr, pq=pq: h.activation(
            out=dnr, in_=t.PB[pq][64:65, 0:128], func=AF.Identity, bias=t.CC[64:65, 1:2]),
            reads=[t.b_pb(pq), t.b_const("cc1")], writes=[t.b_sb(2)], tag="dn")
        em.op("vector", lambda h, dnr=dnr: h.reciprocal(dnr, dnr),
            reads=[t.b_sb(2)], writes=[t.b_sb(2)], tag="dr")
        em.op("tensor", lambda h, dnr=dnr: h.matmul(
            t.PB[3][0:64, 0:128], t.ONESB[64:65, 0:64], dnr,
            start=True, stop=True, tile_position=(64, 0), skip_group_check=True),
            reads=[t.b_sb(2), t.b_const("onesb")], writes=[t.b_pb(3)], tag="dbc")
        em.op("scalar", lambda h, qsb=qsb, pq=pq: h.copy(out=qsb, in_=t.PB[pq][0:64, 0:128]),
            reads=[t.b_pb(pq)], writes=[t.b_sf(sfb)], tag="qev")
        acol = 4096 + (hh // 2) * 512 + (s % 4) * 128
        arow = 64 * (hh % 2)
        em.op("vector", lambda h, qsb=qsb, acol=acol, arow=arow: h.tensor_tensor(
            out=t.WS[arow:arow + 64, acol:acol + 128], in0=qsb,
            in1=t.PB[3][0:64, 0:128], op=OP.mult),
            reads=[t.b_sf(sfb), t.b_pb(3)], writes=[t.b_ws(2 + (acol - 4096) // 2048)], tag="att")

    def groups_of(s):
        grps = [list(range(j0, min(j0 + 4, s))) for j0 in range(0, s, 4)]
        return [gg for gg in grps if gg] + [[s]]

    INTERLEAVE = ATTN_IL
    for sg in range(4):
        for hh in range(H):
            for pair in range(2):
                sa, sb = sg * 4 + 2 * pair, sg * 4 + 2 * pair + 1
                ga, gb = groups_of(sa), groups_of(sb)
                na, nb = len(ga), len(gb)
                if not INTERLEAVE:
                    for s, gl, ch in ((sa, ga, 0), (sb, gb, 1)):
                        for k, grp in enumerate(gl):
                            emit_A(hh, s, grp, k, ch)
                            emit_sqqk(s, grp, k, ch)
                            emit_qkv(hh, s, grp, k, ch, first=(k == 0))
                        emit_tail(hh, s, ch)
                    continue
                for k in range(max(na, nb) + 1):
                    if k < na:
                        emit_A(hh, sa, ga[k], k, 0)
                    if k < nb:
                        emit_A(hh, sb, gb[k], k, 1)
                    if k < na:
                        emit_sqqk(sa, ga[k], k, 0)
                    if k < nb:
                        emit_sqqk(sb, gb[k], k, 1)
                    if 1 <= k <= na:
                        emit_qkv(hh, sa, ga[k - 1], k - 1, 0, first=(k == 1))
                        if k == na:
                            emit_tail(hh, sa, 0)
                    if 1 <= k <= nb:
                        emit_qkv(hh, sb, gb[k - 1], k - 1, 1, first=(k == 1))
                        if k == nb:
                            emit_tail(hh, sb, 1)
        woslot = (0, 1, 4, 5)
        for fK in range(FT):
            sl = woslot[fK // 2]
            wcol = sl * 2048 + (fK % 2) * 1024
            em.op("gpsimd", lambda h, fK=fK, wcol=wcol: h.dma_start(
                out=t.WS[:, wcol:wcol + 1024],
                in_=t.dram["wo"].ap()[fK * 128:(fK + 1) * 128, :]),
                writes=[t.b_ws(sl)], dma_slot=em.slot(f"wos{fK}"), tag="ldwo")
        for mh in range(2):
            for fK in range(FT):
                sl = woslot[fK // 2]
                wcol = sl * 2048 + (fK % 2) * 1024
                for m in range(4):
                    mm = mh * 4 + m
                    em.op("tensor", lambda h, fK=fK, wcol=wcol, m=m, mm=mm: h.matmul(
                        t.PB[3 + m][:, 0:512],
                        t.WS[:, wcol + mm * 128: wcol + (mm + 1) * 128],
                        t.WS[:, 4096 + fK * 512: 4096 + (fK + 1) * 512],
                        start=(fK == 0), stop=(fK == FT - 1), skip_group_check=True),
                        reads=[t.b_ws(sl), t.b_ws(2), t.b_ws(3)], writes=[t.b_pb(3 + m)], tag="wo")
            for m in range(4):
                mm = mh * 4 + m
                xcol = mm * TOK + b * S + sg * 512
                gch = (b * S + sg * 512) // 512
                em.op("vector", lambda h, xcol=xcol, m=m: h.tensor_tensor(
                    out=t.XT[:, xcol:xcol + 512], in0=t.XT[:, xcol:xcol + 512],
                    in1=t.PB[3 + m][:, 0:512], op=OP.add),
                    reads=[t.b_pb(3 + m), t.b_xt((mm, gch))], writes=[t.b_xt((mm, gch))], tag="woadd")
def emit_gating(t, b, half):
    em = t.em
    for tt_h in range(8):
        xnb = [t.b_xn((f, tt_h // 4)) for f in range(FT)]
        em.op("tensor", lambda h, tt_h=tt_h: [
            h.matmul(t.PB[6][:, 0:8],
                     t.XN[:, fK * HB + tt_h * 128: fK * HB + (tt_h + 1) * 128],
                     t.WG[:, fK * 8:(fK + 1) * 8],
                     start=(fK == 0), stop=False, skip_group_check=True) for fK in range(FT)][-1],
            reads=xnb + [t.b_const("wg")], writes=[t.b_pb(6)], tag="gmm")
        em.op("tensor", lambda h: h.matmul(
            t.PB[6][:, 0:8], t.ONESF[0:1, 0:128], t.BGR[:],
            start=False, stop=True, tile_position=(0, 0), skip_group_check=True),
            reads=[t.b_const("onesf"), t.b_const("bgr")], writes=[t.b_pb(6)], tag="gbias")
        s8 = t.SF[:, 2080:2088]
        nmax = t.SF[:, 2088:2089]
        sumr = t.SF[:, 2089:2090]
        rec = t.SF[:, 2090:2091]
        probs = t.SF[:, 2096:2104]
        mask = t.SF[:, 2104:2112]
        em.op("vector", lambda h, s8=s8: h.max(s8, t.PB[6][:, 0:8]),
            reads=[t.b_pb(6)], writes=[t.b_sf(4)], tag="gmax")
        em.op("vector", lambda h, s8=s8, nmax=nmax: h.tensor_scalar(
            out=nmax, in0=s8[:, 0:1], scalar1=-1.0, scalar2=None, op0=OP.mult),
            reads=[t.b_sf(4)], writes=[t.b_sf(4)], tag="gnm")
        em.op("scalar", lambda h, probs=probs, nmax=nmax, sumr=sumr: h.activation(
            out=probs, in_=t.PB[6][:, 0:8], func=AF.Exp, bias=nmax, accum_out=sumr),
            reads=[t.b_pb(6), t.b_sf(4)], writes=[t.b_sf(4)], tag="gexp")
        em.op("vector", lambda h, rec=rec, sumr=sumr: h.reciprocal(rec, sumr),
            reads=[t.b_sf(4)], writes=[t.b_sf(4)], tag="grec")
        em.op("vector", lambda h, probs=probs, rec=rec: h.tensor_scalar(
            out=probs, in0=probs, scalar1=rec, scalar2=None, op0=OP.mult),
            reads=[t.b_sf(4)], writes=[t.b_sf(4)], tag="gnorm")
        em.op("vector", lambda h, mask=mask, s8=s8: h.tensor_scalar(
            out=mask, in0=t.PB[6][:, 0:8], scalar1=s8[:, 1:2], scalar2=None, op0=OP.is_ge),
            reads=[t.b_pb(6), t.b_sf(4)], writes=[t.b_sf(4)], tag="gmask")
        wcol = half * 64 + tt_h * 8
        em.op("vector", lambda h, probs=probs, mask=mask, wcol=wcol: h.tensor_tensor(
            out=t.WB[:, wcol:wcol + 8], in0=probs, in1=mask, op=OP.mult),
            reads=[t.b_sf(4)], writes=[t.b_const("wb")], tag="gw")


def emit_moe(t, b, half):
    em = t.em
    def build_wbc(e):
        if e >= NE:
            return
        for tb in range(2):
            rk = (89, (e % 2) * 2 + tb)
            rcol = 8192 + (e % 2) * 1024 + tb * 512
            em.op("tensor", lambda h, e=e, tb=tb, half=half: [
                h.matmul(t.PB[6][0:1, tq * 128:(tq + 1) * 128],
                         t.WB[:, half * 64 + (tb * 4 + tq) * 8 + e: half * 64 + (tb * 4 + tq) * 8 + e + 1],
                         t.IDF[:], start=True, stop=True, skip_group_check=True)
                for tq in range(4)][-1],
                reads=[t.b_const("wb"), t.b_const("ident_f32")], writes=[t.b_pb(6)], tag="wrmm")
            wr = t.SF[0:1, 2048:2560]
            em.op("vector", lambda h, wr=wr: h.tensor_copy(wr, t.PB[6][0:1, 0:512]),
                reads=[t.b_pb(6)], writes=[t.b_sf(4)], tag="wrcp")
            em.op("tensor", lambda h, wr=wr: h.matmul(
                t.PB[6][:, 0:512], t.ONESF[0:1, 0:128], wr,
                start=True, stop=True, tile_position=(0, 0), skip_group_check=True),
                reads=[t.b_sf(4), t.b_const("onesf")], writes=[t.b_pb(6)], tag="wbcmm")
            em.op("vector", lambda h, rcol=rcol, rk=rk: h.tensor_copy(
                t.ARF[:, rcol:rcol + 512], t.PB[6][:, 0:512]),
                reads=[t.b_pb(6)], writes=[t.b_ar(rk)], tag="wbc")

    for e in range(NE):
        if e == 0:
            build_wbc(0)
        for hf in range(2):
            for wh2 in range(2):
                em.op("gpsimd", lambda h, e=e, hf=hf, wh2=wh2: h.dma_start(
                    out=t.WS[:, 8192 + wh2 * 4096:12288 + wh2 * 4096].rearrange("p (k n) -> p k n", k=4),
                    in_=t.dram["w2"].ap()[e, hf, :, wh2 * 4:(wh2 + 1) * 4, :]),
                    writes=[t.b_ws(4 + 2 * wh2), t.b_ws(5 + 2 * wh2)],
                    dma_slot=em.slot(f"w2h{wh2}"), tag="ldw2")
            for oct in range(4):
                par = oct % 2
                kbase = hf * 8 + oct * 2
                em.op("gpsimd", lambda h, e=e, kbase=kbase, par=par: h.dma_start(
                    out=t.WS[:, par * 2048: (par + 1) * 2048].rearrange("p (k n) -> p k n", k=8),
                    in_=t.dram["w1"].ap()[e, kbase // 2]),
                    writes=[t.b_ws(par)], dma_slot=em.slot(f"w1o{par}"), tag="ldw1")
                em.op("gpsimd", lambda h, e=e, kbase=kbase, par=par: h.dma_start(
                    out=t.WS[:, 4096 + par * 2048: 4096 + (par + 1) * 2048].rearrange("p (k n) -> p k n", k=8),
                    in_=t.dram["w3"].ap()[e, kbase // 2]),
                    writes=[t.b_ws(2 + par)], dma_slot=em.slot(f"w3o{par}"), tag="ldw3")
                for krel in range(2):
                    kih = oct * 2 + krel
                    kg = hf * 8 + kih
                    for tb in range(2):
                        it = kih * 2 + tb
                        pa, pb_ = 2 * (it % 2), 2 * (it % 2) + 1
                        xnb = [t.b_xn((f, tb)) for f in range(FT)]
                        em.op("tensor", lambda h, par=par, krel=krel, tb=tb, pa=pa: [
                            h.matmul(t.PB[pa][:, 0:512],
                                     t.WS[:, par * 2048 + fK * 256 + krel * 128: par * 2048 + fK * 256 + (krel + 1) * 128],
                                     t.XN[:, fK * HB + tb * 512: fK * HB + (tb + 1) * 512],
                                     start=(fK == 0), stop=(fK == FT - 1)) for fK in range(FT)][-1],
                            reads=[t.b_ws(par)] + xnb, writes=[t.b_pb(pa)], tag="h1")
                        em.op("tensor", lambda h, par=par, krel=krel, tb=tb, pb_=pb_: [
                            h.matmul(t.PB[pb_][:, 0:512],
                                     t.WS[:, 4096 + par * 2048 + fK * 256 + krel * 128: 4096 + par * 2048 + fK * 256 + (krel + 1) * 128],
                                     t.XN[:, fK * HB + tb * 512: fK * HB + (tb + 1) * 512],
                                     start=(fK == 0), stop=(fK == FT - 1)) for fK in range(FT)][-1],
                            reads=[t.b_ws(2 + par)] + xnb, writes=[t.b_pb(pb_)], tag="h3")
                        h3b = t.SF[:, 1024:1536]
                        em.op("scalar", lambda h, pb_=pb_, e=e, kg=kg, h3b=h3b: h.activation(
                            out=h3b, in_=t.PB[pb_][:, 0:512], func=AF.Identity,
                            bias=t.B3C[:, e * 16 + kg: e * 16 + kg + 1]),
                            reads=[t.b_pb(pb_), t.b_const("b3c")], writes=[t.b_sf(2)], tag="h3b")
                        tt_ = t.SF[:, 1536:2048]
                        em.op("vector", lambda h, pa=pa, e=e, kg=kg, h3b=h3b, tt_=tt_: h.scalar_tensor_tensor(
                            out=tt_, in0=t.PB[pa][:, 0:512], scalar=t.B1C[:, e * 16 + kg: e * 16 + kg + 1],
                            in1=h3b, op0=OP.add, op1=OP.mult),
                            reads=[t.b_pb(pa), t.b_const("b1c"), t.b_sf(2)], writes=[t.b_sf(3)], tag="hm")
                        uu = t.SF[:, 1024:1536]
                        em.op("scalar", lambda h, tt_=tt_, uu=uu: h.activation(
                            out=uu, in_=tt_, func=AF.Silu),
                            reads=[t.b_sf(3)], writes=[t.b_sf(2)], tag="silu")
                        whc = kih * 1024 + tb * 512
                        rcol = 8192 + (e % 2) * 1024 + tb * 512
                        em.op("vector", lambda h, uu=uu, rcol=rcol, whc=whc: h.tensor_tensor(
                            out=t.SB[:, whc:whc + 512], in0=uu,
                            in1=t.ARF[:, rcol:rcol + 512], op=OP.mult),
                            reads=[t.b_sf(2), t.b_ar((89, (e % 2) * 2 + tb))],
                            writes=[t.b_sb(whc // 2048)], tag="wh")
            if hf == 0:
                build_wbc(e + 1)
            for tb in range(2):
                for m in range(FT):
                    pp = 4 + (m % 2)
                    for wh2 in range(2):
                        em.op("tensor", lambda h, tb=tb, m=m, pp=pp, wh2=wh2: [
                            h.matmul(t.PB[pp][:, 0:512],
                                     t.WS[:, 8192 + kih * 1024 + m * 128: 8192 + kih * 1024 + (m + 1) * 128],
                                     t.SB[:, kih * 1024 + tb * 512: kih * 1024 + tb * 512 + 512],
                                     start=(kih == 0), stop=(kih == 7))
                            for kih in range(wh2 * 4, wh2 * 4 + 4)][-1],
                            reads=[t.b_ws(4 + 2 * wh2), t.b_ws(5 + 2 * wh2)] +
                                  [t.b_sb(i) for i in range(2 * wh2, 2 * wh2 + 2)],
                            writes=[t.b_pb(pp)], tag="w2")
                    mcol = tb * 4096 + m * 512
                    mslot = [t.b_ar(mcol // 1024)]
                    if e == 0 and hf == 0:
                        em.op("vector", lambda h, mcol=mcol, pp=pp: h.tensor_copy(
                            t.ARF[:, mcol:mcol + 512], t.PB[pp][:, 0:512]),
                            reads=[t.b_pb(pp)], writes=mslot, tag="mxi")
                    else:
                        em.op("vector", lambda h, mcol=mcol, pp=pp: h.tensor_tensor(
                            out=t.ARF[:, mcol:mcol + 512], in0=t.ARF[:, mcol:mcol + 512],
                            in1=t.PB[pp][:, 0:512], op=OP.add),
                            reads=[t.b_pb(pp)] + mslot, writes=mslot, tag="mxa")
    for tb in range(2):
        for fK in range(FT):
            mcol = tb * 4096 + fK * 512
            em.op("vector", lambda h, mcol=mcol, fK=fK: h.tensor_copy(
                t.SB[:, fK * 512:(fK + 1) * 512], t.ARF[:, mcol:mcol + 512]),
                reads=[t.b_ar(mcol // 1024)], writes=[t.b_sb(fK // 4)], tag="mxc")
        for m in range(FT):
            par = m % 2
            em.op("gpsimd", lambda h, m=m, par=par: h.dma_start(
                out=t.WS[:, par * 1024:(par + 1) * 1024].rearrange("p (k n) -> p k n", k=8),
                in_=t.dram["wf"].ap()[m]),
                writes=[t.b_ws(0)], dma_slot=em.slot(f"wfs{par}"), tag="ldwf")
            pp = 4 + par
            em.op("tensor", lambda h, m=m, par=par, pp=pp: [
                h.matmul(t.PB[pp][:, 0:512],
                         t.WS[:, par * 1024 + fK * 128: par * 1024 + (fK + 1) * 128],
                         t.SB[:, fK * 512:(fK + 1) * 512],
                         start=(fK == 0), stop=(fK == FT - 1)) for fK in range(FT)][-1],
                reads=[t.b_ws(0), t.b_sb(0), t.b_sb(1)], writes=[t.b_pb(pp)], tag="wf")
            fo = t.SF[:, 1024:1536]
            em.op("scalar", lambda h, m=m, pp=pp, fo=fo: h.activation(
                out=fo, in_=t.PB[pp][:, 0:512], func=AF.Identity, bias=t.BFC[:, m:m + 1]),
                reads=[t.b_pb(pp), t.b_const("bfc")], writes=[t.b_sf(2)], tag="wfb")
            xcol = m * TOK + b * S + half * 1024 + tb * 512
            gch = (b * S + half * 1024 + tb * 512) // 512
            em.op("vector", lambda h, xcol=xcol, fo=fo: h.tensor_tensor(
                out=t.XT[:, xcol:xcol + 512], in0=t.XT[:, xcol:xcol + 512], in1=fo, op=OP.add),
                reads=[t.b_sf(2), t.b_xt((m, gch))], writes=[t.b_xt((m, gch))], tag="wfadd")


def emit_l1_attn(t, b):
    em = t.em
    xnb_all = [t.b_ar(s) for s in range(8)]
    for et in range(FT):
        par = et % 2
        em.op("gpsimd", lambda h, et=et, par=par: h.dma_start(
            out=t.WS[:, par * 2048:(par + 1) * 2048].rearrange("p (mt k n) -> p mt k n", mt=2, k=8),
            in_=t.dram["wft"].ap()[et]),
            writes=[t.b_ws(par)], dma_slot=em.slot(f"wfts{par}"), tag="ldwft")
        for ch in range(4):
            pk = 0 + (ch % 2)
            ph = 2 + (ch % 2)
            em.op("tensor", lambda h, par=par, ch=ch, pk=pk: [
                h.matmul(t.PB[pk][:, 0:512],
                         t.WS[:, par * 2048 + fK * 128: par * 2048 + (fK + 1) * 128],
                         t.AR[:, fK * 2048 + ch * 512: fK * 2048 + (ch + 1) * 512],
                         start=(fK == 0), stop=(fK == FT - 1)) for fK in range(FT)][-1],
                reads=[t.b_ws(par)] + xnb_all, writes=[t.b_pb(pk)], tag="kf")
            em.op("tensor", lambda h, par=par, ch=ch, ph=ph: [
                h.matmul(t.PB[ph][:, 0:512],
                         t.WS[:, par * 2048 + 1024 + fK * 128: par * 2048 + 1024 + (fK + 1) * 128],
                         t.AR[:, fK * 2048 + ch * 512: fK * 2048 + (ch + 1) * 512],
                         start=(fK == 0), stop=(fK == FT - 1)) for fK in range(FT)][-1],
                reads=[t.b_ws(par)] + xnb_all, writes=[t.b_pb(ph)], tag="hx")
            if L1A_LIMIT < 2:
                continue
            aa = t.SF[:, 0:512]
            zz = t.SF[:, 512:1024]
            sg_ = t.SF[:, 1024:1536]
            lg = t.SF[:, 1536:2048]
            em.op("scalar", lambda h, pk=pk, aa=aa: h.activation(
                out=aa, in_=t.PB[pk][:, 0:512], func=AF.Sigmoid, scale=-1.0),
                reads=[t.b_pb(pk)], writes=[t.b_sf(0)], tag="siga")
            em.op("scalar", lambda h, pk=pk, zz=zz: h.activation(
                out=zz, in_=t.PB[pk][:, 0:512], func=AF.Sigmoid),
                reads=[t.b_pb(pk)], writes=[t.b_sf(1)], tag="sigz")
            em.op("scalar", lambda h, ph=ph, sg_=sg_: h.activation(
                out=sg_, in_=t.PB[ph][:, 0:512], func=AF.Sigmoid),
                reads=[t.b_pb(ph)], writes=[t.b_sf(2)], tag="sigg")
            if L1A_LIMIT < 3:
                continue
            em.op("vector", lambda h, ph=ph, lg=lg: h.tensor_scalar(
                out=lg, in0=t.PB[ph][:, 0:512], scalar1=0.5, scalar2=None, op0=OP.add),
                reads=[t.b_pb(ph)], writes=[t.b_sf(3)], tag="lin")
            em.op("vector", lambda h, sg_=sg_, lg=lg: h.tensor_tensor(
                out=lg, in0=sg_, in1=lg, op=OP.max),
                reads=[t.b_sf(2), t.b_sf(3)], writes=[t.b_sf(3)], tag="gmx")
            em.op("vector", lambda h, zz=zz, lg=lg: h.tensor_tensor(
                out=zz, in0=zz, in1=lg, op=OP.mult),
                reads=[t.b_sf(1), t.b_sf(3)], writes=[t.b_sf(1)], tag="bv")
            if L1A_LIMIT < 4:
                continue
            hp = 4 + (ch % 2)
            hprev = 4 + ((ch + 1) % 2)
            init = 0.5 if ch == 0 else t.SF[:, hprev * 512 + 511: hprev * 512 + 512]
            rd = [t.b_sf(0), t.b_sf(1)] + ([] if ch == 0 else [t.b_sf(hprev)])
            if SCAN_FAKE:
                em.op("vector", lambda h, aa=aa, zz=zz, hp=hp: h.tensor_tensor(
                    out=t.SF[:, hp * 512:(hp + 1) * 512], in0=aa, in1=zz, op=OP.add),
                    reads=rd, writes=[t.b_sf(hp)], tag="scan")
            else:
                em.op("vector", lambda h, aa=aa, zz=zz, hp=hp, init=init: h.tensor_tensor_scan(
                    out=t.SF[:, hp * 512:(hp + 1) * 512], data0=aa, data1=zz,
                    initial=init, op0=OP.mult, op1=OP.add),
                    reads=rd, writes=[t.b_sf(hp)], tag="scan")
            if L1A_LIMIT < 5:
                continue
            xcol = et * TOK + b * S + ch * 512
            gch = (b * S + ch * 512) // 512
            em.op("vector", lambda h, xcol=xcol, hp=hp: h.tensor_tensor(
                out=t.XT[:, xcol:xcol + 512], in0=t.XT[:, xcol:xcol + 512],
                in1=t.SF[:, hp * 512:(hp + 1) * 512], op=OP.add),
                reads=[t.b_sf(hp), t.b_xt((et, gch))], writes=[t.b_xt((et, gch))], tag="hadd")


def emit_l1_ffn(t, b):
    em = t.em
    xnb_all = [t.b_ar(s) for s in range(8)]
    for ch in range(4):
        gch = (b * S + ch * 512) // 512
        for mh in range(32):
            par = mh % 2
            em.op("gpsimd", lambda h, mh=mh, par=par: h.dma_start(
                out=t.WS[:, 4096 + par * 1024: 4096 + (par + 1) * 1024].rearrange("p (k n) -> p k n", k=8),
                in_=t.dram["l1w1"].ap()[mh]),
                writes=[t.b_ws(2 + par)], dma_slot=em.slot(f"l1w1{par}"), tag="ldl1a")
            pk = par
            em.op("tensor", lambda h, mh=mh, par=par, ch=ch, pk=pk: [
                h.matmul(t.PB[pk][:, 0:512],
                         t.WS[:, 4096 + par * 1024 + fK * 128: 4096 + par * 1024 + (fK + 1) * 128],
                         t.AR[:, fK * 2048 + ch * 512: fK * 2048 + (ch + 1) * 512],
                         start=(fK == 0), stop=(fK == FT - 1)) for fK in range(FT)][-1],
                reads=[t.b_ws(2 + par)] + xnb_all, writes=[t.b_pb(pk)], tag="f1")
            em.op("scalar", lambda h, mh=mh, pk=pk: h.activation(
                out=t.AR[:, 16384 + mh * 512: 16384 + (mh + 1) * 512], in_=t.PB[pk][:, 0:512],
                func=AF.Relu, bias=t.L1B1C[:, mh:mh + 1]),
                reads=[t.b_pb(pk), t.b_const("l1b1c")],
                writes=[t.b_ar(8 + mh // 4)], tag="relu")
        rhb = [t.b_ar(8 + i) for i in range(8)]
        for mo in range(FT):
            par = mo % 2
            wsl = (4 + 2 * par, 6 + 2 * par)
            em.op("gpsimd", lambda h, mo=mo, par=par: h.dma_start(
                out=t.WS[:, 8192 + par * 4096: 8192 + (par + 1) * 4096].rearrange("p (k n) -> p k n", k=32),
                in_=t.dram["l1w2"].ap()[mo]),
                writes=[t.b_ws(wsl[0]), t.b_ws(wsl[0] + 1)], dma_slot=em.slot(f"l1w2{par}"), tag="ldl1b")
            po = 2 + par
            em.op("tensor", lambda h, mo=mo, par=par, po=po: [
                h.matmul(t.PB[po][:, 0:512],
                         t.WS[:, 8192 + par * 4096 + k * 128: 8192 + par * 4096 + (k + 1) * 128],
                         t.AR[:, 16384 + k * 512: 16384 + (k + 1) * 512],
                         start=(k == 0), stop=(k == 31)) for k in range(32)][-1],
                reads=[t.b_ws(wsl[0]), t.b_ws(wsl[0] + 1)] + rhb, writes=[t.b_pb(po)], tag="f2")
            xa = t.SF[:, (par) * 512:(par + 1) * 512]
            em.op("scalar", lambda h, mo=mo, po=po, xa=xa: h.activation(
                out=xa, in_=t.PB[po][:, 0:512], func=AF.Identity, bias=t.L1B2C[:, mo:mo + 1]),
                reads=[t.b_pb(po), t.b_const("l1b2c")], writes=[t.b_sf(par)], tag="ob")
            of = t.SF[:, 1024 + par * 512: 1024 + (par + 1) * 512]
            xcol = mo * TOK + b * S + ch * 512
            em.op("vector", lambda h, xa=xa, of=of, xcol=xcol: h.tensor_tensor(
                out=of, in0=xa, in1=t.XT[:, xcol:xcol + 512], op=OP.add),
                reads=[t.b_sf(par), t.b_xt((mo, gch))], writes=[t.b_sf(2 + par)], tag="osum")
            for tq in range(4):
                em.op("tensor", lambda h, of=of, tq=tq: h.transpose(
                    t.PB[6][:, tq * 128:(tq + 1) * 128], of[:, tq * 128:(tq + 1) * 128], t.IDF[:]),
                    reads=[t.b_sf(2 + par), t.b_const("ident_f32")], writes=[t.b_pb(6)], tag="oT")
            ost = t.SF[:, 2048 + par * 512: 2048 + (par + 1) * 512]
            em.op("vector", lambda h, ost=ost: h.tensor_copy(ost, t.PB[6][:, 0:512]),
                reads=[t.b_pb(6)], writes=[t.b_sf(4 + par)], tag="oTe")
            r0 = b * S + ch * 512
            em.op("sync", lambda h, ost=ost, r0=r0, mo=mo: h.dma_start(
                out=t.out.ap().rearrange("(tq p) n -> p tq n", p=128)[:, r0 // 128: r0 // 128 + 4, mo * 128:(mo + 1) * 128],
                in_=ost.rearrange("p (tq n) -> p tq n", tq=4)),
                reads=[t.b_sf(4 + par)], writes=[t.b_out((b, ch, mo))], dma_slot=em.slot(f"out{par}"), tag="ost")


def emit_tap(t, name, src_ap, reads):
    em = t.em
    em.op("gpsimd", lambda h: h.dma_start(out=t.dbg[name].ap(), in_=src_ap),
        reads=reads, writes=[], dma_slot=em.slot("dbgs"), tag="tap")


def emit_dump_x1(t):
    em = t.em
    x1f = t.nc.dram_tensor("x1f", [128, FT * TOK], F32, kind="ExternalOutput")
    for c in range(8):
        rd = [t.b_xt((f, c)) for f in range(FT)]
        em.op("gpsimd", lambda h, c=c: h.dma_start(
            out=x1f.ap().rearrange("p (f g ch) -> p f g ch", f=FT, g=8)[:, :, c, :],
            in_=t.XT[:].rearrange("p (f g ch) -> p f g ch", f=FT, g=8)[:, :, c, :]),
            reads=rd, writes=[], dma_slot=em.slot(f"x1d{c % 2}"), tag="x1d")


def emit_load_x1(t):
    em = t.em
    x1f = t.nc.dram_tensor("x1f", [128, FT * TOK], F32, kind="ExternalInput")
    for c in range(8):
        em.op("gpsimd", lambda h, c=c: h.dma_start(
            out=t.XT[:].rearrange("p (f g ch) -> p f g ch", f=FT, g=8)[:, :, c, :],
            in_=x1f.ap().rearrange("p (f g ch) -> p f g ch", f=FT, g=8)[:, :, c, :]),
            reads=[], writes=[t.b_xt((f, c)) for f in range(FT)],
            dma_slot=em.slot(f"x1l{c}"), tag="x1l")


def build_program(nc, taps=(), parts=("load", "l0", "l1")):
    dbg_specs = []
    for tp in taps:
        if tp.startswith("xt"):
            dbg_specs.append((tp, [128, FT * TOK]))
        elif tp.startswith("xn1"):
            dbg_specs.append((tp, [128, 16384]))
        elif tp.startswith("xn"):
            dbg_specs.append((tp, [128, FT * HB]))
    t = setup_tensors(nc, dbg=dbg_specs)
    t.ONESF = t.IDF
    em = t.em
    c = nc.sbuf_tensor("ONESF", [128, 128], F32)
    t.ctxs.append(c)
    t.ONESF = c.__enter__()
    em.op("vector", lambda h: h.memset(t.ONESF[:], 1.0), writes=[t.b_const("onesf")])
    emit_consts(t)
    if "load" in parts:
        emit_load_x(t)
        if "xt0" in taps:
            emit_tap(t, "xt0", t.XT[:], [t.b_xt((f, c)) for f in range(FT) for c in range(8)])
    if "lntest" in parts:
        emit_ln(t, t.XN, 0, HB, lambda f, i: [t.b_xn((f, i))], [0, 1])
        emit_tap(t, "xn0", t.XN[:], [t.b_xn((f, i)) for f in range(FT) for i in range(2)])
    if "loadx1" in parts:
        emit_load_x1(t)
    do_attn = ("l0" in parts) or ("l0attn" in parts)
    do_moe = ("l0" in parts) or ("l0moe" in parts)
    if do_attn or do_moe:
        for b in (B_LIST if B_LIST is not None else range(B)):
            if do_attn:
                for half in range(2):
                    chunks = [(b * S + half * 1024) // 512, (b * S + half * 1024) // 512 + 1]
                    emit_ln(t, t.XN, 0, HB, lambda f, i: [t.b_xn((f, i))], chunks)
                    if half == 0 and b == 0 and "xn0" in taps:
                        emit_tap(t, "xn0", t.XN[:],
                                 [t.b_xn((f, i)) for f in range(FT) for i in range(2)])
                    emit_attn_proj(t, b, half)
                emit_attn_loop(t, b)
            if do_moe:
                for half in range(2):
                    chunks = [(b * S + half * 1024) // 512, (b * S + half * 1024) // 512 + 1]
                    emit_ln(t, t.XN, 0, HB, lambda f, i: [t.b_xn((f, i))], chunks,
                            src=("x0" if do_attn else "xt"))
                    emit_gating(t, b, half)
                    emit_moe(t, b, half)
        if "xt1" in taps:
            emit_tap(t, "xt1", t.XT[:], [t.b_xt((f, c)) for f in range(FT) for c in range(8)])
    if "dumpx1" in parts:
        emit_dump_x1(t)
    if "l1" in parts:
        for b in (B_LIST if B_LIST is not None else range(B)):
            chunks = [b * 4 + i for i in range(4)]
            emit_ln(t, t.AR, 0, 2048, lambda f, i: [t.b_ar(f)], chunks)
            if b == 0 and "xn1a" in taps:
                emit_tap(t, "xn1a", t.AR[:, 0:16384],
                         [t.b_ar(s) for s in range(8)])
            emit_l1_attn(t, b)
            emit_l1_ffn(t, b)
    em.barrier()
    sem_ctxs = em.finalize()
    return t, sem_ctxs


FUSED = True


def _run_build(parts, in_maps, extra=None):
    nc = bass.Bass("TRN2", target_bir_lowering=False)
    t, sem_ctxs = build_program(nc, parts=parts)
    res = run_bass_kernel_spmd(nc, in_maps, core_ids=list(range(NCORES)))
    for c in reversed(sem_ctxs):
        c.__exit__(None, None, None)
    for c in reversed(t.ctxs):
        c.__exit__(None, None, None)
    return res


def kernel(**inputs):
    global B_LIST
    in_maps = [prep_inputs(core, inputs) for core in range(NCORES)]
    if FUSED:
        res = _run_build(("load", "l0", "l1"), in_maps)
        shards = [res.results[i]["out"].reshape(B, S, DM) for i in range(NCORES)]
    else:
        outs = []
        for bsel in range(B):
            B_LIST = [bsel]
            res = _run_build(("load", "l0", "l1"), in_maps)
            outs.append([res.results[i]["out"].reshape(B, S, DM)[bsel] for i in range(NCORES)])
        B_LIST = None
        shards = [np.stack([outs[0][i], outs[1][i]], axis=0) for i in range(NCORES)]
    return np.concatenate(shards, axis=0).astype(np.float32)
